# version 05
# baseline (speedup 1.0000x reference)
import numpy as np
import ml_dtypes
import concourse.bass as bass
import concourse.mybir as mybir
from concourse.bass_utils import run_bass_kernel_spmd

F32 = mybir.dt.float32
BF16 = mybir.dt.bfloat16
AF = mybir.ActivationFunctionType
ALU = mybir.AluOpType
AX = mybir.AxisListType

N = 2048
D = 1024
DC = 8
NH = 8
DH = 128
LAMBDA_INIT = 0.8 - 0.6 * float(np.exp(-0.3))
SC0 = 1.0 / float(np.sqrt(128.0))
SC1 = 1.0 / 32.0
NHALF = 1024
NT = 512

ACOLS = 46080
BLK = 512

DEBUG = []


class EM:
    QUEUES = ("sp", "gp")
    NLANES = 6

    def __init__(self, nc, es):
        self.nc = nc
        self.streams = ("pe", "act", "dve", "sp", "gp")
        self.entities = ["pe", "act", "dve", "sp", "gp"]
        for q in self.QUEUES:
            self.entities += [f"{q}@{l}" for l in range(self.NLANES)]
        self.sems = {e: es.enter_context(nc.semaphore(f"sem_{e.replace('@','_')}"))
                     for e in self.entities}
        self.cnt = {e: 0 for e in self.entities}
        self.rr = {q: 0 for q in self.QUEUES}
        self.waited = {e: {f: 0 for f in self.entities} for e in self.streams}
        self.last_w = {}
        self.readers = {}
        self.plan = {e: [] for e in self.streams}

    def _unit(self, ent):
        return 16 if "@" in ent else 1

    def _need(self, stream, ent, c):
        if self.waited[stream][ent] < c:
            self.plan[stream].append(("w", ent, c * self._unit(ent)))
            self.waited[stream][ent] = c

    def _pre(self, stream, ent, r, w):
        for k in r:
            lw = self.last_w.get(k)
            if lw is not None:
                self._need(stream, lw[0], lw[1])
        for k in w:
            lw = self.last_w.get(k)
            if lw is not None and lw[0] != ent:
                self._need(stream, lw[0], lw[1])
            for f, c in self.readers.get(k, {}).items():
                if f != ent:
                    self._need(stream, f, c)

    def _post(self, ent, r, w):
        self.cnt[ent] += 1
        c = self.cnt[ent]
        for k in w:
            self.last_w[k] = (ent, c)
            self.readers[k] = {}
        for k in r:
            self.readers.setdefault(k, {})[ent] = c
        return c

    def op(self, e, fn, r=(), w=()):
        self._pre(e, e, r, w)
        self.plan[e].append(("o", fn, e))
        self._post(e, r, w)

    def grp(self, e, fns, r=(), w=()):
        self._pre(e, e, r, w)
        self.plan[e].append(("g", fns, e))
        self._post(e, r, w)

    def dma(self, q, out, in_, r=(), w=()):
        lane = self.rr[q]
        self.rr[q] = (lane + 1) % self.NLANES
        ent = f"{q}@{lane}"
        self._need(q, ent, self.cnt[ent])
        self._pre(q, ent, r, w)
        self.plan[q].append(("o", lambda eng: eng.dma_start(out=out, in_=in_), ent))
        self._post(ent, r, w)

    def final_wait(self):
        for f in self.entities:
            if self.cnt[f]:
                self._need("sp", f, self.cnt[f])

    def emit(self, block):
        secs = dict(pe=block.tensor, act=block.scalar, dve=block.vector,
                    sp=block.sync, gp=block.gpsimd)
        sems = self.sems
        for e in self.streams:
            plan = self.plan[e]

            def body(eng, _plan=plan):
                for item in _plan:
                    if item[0] == "w":
                        eng.wait_ge(sems[item[1]], item[2])
                    elif item[0] == "o":
                        ins = item[1](eng)
                        ins.then_inc(sems[item[2]], self._unit(item[2]))
                    else:
                        ins = None
                        for fn in item[1]:
                            ins = fn(eng)
                        ins.then_inc(sems[item[2]], self._unit(item[2]))
            secs[e](body)


class Reg:

    def __init__(self, ctx, name, c0, cols, dtype=F32):
        self.ctx, self.name, self.c0, self.cols, self.dtype = ctx, name, c0, cols, dtype
        assert c0 % BLK == 0 or dtype is not None
        assert c0 + cols <= ACOLS, (name, c0, cols)

    def keys(self, rc0=0, rcols=None):
        div = 1 if self.dtype == F32 else 2
        if rcols is None:
            rcols = self.cols * div - rc0
        a = (self.c0 + rc0 // div) // BLK
        b = (self.c0 + (rc0 + rcols + div - 1) // div + BLK - 1) // BLK
        return [("A", i) for i in range(a, b)]

    def ap(self, p0=0, pn=128, rc0=0, rcols=None):
        if self.dtype == F32:
            if rcols is None:
                rcols = self.cols - rc0
            return self.ctx.a32[p0:p0 + pn, self.c0 + rc0:self.c0 + rc0 + rcols]
        else:
            if rcols is None:
                rcols = 2 * self.cols - rc0
            return self.ctx.a16[p0:p0 + pn, 2 * self.c0 + rc0:2 * self.c0 + rc0 + rcols]

    def v3(self, c, inner):
        return self.ap().rearrange("p (c n) -> p c n", c=c)


class Ctx:
    pass


def host_prep(inp):
    f = np.float32
    c = lambda a: np.ascontiguousarray(a, dtype=f)
    H = {}
    H["wq_t"] = c(inp["L0_attn_Wq"].T)
    H["wk_t"] = c(inp["L0_attn_Wk"].T)
    H["wv_t"] = c(inp["L0_attn_Wv"].T)
    H["wo_t"] = c(inp["L0_attn_Wo"].T)
    H["lamA"] = c(np.concatenate([inp["L0_attn_lambda_q1"], inp["L0_attn_lambda_k1"]], 1))
    H["lamB"] = c(np.concatenate([inp["L0_attn_lambda_q2"], inp["L0_attn_lambda_k2"]], 1))
    H["rms_row"] = c(inp["L0_attn_rms_scale"][None, :])
    H["gate_t"] = c(inp["L0_ffn_gate_w"].T)
    H["w1t"] = c(np.transpose(inp["L0_ffn_w1"], (0, 2, 1)))
    H["w3t"] = c(np.transpose(inp["L0_ffn_w3"], (0, 2, 1)))
    w2 = np.asarray(inp["L0_ffn_w2"], f)
    H["w2cm"] = c(w2.reshape(8, 8, 128, 16, 128).transpose(0, 1, 4, 3, 2))
    H["wq1"] = c(inp["L1_attn_W_query"])
    H["wk1"] = c(inp["L1_attn_W_key"])
    H["wv1"] = c(inp["L1_attn_W_value"])
    H["nwc"] = c(inp["L1_ffn_norm_weight"].reshape(8, 128).T)
    H["rout_t"] = c(inp["L1_ffn_router_w"].T)
    H["sw1"] = c(inp["L1_ffn_sw1"])
    H["sb1c"] = c(inp["L1_ffn_sb1"].reshape(32, 128).T)
    sw2 = np.asarray(inp["L1_ffn_sw2"], f)
    H["sw2cm"] = c(sw2.reshape(32, 128, 8, 128).transpose(2, 1, 0, 3))
    H["sb2c"] = c(inp["L1_ffn_sb2"].reshape(8, 128).T)
    H["ew1"] = c(inp["L1_ffn_ew1"])
    H["eb1c"] = c(np.transpose(np.asarray(inp["L1_ffn_eb1"], f).reshape(8, 32, 128), (2, 0, 1)))
    ew2 = np.asarray(inp["L1_ffn_ew2"], f)
    H["ew2cm"] = c(ew2.reshape(8, 32, 128, 8, 128).transpose(0, 3, 2, 1, 4))
    H["eb2"] = c(inp["L1_ffn_eb2"])
    H["ident_b"] = np.eye(128, dtype=ml_dtypes.bfloat16)
    H["ident_f"] = np.eye(128, dtype=f)
    H["ones_c"] = np.ones((128, 1), f)
    H["ones_r"] = np.ones((1, 128), f)
    H["ones3"] = np.ones((128, 128), f)
    sel = np.zeros((8, 1024), f)
    for e in range(8):
        sel[e, e * 128:(e + 1) * 128] = 1.0
    H["sel"] = sel
    return H


PER_CORE = "xT"
COMMON = ["wq_t", "wk_t", "wv_t", "wo_t", "lamA", "lamB", "rms_row", "gate_t",
          "w1t", "w3t", "w2cm", "wq1", "wk1", "wv1", "nwc", "rout_t", "sw1",
          "sb1c", "sw2cm", "sb2c", "ew1", "eb1c", "ew2cm", "eb2",
          "ident_b", "ident_f", "ones_c", "ones_r", "ones3", "sel"]


def build_program(debug=()):
    from contextlib import ExitStack
    es = ExitStack()
    nc = bass.Bass("TRN2", target_bir_lowering=False, debug=False)
    X = Ctx()
    X.nc = nc

    d = {}
    d["xT"] = nc.dram_tensor("xT", [1024, 2048], F32, kind="ExternalInput").ap()
    for nm, shp in [("wq_t", [1024, 2048]), ("wk_t", [1024, 2048]),
                    ("wv_t", [1024, 2048]), ("wo_t", [2048, 1024]),
                    ("lamA", [8, 256]), ("lamB", [8, 256]), ("rms_row", [1, 256]),
                    ("gate_t", [1024, 8]),
                    ("w1t", [8, 1024, 2048]), ("w3t", [8, 1024, 2048]),
                    ("w2cm", [8, 8, 128, 16, 128]),
                    ("wq1", [1024, 1024]), ("wk1", [1024, 1024]), ("wv1", [1024, 1024]),
                    ("nwc", [128, 8]), ("rout_t", [1024, 8]), ("sw1", [1024, 4096]),
                    ("sb1c", [128, 32]), ("sw2cm", [8, 128, 32, 128]), ("sb2c", [128, 8]),
                    ("ew1", [8, 1024, 4096]), ("eb1c", [128, 8, 32]),
                    ("ew2cm", [8, 8, 128, 32, 128]), ("eb2", [8, 1024]),
                    ("ident_b", None), ("ident_f", [128, 128]), ("ones_c", [128, 1]),
                    ("ones_r", [1, 128]), ("ones3", [128, 128]), ("sel", [8, 1024])]:
        if nm == "ident_b":
            d[nm] = nc.dram_tensor(nm, [128, 128], BF16, kind="ExternalInput").ap()
        else:
            d[nm] = nc.dram_tensor(nm, shp, F32, kind="ExternalInput").ap()
    out = nc.dram_tensor("out", [1024, 2048], F32, kind="ExternalOutput").ap()
    x1d = nc.dram_tensor("x1d", [1024, 2048], F32, kind="Internal").ap()
    s0d = nc.dram_tensor("s0d", [1024, 2048], BF16, kind="Internal").ap()
    dbg = {}
    for nm in debug:
        dbg[nm] = nc.dram_tensor("dbg_" + nm, [1024, 2048], F32, kind="ExternalOutput").ap()
    X.d, X.out, X.x1d, X.s0d, X.dbg = d, out, x1d, s0d, dbg

    AR = nc.alloc_sbuf_tensor("arena", [128, ACOLS], F32)
    X.a32 = AR.ap()
    X.a16 = AR.ap().bitcast(BF16)
    C = {}
    C["idb"] = nc.alloc_sbuf_tensor("sb_idb", [128, 128], BF16)
    C["idf"] = nc.alloc_sbuf_tensor("sb_idf", [128, 128], F32)
    C["ones_c"] = nc.alloc_sbuf_tensor("sb_ones_c", [128, 1], F32)
    C["ones_r"] = nc.alloc_sbuf_tensor("sb_ones_r", [1, 128], F32)
    C["ones3"] = nc.alloc_sbuf_tensor("sb_ones3", [128, 128], F32)
    C["rms_bc"] = nc.alloc_sbuf_tensor("sb_rms_bc", [128, 256], F32)
    C["nlam"] = nc.alloc_sbuf_tensor("sb_nlam", [128, 8], F32)
    C["gate"] = nc.alloc_sbuf_tensor("sb_gate", [128, 8, 8], F32)
    C["rout"] = nc.alloc_sbuf_tensor("sb_rout", [128, 8, 8], F32)
    C["routnw"] = nc.alloc_sbuf_tensor("sb_routnw", [128, 8, 8], F32)
    C["eb1c"] = nc.alloc_sbuf_tensor("sb_eb1c", [128, 8, 32], F32)
    C["sb1c"] = nc.alloc_sbuf_tensor("sb_sb1c", [128, 32], F32)
    C["sb2c"] = nc.alloc_sbuf_tensor("sb_sb2c", [128, 8], F32)
    C["nwc"] = nc.alloc_sbuf_tensor("sb_nwc", [128, 8], F32)
    C["lamA"] = nc.alloc_sbuf_tensor("sb_lamA", [8, 256], F32)
    C["lamB"] = nc.alloc_sbuf_tensor("sb_lamB", [8, 256], F32)
    C["sel"] = nc.alloc_sbuf_tensor("sb_sel", [8, 1024], F32)
    C["eb2"] = nc.alloc_sbuf_tensor("sb_eb2", [8, 1024], F32)
    C["rmsrow"] = nc.alloc_sbuf_tensor("sb_rmsrow", [1, 256], F32)
    C["lamrow"] = nc.alloc_sbuf_tensor("sb_lamrow", [1, 16], F32)
    X.C = C

    PSA = nc.alloc_psum_tensor("psa", [128, 2048], F32)
    X.PSA = PSA
    PB = [PSA.ap()[:, i * 512:(i + 1) * 512] for i in range(4)] + \
         [nc.alloc_psum_tensor(f"pb{i}", [128, 512], F32).ap() for i in range(4, 8)]
    X.PB = PB
    X.PBb = [p.bitcast(BF16) for p in PB]
    PK = [("P", i) for i in range(8)]
    X.PK = PK

    em = EM(nc, es)
    X.em = em

    R = lambda name, c0, cols, dt=F32: Reg(X, name, c0, cols, dt)
    X.XN = [R(f"xn{c}", c * 1024, 1024, BF16) for c in range(DC)]
    X.ROWA = R("rowA", 8192, 2048)
    X.ROWB = R("rowB", 10240, 2048)
    X.ROWC = R("rowC", 12288, 2048)
    X.SMALL = R("small", 14336, 128)
    X.DWFM = R("dwfm", 14464, 2048)
    X.LAMW = R("lamw", 16512, 128)
    X.DWBC = R("dwbc", 16640, 2048)
    X.P0 = 18688

    emit_all(X, debug)

    em.final_wait()
    block = es.enter_context(nc.Block())
    em.emit(block)
    es.close()
    return nc


def k1_broadcast(X, src_row_ap, psb, dst_ap, dst_keys, src_keys, scale=1.0, pbase=0):
    em, C = X.em, X.C
    n = src_row_ap.shape[-1]
    em.op("pe", lambda t: t.matmul(X.PB[psb][:, 0:n], C["ones3"].ap()[pbase:pbase + 1, :],
                                   src_row_ap, start=True, stop=True),
          r=src_keys + [("C", "ones3")], w=[X.PK[psb]])
    em.op("act", lambda s: s.activation(dst_ap, X.PB[psb][:, 0:n], AF.Copy, scale=scale),
          r=[X.PK[psb]], w=dst_keys)


def stat_mm(X, psb, row_slice, rhs_ap, rhs_keys, start, stop, prow=0):
    em = X.em
    em.grp("pe", [lambda t: t.matmul(X.PB[psb][prow:prow + 1, row_slice],
                                     X.C["ones_c"].ap(), rhs_ap, start=start, stop=stop)],
           r=rhs_keys + [("C", "ones_c")], w=[X.PK[psb]])


def emit_ln_rows_slice(X, layer, j, sum_b, sq_b):
    em = X.em
    RA, RB, RC = X.ROWA, X.ROWB, X.ROWC
    row = lambda R, c0, n=512: R.ap(p0=0, pn=1, rc0=c0, rcols=n)
    ss = lambda i: row(RB, i * 512)
    em.op("act", lambda s: s.activation(row(RA, j * 512), X.PB[sum_b][0:1, :],
                                        AF.Copy, scale=1.0 / D),
          r=[X.PK[sum_b]], w=RA.keys())
    em.op("act", lambda s: s.activation(ss(0), row(RA, j * 512), AF.Square),
          r=RA.keys(), w=RB.keys())
    em.op("dve", lambda v: v.scalar_tensor_tensor(ss(1), X.PB[sq_b][0:1, :],
                                                  1.0 / D, ss(0), ALU.mult,
                                                  ALU.subtract),
          r=[X.PK[sq_b]] + RB.keys(), w=RB.keys())
    em.op("dve", lambda v: v.tensor_scalar(ss(2), ss(1), 1e-5, None, ALU.add),
          r=RB.keys(), w=RB.keys())
    em.op("act", lambda s: s.activation(ss(3), ss(2), AF.Ln), r=RB.keys(), w=RB.keys())
    em.op("act", lambda s: s.activation(row(RC, j * 512), ss(3), AF.Exp, scale=-0.5),
          r=RB.keys(), w=RC.keys())
    if layer == 1:
        em.op("dve", lambda v: v.reciprocal(ss(3), ss(2)), r=RB.keys(), w=RB.keys())
        em.op("dve", lambda v: v.tensor_tensor(ss(0), ss(1), ss(3), ALU.mult),
              r=RB.keys(), w=RB.keys())
        em.op("dve", lambda v: v.tensor_scalar(ss(0), ss(0), 1e-8, None, ALU.add),
              r=RB.keys(), w=RB.keys())
        em.op("act", lambda s: s.activation(ss(3), ss(0), AF.Ln), r=RB.keys(), w=RB.keys())
        em.op("act", lambda s: s.activation(
            X.DWFM.ap(p0=0, pn=1, rc0=j * 512, rcols=512), ss(3), AF.Exp, scale=-0.5),
            r=RB.keys(), w=X.DWFM.keys())


def emit_consts(X):
    em, C, d = X.em, X.C, X.d
    kc = lambda n: [("C", n)]
    em.dma("sp", C["idb"].ap(), d["ident_b"], w=kc("idb"))
    em.dma("sp", C["idf"].ap(), d["ident_f"], w=kc("idf"))
    em.dma("sp", C["ones_c"].ap(), d["ones_c"], w=kc("ones_c"))
    em.dma("sp", C["ones_r"].ap(), d["ones_r"], w=kc("ones_r"))
    em.dma("sp", C["ones3"].ap(), d["ones3"], w=kc("ones3"))
    em.dma("sp", C["rmsrow"].ap(), d["rms_row"], w=kc("rmsrow"))
    em.dma("sp", C["gate"].ap(), d["gate_t"].rearrange("(c p) e -> p c e", p=128),
           w=kc("gate"))
    em.dma("sp", C["rout"].ap(), d["rout_t"].rearrange("(c p) e -> p c e", p=128),
           w=kc("rout"))
    em.dma("sp", C["eb1c"].ap(), d["eb1c"], w=kc("eb1c"))
    em.dma("sp", C["sb1c"].ap(), d["sb1c"], w=kc("sb1c"))
    em.dma("sp", C["sb2c"].ap(), d["sb2c"], w=kc("sb2c"))
    em.dma("sp", C["nwc"].ap(), d["nwc"], w=kc("nwc"))
    em.dma("sp", C["lamA"].ap(), d["lamA"], w=kc("lamA"))
    em.dma("sp", C["lamB"].ap(), d["lamB"], w=kc("lamB"))
    em.dma("sp", C["sel"].ap(), d["sel"], w=kc("sel"))
    em.dma("sp", C["eb2"].ap(), d["eb2"], w=kc("eb2"))
    for c in range(DC):
        em.op("dve", lambda v, c=c: v.tensor_scalar(
            C["routnw"].ap()[:, c, :], C["rout"].ap()[:, c, :],
            C["nwc"].ap()[:, c:c + 1], None, ALU.mult),
            r=kc("rout") + kc("nwc"), w=kc("routnw"))
    em.op("pe", lambda t: t.matmul(X.PB[6][:, 0:256], C["ones_r"].ap(),
                                   C["rmsrow"].ap(), start=True, stop=True),
          r=kc("rmsrow") + kc("ones_r"), w=[X.PK[6]])
    em.op("act", lambda s: s.activation(C["rms_bc"].ap(), X.PB[6][:, 0:256],
                                        AF.Copy, scale=(1.0 - LAMBDA_INIT)),
          r=[X.PK[6]], w=kc("rms_bc"))
    LW = X.LAMW
    for i, nm in enumerate(("lamA", "lamB")):
        t = C[nm]
        em.op("dve", lambda v, t=t, i=i: v.tensor_tensor(
            LW.ap(p0=0, pn=8, rc0=0, rcols=128)[0:8, :],
            t.ap()[:, 0:128], t.ap()[:, 128:256], ALU.mult),
            r=kc(nm), w=LW.keys())
        em.op("dve", lambda v, i=i: v.reduce_sum(
            X.SMALL.ap(p0=0, pn=8, rc0=8 + i, rcols=1)[0:8, :],
            LW.ap(p0=0, pn=8)[0:8, 0:128], axis=AX.X),
            r=LW.keys(), w=X.SMALL.keys())
        em.op("pe", lambda t_, i=i: t_.transpose(
            X.PB[6][0:1, i * 8:(i + 1) * 8],
            X.SMALL.ap(p0=0, pn=8, rc0=8 + i, rcols=1)[0:8, :],
            C["idf"].ap()[0:8, 0:8]),
            r=X.SMALL.keys() + kc("idf"), w=[X.PK[6]])
        em.op("act", lambda s, i=i: s.activation(
            C["lamrow"].ap()[0:1, i * 8:(i + 1) * 8],
            X.PB[6][0:1, i * 8:(i + 1) * 8], AF.Exp),
            r=[X.PK[6]], w=kc("lamrow"))
    em.op("dve", lambda v: v.tensor_tensor(
        C["lamrow"].ap()[0:1, 8:16], C["lamrow"].ap()[0:1, 8:16],
        C["lamrow"].ap()[0:1, 0:8], ALU.subtract),
        r=kc("lamrow"), w=kc("lamrow"))
    em.op("pe", lambda t: t.matmul(X.PB[6][:, 0:8], C["ones_r"].ap(),
                                   C["lamrow"].ap()[0:1, 8:16], start=True, stop=True),
          r=kc("lamrow") + kc("ones_r"), w=[X.PK[6]])
    em.op("dve", lambda v: v.tensor_scalar(C["nlam"].ap(), X.PB[6][:, 0:8],
                                           -LAMBDA_INIT, None, ALU.add),
          r=[X.PK[6]], w=kc("nlam"))


def emit_ln(X, layer, src, STR0, STR1, SQ, MBM, MBR):
    em, C = X.em, X.C
    dsrc = X.d["xT"] if src == "x0" else X.x1d
    dkey = [("D", src)]
    strs = [STR0, STR1]
    if src == "x0":
        for c in range(DC):
            S = strs[c % 2]
            em.dma("sp", S.ap(), dsrc[c * 128:(c + 1) * 128, :], r=dkey, w=S.keys())
            em.op("act", lambda s, S=S: s.activation(SQ.ap(), S.ap(), AF.Square),
                  r=S.keys(), w=SQ.keys())
            for j in range(4):
                stat_mm(X, j, slice(0, 512), S.ap(rc0=j * 512, rcols=512), S.keys(j * 512, 512),
                        start=(c == 0), stop=(c == DC - 1))
                stat_mm(X, 4 + j, slice(0, 512), SQ.ap(rc0=j * 512, rcols=512), SQ.keys(j * 512, 512),
                        start=(c == 0), stop=(c == DC - 1))
        emit_ln_rows(X, layer)
    RA, RC = X.ROWA, X.ROWC
    for j in range(4):
        k1_broadcast(X, RA.ap(p0=0, pn=1, rc0=j * 512, rcols=512), 6,
                     MBM.ap(rc0=j * 512, rcols=512), MBM.keys(j * 512, 512), RA.keys(), 1.0)
        k1_broadcast(X, RC.ap(p0=0, pn=1, rc0=j * 512, rcols=512), 7,
                     MBR.ap(rc0=j * 512, rcols=512), MBR.keys(j * 512, 512), RC.keys(), 1.0)
    if layer == 1:
        emit_alpha(X)
    for c in range(DC):
        S = strs[c % 2]
        em.dma("sp", S.ap(), dsrc[c * 128:(c + 1) * 128, :], r=dkey, w=S.keys())
        em.op("dve", lambda v, S=S: v.tensor_tensor(SQ.ap(), S.ap(), MBM.ap(), ALU.subtract),
              r=S.keys() + MBM.keys(), w=SQ.keys())
        em.op("dve", lambda v, S=S: v.tensor_tensor(S.ap(), SQ.ap(), MBR.ap(), ALU.mult),
              r=SQ.keys() + MBR.keys(), w=S.keys())
        em.op("dve", lambda v, c=c, S=S: v.tensor_copy(X.XN[c].ap(), S.ap()),
              r=S.keys(), w=X.XN[c].keys())
        if layer == 0:
            for t in range(16):
                em.grp("pe", [lambda tt_, c=c, t=t, S=S: tt_.matmul(
                    X.PB[0][:, t * 8:(t + 1) * 8],
                    S.ap(rc0=t * 128, rcols=128), X.C["gate"].ap()[:, c, :],
                    start=(c == 0 and t == 0), stop=(c == DC - 1 and t == 15))],
                    r=S.keys() + [("C", "gate")], w=[X.PK[0]])


def emit_ln_rows(X, layer):
    for j in range(4):
        emit_ln_rows_slice(X, layer, j, j, 4 + j)


def emit_alpha(X):
    em = X.em
    em.op("dve", lambda v: v.tensor_tensor(
        X.ROWC.ap(p0=0, pn=1), X.ROWC.ap(p0=0, pn=1),
        X.DWFM.ap(p0=0, pn=1, rc0=0, rcols=2048), ALU.mult),
        r=X.ROWC.keys() + X.DWFM.keys(), w=X.ROWC.keys())


def emit_mixer_l0(X):
    em, C = X.em, X.C
    SM = X.SMALL
    P0 = X.P0
    LG = Reg(X, "lg", P0, 512)
    for tile in range(16):
        emit_mixer_tail(X, tile, LG, l1=False, logits_ap=X.PB[0][:, tile * 8:(tile + 1) * 8],
                        logits_keys=[X.PK[0]])


def emit_mixer_tail(X, tile, LG, l1, logits_ap, logits_keys, tr_bank=6):
    em, C = X.em, X.C
    SM = X.SMALL
    sm = lambda i: SM.ap(rc0=16 + i, rcols=1)
    smk = SM.keys()
    g = lambda i: LG.ap(rc0=i, rcols=8)
    gk = LG.keys()
    if l1:
        em.op("act", lambda s: s.activation(g(0), logits_ap, AF.Sigmoid), r=logits_keys, w=gk)
    else:
        em.op("dve", lambda v: v.tensor_copy(g(0), logits_ap), r=logits_keys, w=gk)
    em.op("dve", lambda v: v.reduce_max(sm(0), g(0), axis=AX.X), r=gk, w=smk)
    em.op("dve", lambda v: v.tensor_scalar(g(56), g(0), sm(0), None, ALU.is_ge), r=gk + smk, w=gk)
    em.op("dve", lambda v: v.scalar_tensor_tensor(g(64), g(56), -1e30 if not l1 else -1e3,
                                                  g(0), ALU.mult, ALU.add), r=gk, w=gk)
    em.op("dve", lambda v: v.reduce_max(sm(4), g(64), axis=AX.X), r=gk, w=smk)
    if not l1:
        em.op("dve", lambda v: v.reduce_max(sm(1), g(0), axis=AX.X, negate=True), r=gk, w=smk)
        em.op("dve", lambda v: v.reduce_max(sm(5), g(64), axis=AX.X, negate=True), r=gk, w=smk)
        em.op("act", lambda sc: sc.activation(g(8), g(0), AF.Abs), r=gk, w=gk)
        for br, (vv, nvv, zz, rr, srccol) in enumerate([(0, 1, 2, 3, 0), (4, 5, 6, 7, 64)]):
            em.op("dve", lambda v, vv=vv: v.tensor_scalar(g(16), g(8), sm(vv), None, ALU.max),
                  r=gk + smk, w=gk)
            em.op("dve", lambda v: v.reciprocal(g(24), g(16)), r=gk, w=gk)
            em.op("dve", lambda v, vv=vv: v.tensor_scalar(g(16), g(0), sm(vv), None, ALU.subtract),
                  r=gk + smk, w=gk)
            em.op("dve", lambda v: v.tensor_tensor(g(16), g(16), g(24), ALU.mult), r=gk, w=gk)
            em.op("dve", lambda v: v.tensor_scalar(g(32), g(16), -0.02, None, ALU.is_ge),
                  r=gk, w=gk)
            em.op("act", lambda s, nvv=nvv, srccol=srccol: s.activation(
                g(40), g(srccol), AF.Exp, bias=sm(nvv)), r=gk + smk, w=gk)
            em.op("dve", lambda v: v.tensor_tensor(g(48), g(32), g(40), ALU.mult), r=gk, w=gk)
            em.op("dve", lambda v, zz=zz: v.reduce_sum(sm(zz), g(48), axis=AX.X), r=gk, w=smk)
            em.op("dve", lambda v, zz=zz, rr=rr: v.reciprocal(sm(rr), sm(zz)), r=smk, w=smk)
    else:
        em.op("dve", lambda v: v.tensor_tensor(sm(2), sm(0), sm(4), ALU.add), r=smk, w=smk)
        em.op("dve", lambda v: v.tensor_scalar(sm(2), sm(2), 1e-6, None, ALU.add), r=smk, w=smk)
        em.op("dve", lambda v: v.reciprocal(sm(6), sm(2)), r=smk, w=smk)
        em.op("dve", lambda v: v.tensor_tensor(sm(3), sm(0), sm(6), ALU.mult), r=smk, w=smk)
        em.op("dve", lambda v: v.tensor_tensor(sm(7), sm(4), sm(6), ALU.mult), r=smk, w=smk)
    em.op("dve", lambda v: v.tensor_scalar(g(72), g(64), sm(4), None, ALU.is_ge),
          r=gk + smk, w=gk)
    em.op("dve", lambda v: v.tensor_scalar(g(80), g(56), sm(3), None, ALU.mult),
          r=gk + smk, w=gk)
    em.op("dve", lambda v: v.scalar_tensor_tensor(g(80), g(72), sm(7), g(80), ALU.mult, ALU.add),
          r=gk + smk, w=gk)
    em.op("pe", lambda t: t.transpose(X.PB[tr_bank][0:8, 0:128], g(80), C["idf"].ap()),
          r=gk + [("C", "idf")], w=[X.PK[tr_bank]])
    em.op("dve", lambda v: v.tensor_copy(
        X.DWFM.ap(p0=0, pn=8, rc0=tile * 128, rcols=128)[0:8, :], X.PB[tr_bank][0:8, 0:128]),
        r=[X.PK[tr_bank]], w=X.DWFM.keys(tile * 128, 128))


def emit_l0_attn(X):
    em, C = X.em, X.C
    P0 = X.P0
    R = lambda n, c0, cols, dt=F32: Reg(X, n, c0, cols, dt)
    WH = [R("wh0", P0, 1024, BF16), R("wh1", P0 + 1024, 1024, BF16),
          R("wh2", P0 + 25600, 1024, BF16)]
    VH = R("vh", P0 + 2048, 2048, BF16)
    QH = R("qh", P0 + 4096, 2048, BF16)
    KH = R("kh", 16640, 2048, BF16)
    RB = [R("rb0", P0 + 6144, 1024, BF16), R("rb1", P0 + 7168, 1024, BF16)]
    ATT = R("att", P0 + 8192, 1024, BF16)
    E1 = R("e1", 8192, 1024, BF16)
    E2 = R("e2", 9216, 1024, BF16)
    X.OALL = R("oall", P0 + 9216, 16384, BF16)
    WH3 = lambda b: WH[b].ap().rearrange("p (c k) -> p c k", c=8)
    VH3 = lambda: VH.ap().rearrange("p (c k) -> p c k", c=16)
    sm = lambda i: X.SMALL.ap(rc0=i, rcols=1)
    smk = X.SMALL.keys()
    skc = lambda i, n=1: [("S", j) for j in range(i, i + n)]
    d = X.d
    PK = X.PK

    def stage_a_scores(h, qt, br):
        qoff = br * 2048 + qt * 128
        base = 40 + br * 12
        for i in range(4):
            em.op("pe", lambda t, i=i, qoff=qoff, br=br: t.matmul(
                X.PB[i][:, :], QH.ap(rc0=qoff, rcols=128),
                KH.ap(rc0=br * 2048 + i * 512, rcols=512),
                start=True, stop=True),
                r=QH.keys(qoff, 128) + KH.keys(br * 2048 + i * 512, 512),
                w=[PK[i]])
            if i % 2 == 1:
                em.op("dve", lambda v, i=i, base=base: v.reduce_max(
                    sm(base + i // 2), X.PSA.ap()[:, (i - 1) * 512:(i + 1) * 512],
                    axis=AX.X, negate=True),
                    r=[PK[i - 1], PK[i]], w=skc(base + i // 2))

    def stage_a_softmax(h, qt, br):
        base = 40 + br * 12
        EB = E1 if br == 0 else E2
        for j in range(2):
            em.op("act", lambda sc, j=j, base=base, EB=EB: sc.activation(
                EB.ap(rc0=j * 1024, rcols=1024),
                X.PSA.ap()[:, j * 1024:(j + 1) * 1024], AF.Exp,
                bias=sm(base + j), accum_out=sm(base + 2 + j)),
                r=[PK[2 * j], PK[2 * j + 1]] + skc(base + j), w=EB.keys() + skc(base + 2 + j))
        em.op("dve", lambda v, base=base: v.tensor_reduce(
            sm(base + 4), X.SMALL.ap(rc0=base, rcols=2), AX.X, ALU.min),
            r=skc(base, 2), w=skc(base + 4))
        em.op("dve", lambda v, base=base: v.tensor_scalar(
            X.SMALL.ap(rc0=base + 5, rcols=2), X.SMALL.ap(rc0=base, rcols=2),
            sm(base + 4), -1.0, ALU.subtract, ALU.mult),
            r=skc(base, 2) + skc(base + 4), w=skc(base + 5, 2))
        em.op("act", lambda sc, base=base: sc.activation(
            X.SMALL.ap(rc0=base + 7, rcols=2), X.SMALL.ap(rc0=base + 5, rcols=2),
            AF.Exp), r=skc(base + 5, 2), w=skc(base + 7, 2))
        em.op("dve", lambda v, base=base: v.tensor_tensor(
            X.SMALL.ap(rc0=base + 9, rcols=2), X.SMALL.ap(rc0=base + 2, rcols=2),
            X.SMALL.ap(rc0=base + 7, rcols=2), ALU.mult),
            r=skc(base + 2, 2) + skc(base + 7, 2), w=skc(base + 9, 2))
        em.op("dve", lambda v, base=base: v.reduce_sum(
            sm(base + 11), X.SMALL.ap(rc0=base + 9, rcols=2), AX.X),
            r=skc(base + 9, 2), w=skc(base + 11))
        em.op("dve", lambda v, base=base: v.reciprocal(sm(base + 11), sm(base + 11)),
              r=skc(base + 11), w=skc(base + 11))

    def stage_a_combine(h, qt):
        RBt = RB[qt % 2]
        em.op("dve", lambda v, h=h: v.tensor_tensor(sm(76), sm(63),
                                                    C["nlam"].ap()[:, h:h + 1], ALU.mult),
              r=skc(63) + [("C", "nlam")], w=skc(76))
        em.op("dve", lambda v: v.tensor_scalar(
            X.SMALL.ap(rc0=78, rcols=2), X.SMALL.ap(rc0=47, rcols=2),
            sm(51), None, ALU.mult), r=skc(47, 2) + skc(51), w=skc(78, 2))
        em.op("dve", lambda v: v.tensor_scalar(
            X.SMALL.ap(rc0=80, rcols=2), X.SMALL.ap(rc0=59, rcols=2),
            sm(76), None, ALU.mult), r=skc(59, 2) + skc(76), w=skc(80, 2))
        for j in range(2):
            em.op("dve", lambda v, j=j: v.tensor_scalar(
                E1.ap(rc0=j * 1024, rcols=1024), E1.ap(rc0=j * 1024, rcols=1024),
                sm(78 + j), None, ALU.mult),
                r=E1.keys() + skc(78 + j), w=E1.keys())
            em.op("dve", lambda v, j=j: v.tensor_scalar(
                E2.ap(rc0=j * 1024, rcols=1024), E2.ap(rc0=j * 1024, rcols=1024),
                sm(80 + j), None, ALU.mult),
                r=E2.keys() + skc(80 + j), w=E2.keys())
        em.op("dve", lambda v, RBt=RBt: v.tensor_tensor(
            RBt.ap(), E1.ap(), E2.ap(), ALU.add),
            r=E2.keys() + E1.keys(), w=RBt.keys())

    def stage_b(h, qt):
        RBt = RB[qt % 2]
        for half16 in range(2):
            pb = 6 + half16
            for j in range(8):
                kc = half16 * 8 + j
                em.op("pe", lambda t, kc=kc, j=j, pb=pb, RBt=RBt: t.transpose(
                    X.PBb[pb][:, j * 128:(j + 1) * 128],
                    RBt.ap(rc0=kc * 128, rcols=128), C["idb"].ap()),
                    r=RBt.keys(kc * 128, 128) + [("C", "idb")], w=[PK[pb]])
            em.op("act", lambda sc, half16=half16, pb=pb: sc.activation(
                ATT.ap(rc0=half16 * 1024, rcols=1024), X.PBb[pb][:, 0:1024], AF.Copy),
                r=[PK[pb]], w=ATT.keys(half16 * 1024, 1024))
        pbO = 4 + qt % 2
        fns = []
        for kc in range(16):
            fns.append(lambda t, kc=kc, pbO=pbO: t.matmul(
                X.PB[pbO][:, 0:256], ATT.ap(rc0=kc * 128, rcols=128),
                VH3()[:, kc, :], start=(kc == 0), stop=(kc == 15)))
        em.grp("pe", fns, r=ATT.keys() + VH.keys(), w=[PK[pbO]])
        em.op("act", lambda sc, pbO=pbO, RBt=RBt: sc.activation(
            RBt.ap(rc0=256, rcols=256), X.PB[pbO][:, 0:256], AF.Square,
            accum_out=sm(84)),
            r=[PK[pbO]], w=RBt.keys(0, 512) + skc(84))
        em.op("dve", lambda v: v.tensor_scalar(sm(85), sm(84), 1.0 / 256.0, 1e-5,
                                               ALU.mult, ALU.add), r=skc(84), w=skc(85))
        em.op("act", lambda sc: sc.activation(sm(85), sm(85), AF.Ln), r=skc(85), w=skc(85))
        em.op("act", lambda sc: sc.activation(sm(86), sm(85), AF.Exp, scale=-0.5),
              r=skc(85), w=skc(86))
        em.op("dve", lambda v, pbO=pbO, RBt=RBt: v.scalar_tensor_tensor(
            RBt.ap(rc0=0, rcols=256), X.PB[pbO][:, 0:256], sm(86),
            C["rms_bc"].ap(), ALU.mult, ALU.mult),
            r=[PK[pbO]] + skc(86) + [("C", "rms_bc")], w=RBt.keys(0, 512))
        for j in range(2):
            pb = 6 + j
            em.op("pe", lambda t, j=j, pb=pb, RBt=RBt: t.transpose(
                X.PBb[pb][:, 0:128], RBt.ap(rc0=j * 128, rcols=128), C["idb"].ap()),
                r=RBt.keys(0, 512) + [("C", "idb")], w=[PK[pb]])
            ec = 2 * h + j
            em.op("dve", lambda v, ec=ec, qt=qt, pb=pb: v.tensor_copy(
                X.OALL.ap(rc0=ec * 2048 + qt * 128, rcols=128), X.PBb[pb][:, 0:128]),
                r=[PK[pb]], w=X.OALL.keys(ec * 2048 + qt * 128, 128))

    for h in range(NH):
        wsrc = [("wq_t", 0), ("wk_t", 1), ("wv_t", 2)]
        for nm, b in wsrc:
            em.dma("gp", WH3(b)[:, :, :],
                   d[nm][:, h * 256:(h + 1) * 256].rearrange("(c p) n -> p c n", p=128),
                   r=[("D", nm)], w=WH[b].keys())
            if nm == "wq_t":
                for j in range(2):
                    for nk in range(4):
                        pb = 4 + nk % 2
                        fns = []
                        for c in range(DC):
                            fns.append(lambda t, c=c, j=j, nk=nk, pb=pb, b=b: t.matmul(
                                X.PB[pb][:, :], WH3(b)[:, c, j * 128:(j + 1) * 128],
                                X.XN[c].ap(rc0=nk * 512, rcols=512),
                                start=(c == 0), stop=(c == DC - 1)))
                        em.grp("pe", fns, r=WH[b].keys() + sum(
                            [X.XN[c].keys(nk * 512, 512) for c in range(DC)], []),
                            w=[PK[pb]])
                        em.op("act", lambda sc, j=j, nk=nk, pb=pb: sc.activation(
                            QH.ap(rc0=j * 2048 + nk * 512, rcols=512), X.PB[pb][:, :],
                            AF.Copy, scale=SC0),
                            r=[PK[pb]], w=QH.keys(j * 2048 + nk * 512, 512))
            elif nm == "wk_t":
                for j in range(2):
                    for nk in range(4):
                        pb = 4 + nk % 2
                        fns = []
                        for c in range(DC):
                            fns.append(lambda t, c=c, j=j, nk=nk, pb=pb, b=b: t.matmul(
                                X.PB[pb][:, :], WH3(b)[:, c, j * 128:(j + 1) * 128],
                                X.XN[c].ap(rc0=nk * 512, rcols=512),
                                start=(c == 0), stop=(c == DC - 1)))
                        em.grp("pe", fns, r=WH[b].keys() + sum(
                            [X.XN[c].keys(nk * 512, 512) for c in range(DC)], []),
                            w=[PK[pb]])
                        em.op("act", lambda sc, j=j, nk=nk, pb=pb: sc.activation(
                            KH.ap(rc0=j * 2048 + nk * 512, rcols=512), X.PB[pb][:, :],
                            AF.Copy),
                            r=[PK[pb]], w=KH.keys(j * 2048 + nk * 512, 512))
            else:
                for kt in range(16):
                    pb = 4 + kt % 2
                    fns = []
                    for c in range(DC):
                        fns.append(lambda t, c=c, kt=kt, pb=pb, b=b: t.matmul(
                            X.PB[pb][:, 0:256], X.XN[c].ap(rc0=kt * 128, rcols=128),
                            WH3(b)[:, c, :], start=(c == 0), stop=(c == DC - 1)))
                    em.grp("pe", fns, r=WH[b].keys() + sum(
                        [X.XN[c].keys(kt * 128, 128) for c in range(DC)], []),
                        w=[PK[pb]])
                    em.op("act", lambda sc, kt=kt, pb=pb: sc.activation(
                        VH3()[:, kt, :], X.PB[pb][:, 0:256], AF.Copy),
                        r=[PK[pb]], w=VH.keys(kt * 256, 256))

        for qt in range(16):
            stage_a_scores(h, qt, 0)
            if qt > 0:
                stage_b(h, qt - 1)
            stage_a_softmax(h, qt, 0)
            stage_a_scores(h, qt, 1)
            stage_a_softmax(h, qt, 1)
            stage_a_combine(h, qt)
        stage_b(h, 15)


def emit_l0_wo(X):
    em, C, d = X.em, X.C, X.d
    P0 = X.P0
    WOF = Reg(X, "wof", P0, 8192, BF16)
    SEV = Reg(X, "sev", P0 + 8192, 1024, BF16)
    WOF3 = lambda: WOF.ap().rearrange("p (c k) -> p c k", c=16)
    em.dma("gp", WOF3(), d["wo_t"].rearrange("(c p) n -> p c n", p=128),
           r=[("D", "wo_t")], w=WOF.keys())
    for dc in range(DC):
        for nk in range(4):
            pb = 4 + nk % 2
            fns = []
            for ec in range(16):
                fns.append(lambda t, ec=ec, dc=dc, nk=nk, pb=pb: t.matmul(
                    X.PB[pb][:, :], WOF3()[:, ec, dc * 128:(dc + 1) * 128],
                    X.OALL.ap(rc0=ec * 2048 + nk * 512, rcols=512),
                    start=(ec == 0), stop=(ec == 15)))
            em.grp("pe", fns, r=WOF.keys() + X.OALL.keys(), w=[X.PK[pb]])
            slot = nk % 2
            em.op("act", lambda sc, pb=pb, slot=slot: sc.activation(
                SEV.ap(rc0=slot * 512, rcols=512), X.PB[pb][:, :], AF.Copy),
                r=[X.PK[pb]], w=SEV.keys(slot * 512, 512))
            em.dma("sp", X.s0d[dc * 128:(dc + 1) * 128, nk * 512:(nk + 1) * 512],
                   SEV.ap(rc0=slot * 512, rcols=512), r=SEV.keys(slot * 512, 512),
                   w=[("D", "s0")])


def emit_l0_ffn(X):
    em, C, d = X.em, X.C, X.d
    P0 = X.P0
    R = lambda n, c0, cols, dt=F32: Reg(X, n, c0, cols, dt)
    W1 = [R("w1a", P0, 2048, BF16), R("w1b", P0 + 2048, 2048, BF16)]
    W3 = [R("w3a", P0 + 4096, 2048, BF16), R("w3b", P0 + 6144, 2048, BF16)]
    W2 = [R("w2a", P0 + 8192, 1024, BF16), R("w2b", P0 + 9216, 1024, BF16)]
    ACTW = R("actw", P0 + 10240, 8192, BF16)
    GSIL = R("gsil", P0 + 18432, 512)
    SIL = [GSIL, R("silb", 10240, 512)]
    FLSP = [R("flsa", 10752, 512), R("flsb", 11264, 512)]
    G = [R(f"g{o}", 37888 + o * 1024, 1024) for o in range(8)]
    X.G = G
    w3v = lambda W: W.ap().rearrange("p (c k) -> p c k", c=8)
    aw = lambda ic, nt: ACTW.ap(rc0=(ic * 2 + nt) * 512, rcols=512)
    awk = lambda ic, nt: ACTW.keys((ic * 2 + nt) * 512, 512)
    DW = X.DWBC

    XST = [R("xst0", P0 + 10240, 512), R("xst1", P0 + 10752, 512)]
    SST = [R("sst0", P0 + 11264, 256, BF16), R("sst1", P0 + 11520, 256, BF16)]
    X1T = [R("x1t0", P0 + 11776, 512), R("x1t1", P0 + 12288, 512)]
    SQT = [R("sqt0", P0 + 12800, 512), R("sqt1", P0 + 13312, 512)]
    FLS = GSIL

    for half in range(2):
        for o in range(8):
            em.op("dve", lambda v, o=o: v.memset(G[o].ap(), 0.0), w=G[o].keys())
        for e in range(8):
            for jslice in range(2):
                em.op("pe", lambda t, e=e, half=half, jslice=jslice: t.matmul(
                    X.PB[6][:, :],
                    C["sel"].ap()[:, e * 128:(e + 1) * 128],
                    X.DWFM.ap(p0=0, pn=8, rc0=half * 1024 + jslice * 512, rcols=512),
                    start=True, stop=True),
                    r=[("C", "sel")] + X.DWFM.keys(), w=[X.PK[6]])
                em.op("dve", lambda v, jslice=jslice: v.tensor_copy(
                    DW.ap(rc0=jslice * 512, rcols=512), X.PB[6][:, :]),
                    r=[X.PK[6]], w=DW.keys(jslice * 512, 512))
            def load_band(e_, band_):
                bb_ = band_ % 2
                for hv in range(2):
                    em.dma("gp", w3v(W1[bb_])[:, hv * 4:(hv + 1) * 4, :],
                           d["w1t"][e_][hv * 512:(hv + 1) * 512,
                                        band_ * 512:(band_ + 1) * 512]
                           .rearrange("(c p) n -> p c n", p=128),
                           r=[("D", "w1t")], w=W1[bb_].keys(hv * 2048, 2048))
                    em.dma("gp", w3v(W3[bb_])[:, hv * 4:(hv + 1) * 4, :],
                           d["w3t"][e_][hv * 512:(hv + 1) * 512,
                                        band_ * 512:(band_ + 1) * 512]
                           .rearrange("(c p) n -> p c n", p=128),
                           r=[("D", "w3t")], w=W3[bb_].keys(hv * 2048, 2048))
            X.l0_load_band = load_band
            if e == 0 and half == 0:
                load_band(0, 0)
                load_band(0, 1)
            for band in range(4):
                bb = band % 2
                for ici in range(4):
                    ic = band * 4 + ici
                    for nt in range(2):
                        pbA = (ic * 2 + nt) % 2
                        pbB = 2 + pbA
                        noff = half * 1024 + nt * 512
                        fns = []
                        for c in range(DC):
                            fns.append(lambda t, c=c, ici=ici, bb=bb, pbA=pbA, noff=noff:
                                       t.matmul(X.PB[pbA][:, :],
                                                w3v(W1[bb])[:, c, ici * 128:(ici + 1) * 128],
                                                X.XN[c].ap(rc0=noff, rcols=512),
                                                start=(c == 0), stop=(c == DC - 1)))
                        em.grp("pe", fns, r=W1[bb].keys() +
                               sum([X.XN[c].keys(noff, 512) for c in range(DC)], []),
                               w=[X.PK[pbA]])
                        fns = []
                        for c in range(DC):
                            fns.append(lambda t, c=c, ici=ici, bb=bb, pbB=pbB, noff=noff:
                                       t.matmul(X.PB[pbB][:, :],
                                                w3v(W3[bb])[:, c, ici * 128:(ici + 1) * 128],
                                                X.XN[c].ap(rc0=noff, rcols=512),
                                                start=(c == 0), stop=(c == DC - 1)))
                        em.grp("pe", fns, r=W3[bb].keys() +
                               sum([X.XN[c].keys(noff, 512) for c in range(DC)], []),
                               w=[X.PK[pbB]])
                        par = (ic * 2 + nt) % 2
                        em.op("act", lambda sc, pbA=pbA, par=par: sc.activation(
                            SIL[par].ap(), X.PB[pbA][:, :], AF.Silu),
                            r=[X.PK[pbA]], w=SIL[par].keys())
                        em.op("dve", lambda v, ic=ic, nt=nt, pbB=pbB, par=par: v.tensor_tensor(
                            aw(ic, nt), SIL[par].ap(), X.PB[pbB][:, :], ALU.mult),
                            r=SIL[par].keys() + [X.PK[pbB]], w=awk(ic, nt))
                if band + 2 < 4:
                    load_band(e, band + 2)
            def load_w2(e_, o_):
                for hv in range(2):
                    em.dma("gp", W2[o_ % 2].ap(rc0=hv * 1024, rcols=1024),
                           d["w2cm"][e_, o_][:, hv * 8:(hv + 1) * 8, :]
                           .rearrange("p c k -> p (c k)"),
                           r=[("D", "w2cm")], w=W2[o_ % 2].keys(hv * 1024, 1024))
            load_w2(e, 0)
            load_w2(e, 1)
            for o in range(8):
                ob = o % 2
                if o >= 2:
                    pass
                for nt in range(2):
                    pbo = 4 + (o * 2 + nt) % 2
                    fns = []
                    for ic in range(16):
                        fns.append(lambda t, ic=ic, ob=ob, nt=nt, pbo=pbo: t.matmul(
                            X.PB[pbo][:, :],
                            W2[ob].ap(rc0=ic * 128, rcols=128),
                            aw(ic, nt), start=(ic == 0), stop=(ic == 15)))
                    em.grp("pe", fns, r=W2[ob].keys() + ACTW.keys(), w=[X.PK[pbo]])
                    par = (o * 2 + nt) % 2
                    em.op("dve", lambda v, nt=nt, pbo=pbo, par=par: v.tensor_tensor(
                        FLSP[par].ap(), X.PB[pbo][:, :], DW.ap(rc0=nt * 512, rcols=512),
                        ALU.mult),
                        r=[X.PK[pbo]] + DW.keys(nt * 512, 512), w=FLSP[par].keys())
                    em.op("dve", lambda v, o=o, nt=nt, par=par: v.tensor_tensor(
                        G[o].ap(rc0=nt * 512, rcols=512),
                        G[o].ap(rc0=nt * 512, rcols=512), FLSP[par].ap(), ALU.add),
                        r=G[o].keys(nt * 512, 512) + FLSP[par].keys(),
                        w=G[o].keys(nt * 512, 512))
                if o + 2 < 8:
                    load_w2(e, o + 2)
                nxt = (e + 1, half) if e + 1 < 8 else ((0, half + 1) if half == 0 else None)
                if nxt is not None and nxt[1] < 2:
                    if o == 4:
                        X.l0_load_band_next = True
                        load_band(nxt[0], 0)
                    if o == 6:
                        load_band(nxt[0], 1)
        if "g0" in X.dbg:
            for c in range(DC):
                em.dma("sp", X.dbg["g0"][c * 128:(c + 1) * 128,
                                         half * 1024:(half + 1) * 1024],
                       G[c].ap(), r=G[c].keys(), w=[("D", "g0")])
        for nt in range(2):
            j = half * 2 + nt
            sb, qb = 2 * (j % 2), 2 * (j % 2) + 1
            for c in range(DC):
                u = c % 2
                noff = half * 1024 + nt * 512
                em.dma("sp", XST[u].ap(), d["xT"][c * 128:(c + 1) * 128,
                                                  noff:noff + 512],
                       r=[("D", "x0")], w=XST[u].keys())
                em.dma("sp", SST[u].ap(), X.s0d[c * 128:(c + 1) * 128, noff:noff + 512],
                       r=[("D", "s0")], w=SST[u].keys())
                em.op("act", lambda sc, c=c, nt=nt, half=half: sc.activation(
                    GSIL.ap(), X.G[c].ap(rc0=nt * 512, rcols=512), AF.Silu),
                    r=X.G[c].keys(nt * 512, 512), w=GSIL.keys())
                em.op("dve", lambda v, u=u: v.tensor_tensor(
                    GSIL.ap(), GSIL.ap(), SST[u].ap(), ALU.mult),
                    r=GSIL.keys() + SST[u].keys(), w=GSIL.keys())
                em.op("dve", lambda v, u=u: v.tensor_tensor(
                    X1T[u].ap(), GSIL.ap(), XST[u].ap(), ALU.add),
                    r=GSIL.keys() + XST[u].keys(), w=X1T[u].keys())
                em.dma("sp", X.x1d[c * 128:(c + 1) * 128, noff:noff + 512],
                       X1T[u].ap(), r=X1T[u].keys(), w=[("D", "x1")])
                em.op("act", lambda sc, u=u: sc.activation(SQT[u].ap(), X1T[u].ap(),
                                                           AF.Square),
                      r=X1T[u].keys(), w=SQT[u].keys())
                stat_mm(X, sb, slice(0, 512), X1T[u].ap(), X1T[u].keys(),
                        start=(c == 0), stop=(c == DC - 1))
                stat_mm(X, qb, slice(0, 512), SQT[u].ap(), SQT[u].keys(),
                        start=(c == 0), stop=(c == DC - 1))
            emit_ln_rows_slice(X, 1, j, sb, qb)


def emit_l1_attn(X):
    em, C, d = X.em, X.C, X.d
    P0 = X.P0
    R = lambda n, c0, cols, dt=F32: Reg(X, n, c0, cols, dt)
    K8 = R("l1k", P0, 8192, BF16)
    Q8 = R("l1q", P0 + 8192, 8192, BF16)
    WQA = R("l1wa", 16640, 2048, BF16)
    WQB = R("l1wb", 10240, 2048, BF16)
    SCE = R("l1e", 35072, 1024, BF16)
    SCA = R("l1a", 36096, 1024, BF16)
    SCT = SCE
    OTM = R("l1o", 37120, 512, BF16)
    V16 = R("l1v", 37888, 8192, BF16)
    V3 = lambda: V16.ap().rearrange("p (c k) -> p c k", c=16)

    def wv3(c):
        W = WQA if c < 4 else WQB
        return W.ap().rearrange("p (c k) -> p c k", c=4)[:, c % 4, :]

    wkeys = WQA.keys() + WQB.keys()
    sm = lambda i: X.SMALL.ap(rc0=i, rcols=1)
    smk = X.SMALL.keys()
    skc = lambda i, n=1: [("S", j) for j in range(i, i + n)]

    for wi, (nm, scale) in enumerate((("wq1", SC1), ("wk1", 1.0), ("wv1", 1.0))):
        em.dma("gp", WQA.ap().rearrange("p (c k) -> p c k", c=4),
               d[nm][0:512, :].rearrange("(c p) n -> p c n", p=128),
               r=[("D", nm)], w=WQA.keys())
        em.dma("gp", WQB.ap().rearrange("p (c k) -> p c k", c=4),
               d[nm][512:1024, :].rearrange("(c p) n -> p c n", p=128),
               r=[("D", nm)], w=WQB.keys())
        if wi < 2:
            DST = Q8 if wi == 0 else K8
            for ec in range(DC):
                for nk in range(4):
                    pb = 4 + nk % 2
                    fns = []
                    for c in range(DC):
                        fns.append(lambda t, c=c, ec=ec, nk=nk, pb=pb: t.matmul(
                            X.PB[pb][:, :], wv3(c)[:, ec * 128:(ec + 1) * 128],
                            X.XN[c].ap(rc0=nk * 512, rcols=512),
                            start=(c == 0), stop=(c == DC - 1)))
                    em.grp("pe", fns, r=wkeys + sum(
                        [X.XN[c].keys(nk * 512, 512) for c in range(DC)], []),
                        w=[X.PK[pb]])
                    em.op("act", lambda sc, ec=ec, nk=nk, pb=pb, DST=DST, scale=scale:
                          sc.activation(DST.ap(rc0=ec * 2048 + nk * 512, rcols=512),
                                        X.PB[pb][:, :], AF.Copy, scale=scale),
                          r=[X.PK[pb]], w=DST.keys(ec * 2048 + nk * 512, 512))
        else:
            for kt in range(16):
                for eh in range(2):
                    pb = 4 + (kt * 2 + eh) % 2
                    fns = []
                    for c in range(DC):
                        fns.append(lambda t, c=c, kt=kt, eh=eh, pb=pb: t.matmul(
                            X.PB[pb][:, :], X.XN[c].ap(rc0=kt * 128, rcols=128),
                            wv3(c)[:, eh * 512:(eh + 1) * 512],
                            start=(c == 0), stop=(c == DC - 1)))
                    em.grp("pe", fns, r=wkeys + sum(
                        [X.XN[c].keys(kt * 128, 128) for c in range(DC)], []),
                        w=[X.PK[pb]])
                    em.op("act", lambda sc, kt=kt, eh=eh, pb=pb: sc.activation(
                        V3()[:, kt, eh * 512:(eh + 1) * 512], X.PB[pb][:, :], AF.Copy),
                        r=[X.PK[pb]], w=V16.keys(kt * 1024 + eh * 512, 512))

    if "l1q" in X.dbg:
        for ec in range(DC):
            em.op("dve", lambda v, ec=ec: v.tensor_copy(SCE.ap(), Q8.ap(rc0=ec * 2048, rcols=2048)),
                  r=Q8.keys(), w=SCE.keys())
            em.dma("sp", X.dbg["l1q"][ec * 128:(ec + 1) * 128, :].bitcast(BF16)[:, 0:2048],
                   SCE.ap(), r=SCE.keys(), w=[("D", "l1q")])
            em.op("dve", lambda v, ec=ec: v.tensor_copy(SCE.ap(), K8.ap(rc0=ec * 2048, rcols=2048)),
                  r=K8.keys(), w=SCE.keys())
            em.dma("sp", X.dbg["l1k"][ec * 128:(ec + 1) * 128, :].bitcast(BF16)[:, 0:2048],
                   SCE.ap(), r=SCE.keys(), w=[("D", "l1k")])

    for jt in range(16):
        for i in range(4):
            fns = []
            for ec in range(DC):
                fns.append(lambda t, ec=ec, i=i, jt=jt: t.matmul(
                    X.PB[i][:, :], K8.ap(rc0=ec * 2048 + jt * 128, rcols=128),
                    Q8.ap(rc0=ec * 2048 + i * 512, rcols=512),
                    start=(ec == 0), stop=(ec == DC - 1)))
            em.grp("pe", fns, r=K8.keys() + Q8.keys(), w=[X.PK[i]])
        for i in range(4):
            em.op("dve", lambda v, i=i: v.reduce_max(
                sm(40 + i), X.PB[i][:, :], axis=AX.X, negate=True),
                r=[X.PK[i]], w=skc(40 + i))
        em.op("dve", lambda v: v.tensor_reduce(
            sm(44), X.SMALL.ap(rc0=40, rcols=4), AX.X, ALU.min),
            r=skc(40, 4), w=skc(44))
        for i in range(4):
            em.op("act", lambda sc, i=i: sc.activation(
                SCE.ap(rc0=i * 512, rcols=512), X.PB[i][:, :], AF.Exp,
                bias=sm(44), accum_out=sm(45 + i)),
                r=[X.PK[i]] + skc(44), w=SCE.keys(i * 512, 512) + skc(45 + i))
        em.op("dve", lambda v: v.reduce_sum(sm(49), X.SMALL.ap(rc0=45, rcols=4), AX.X),
              r=skc(45, 4), w=skc(49))
        em.op("dve", lambda v: v.reciprocal(sm(50), sm(49)), r=skc(49), w=skc(50))
        em.op("dve", lambda v: v.tensor_scalar(SCA.ap(), SCE.ap(), sm(50), None, ALU.mult),
              r=SCE.keys() + skc(50), w=SCA.keys())
        for kc in range(16):
            pb = 6 + kc % 2
            em.op("pe", lambda t, kc=kc, pb=pb: t.transpose(
                X.PBb[pb][:, 0:128], SCA.ap(rc0=kc * 128, rcols=128), C["idb"].ap()),
                r=SCA.keys(kc * 128, 128) + [("C", "idb")], w=[X.PK[pb]])
            em.op("dve", lambda v, kc=kc, pb=pb: v.tensor_copy(
                SCT.ap(rc0=kc * 128, rcols=128), X.PBb[pb][:, 0:128]),
                r=[X.PK[pb]], w=SCT.keys(kc * 128, 128))
        for eh in range(2):
            pb = 4 + eh
            fns = []
            for kc in range(16):
                fns.append(lambda t, kc=kc, eh=eh, pb=pb: t.matmul(
                    X.PB[pb][:, :], SCT.ap(rc0=kc * 128, rcols=128),
                    V3()[:, kc, eh * 512:(eh + 1) * 512],
                    start=(kc == 0), stop=(kc == 15)))
            em.grp("pe", fns, r=SCT.keys() + V16.keys(), w=[X.PK[pb]])
            em.op("act", lambda sc, eh=eh, pb=pb: sc.activation(
                OTM.ap(rc0=eh * 512, rcols=512), X.PB[pb][:, :], AF.Copy),
                r=[X.PK[pb]], w=OTM.keys(eh * 512, 512))
        if jt == 15 and "l1at" in X.dbg:
            em.dma("sp", X.dbg["l1at"][0:128, :].bitcast(BF16)[:, 0:2048], SCA.ap(),
                   r=SCA.keys(), w=[("D", "l1at")])
            em.dma("sp", X.dbg["l1at"][256:384, 0:64], X.SMALL.ap(rc0=40, rcols=64),
                   r=X.SMALL.keys(), w=[("D", "l1at")])
            em.dma("sp", X.dbg["l1at"][384:512, :].bitcast(BF16)[:, 0:1024],
                   OTM.ap(rc0=0, rcols=1024), r=OTM.keys(), w=[("D", "l1at")])
        for ec in range(DC):
            pb = 6 + ec % 2
            em.op("pe", lambda t, ec=ec, pb=pb: t.transpose(
                X.PBb[pb][:, 0:128], OTM.ap(rc0=ec * 128, rcols=128), C["idb"].ap()),
                r=OTM.keys(ec * 128, 128) + [("C", "idb")], w=[X.PK[pb]])
            em.op("dve", lambda v, ec=ec, jt=jt, pb=pb: v.tensor_copy(
                X.XN[ec].ap(rc0=jt * 128, rcols=128), X.PBb[pb][:, 0:128]),
                r=[X.PK[pb]], w=X.XN[ec].keys(jt * 128, 128))


def emit_l1_ffn(X):
    em, C, d = X.em, X.C, X.d
    P0 = X.P0
    R = lambda n, c0, cols, dt=F32: Reg(X, n, c0, cols, dt)
    NHR = R("nh", P0, 4096, BF16)
    ACTW = R("actw1", P0 + 4096, 8192, BF16)
    MH = R("mh", P0 + 4096, 1024)
    AH = R("ah", P0 + 5120, 1024)
    X1S = [R("x1sa", P0 + 6144, 1024), R("x1sb", P0 + 7168, 1024)]
    XC = R("xc", P0 + 8192, 1024)
    T2 = R("t2", P0 + 9216, 1024)
    LG1 = R("lg1", P0 + 10240, 512)
    W1 = [R("l1w1a", P0 + 12288, 2048, BF16), R("l1w1b", P0 + 14336, 2048, BF16)]
    W2 = [R("l1w2a", P0 + 16384, 1024, BF16), R("l1w2b", P0 + 17408, 1024, BF16)]
    GSIL = R("gsil1", P0 + 18432, 512)
    FLSP = [R("fls1a", 10752, 512), R("fls1b", 11264, 512)]
    G = [R(f"g1_{o}", 37888 + o * 1024, 1024) for o in range(8)]
    X1C = [R("x1ca", P0 + 4096, 512), R("x1cb", P0 + 4608, 512)]
    OUTT = [R("outta", P0 + 5120, 512), R("outtb", P0 + 5632, 512)]
    w1v = lambda W: W.ap().rearrange("p (c k) -> p c k", c=8)
    aw = lambda hc, nt: ACTW.ap(rc0=((hc % 16) * 2 + nt) * 512, rcols=512)
    awk = lambda hc, nt: ACTW.keys(((hc % 16) * 2 + nt) * 512, 512)
    DW = X.DWBC
    nh = lambda c, nt: NHR.ap(rc0=c * 1024 + nt * 512, rcols=512)
    nhk = lambda c, nt: NHR.keys(c * 1024 + nt * 512, 512)

    for half in range(2):
        for j in range(2):
            k1_broadcast(X, X.ROWA.ap(p0=0, pn=1, rc0=half * 1024 + j * 512, rcols=512),
                         2, MH.ap(rc0=j * 512, rcols=512), MH.keys(j * 512, 512),
                         X.ROWA.keys())
            k1_broadcast(X, X.ROWC.ap(p0=0, pn=1, rc0=half * 1024 + j * 512, rcols=512),
                         3, AH.ap(rc0=j * 512, rcols=512), AH.keys(j * 512, 512),
                         X.ROWC.keys())
        for c in range(DC):
            u = c % 2
            em.dma("sp", X1S[u].ap(), X.x1d[c * 128:(c + 1) * 128,
                                           half * 1024:(half + 1) * 1024],
                   r=[("D", "x1")], w=X1S[u].keys())
            em.op("dve", lambda v, u=u: v.tensor_tensor(XC.ap(), X1S[u].ap(), MH.ap(),
                                                        ALU.subtract),
                  r=X1S[u].keys() + MH.keys(), w=XC.keys())
            em.op("dve", lambda v: v.tensor_tensor(T2.ap(), XC.ap(), AH.ap(), ALU.mult),
                  r=XC.keys() + AH.keys(), w=T2.keys())
            for nt in range(2):
                em.op("dve", lambda v, c=c, nt=nt: v.tensor_scalar(
                    nh(c, nt), T2.ap(rc0=nt * 512, rcols=512),
                    C["nwc"].ap()[:, c:c + 1], None, ALU.mult),
                    r=T2.keys() + [("C", "nwc")], w=nhk(c, nt))
            for t in range(8):
                em.grp("pe", [lambda tt_, c=c, t=t: tt_.matmul(
                    X.PB[6][:, t * 8:(t + 1) * 8], T2.ap(rc0=t * 128, rcols=128),
                    C["routnw"].ap()[:, c, :],
                    start=(c == 0 and t == 0), stop=(c == DC - 1 and t == 7))],
                    r=T2.keys() + [("C", "routnw")], w=[X.PK[6]])
        if "l1lg" in X.dbg:
            em.op("dve", lambda v, half=half: v.tensor_copy(
                XC.ap(rc0=0, rcols=64), X.PB[6][:, 0:64]), r=[X.PK[6]], w=XC.keys())
            em.dma("sp", X.dbg["l1lg"][half * 128:(half + 1) * 128, 0:64],
                   XC.ap(rc0=0, rcols=64), r=XC.keys(), w=[("D", "l1lg")])
            em.dma("sp", X.dbg["l1lg"][256:384, 0:64],
                   C["routnw"].ap().rearrange("p c e -> p (c e)"), r=[("C", "routnw")],
                   w=[("D", "l1lg")])
            em.dma("sp", X.dbg["l1lg"][384:512, 0:1024], T2.ap(), r=T2.keys(),
                   w=[("D", "l1lg")])
            for c in range(DC):
                em.dma("sp", X.dbg["l1nh"][c * 128:(c + 1) * 128, :].bitcast(BF16)[:, half * 1024:(half + 1) * 1024],
                       NHR.ap(rc0=c * 1024, rcols=1024), r=NHR.keys(), w=[("D", "l1nh")])
        for t in range(8):
            emit_mixer_tail(X, half * 8 + t, LG1, l1=True,
                            logits_ap=X.PB[6][:, t * 8:(t + 1) * 8],
                            logits_keys=[X.PK[6]], tr_bank=7)
        for o in range(8):
            for nt in range(2):
                em.grp("pe", [lambda t, o=o, nt=nt, half=half: t.matmul(
                    X.PB[7][:, 0:512], C["eb2"].ap()[0:8, o * 128:(o + 1) * 128],
                    X.DWFM.ap(p0=0, pn=8, rc0=half * 1024 + nt * 512, rcols=512),
                    start=True, stop=True)],
                    r=[("C", "eb2")] + X.DWFM.keys(), w=[X.PK[7]])
                em.op("dve", lambda v, o=o, nt=nt: v.tensor_scalar(
                    G[o].ap(rc0=nt * 512, rcols=512), X.PB[7][:, 0:512],
                    C["sb2c"].ap()[:, o:o + 1], None, ALU.add),
                    r=[X.PK[7]] + [("C", "sb2c")], w=G[o].keys(nt * 512, 512))
        for e in range(9):
            if e < 8:
                for js in range(2):
                    em.op("pe", lambda t, e=e, half=half, js=js: t.matmul(
                        X.PB[6][:, :], C["sel"].ap()[:, e * 128:(e + 1) * 128],
                        X.DWFM.ap(p0=0, pn=8, rc0=half * 1024 + js * 512, rcols=512),
                        start=True, stop=True),
                        r=[("C", "sel")] + X.DWFM.keys(), w=[X.PK[6]])
                    em.op("dve", lambda v, js=js: v.tensor_copy(
                        DW.ap(rc0=js * 512, rcols=512), X.PB[6][:, :]),
                        r=[X.PK[6]], w=DW.keys(js * 512, 512))
            def load_band1(e_, bg_, band_):
                bb_ = band_ % 2
                hb_ = bg_ * 2048 + band_ * 512
                srcf = d["ew1"][e_] if e_ < 8 else d["sw1"]
                for hv in range(2):
                    em.dma("gp", w1v(W1[bb_])[:, hv * 4:(hv + 1) * 4, :],
                           srcf[hv * 512:(hv + 1) * 512, hb_:hb_ + 512]
                           .rearrange("(c p) n -> p c n", p=128),
                           r=[("D", "ew1" if e_ < 8 else "sw1")],
                           w=W1[bb_].keys(hv * 2048, 2048))

            def load_w2b(e_, o_, bg_):
                srcf = (d["ew2cm"][e_, o_] if e_ < 8 else d["sw2cm"][o_])
                for hv in range(2):
                    em.dma("gp", W2[o_ % 2].ap(rc0=hv * 1024, rcols=1024),
                           srcf[:, bg_ * 16 + hv * 8:bg_ * 16 + (hv + 1) * 8, :]
                           .rearrange("p c k -> p (c k)"),
                           r=[("D", "ew2cm" if e_ < 8 else "sw2cm")],
                           w=W2[o_ % 2].keys(hv * 1024, 1024))

            if e == 0 and half == 0:
                load_band1(0, 0, 0)
                load_band1(0, 0, 1)
            for bg in range(2):
                for band in range(4):
                    bb = band % 2
                    for hci in range(4):
                        hc = bg * 16 + band * 4 + hci
                        for nt in range(2):
                            pbA = (hc * 2 + nt) % 2
                            fns = []
                            for c in range(DC):
                                fns.append(lambda t, c=c, hci=hci, bb=bb, pbA=pbA, nt=nt:
                                           t.matmul(X.PB[pbA][:, :],
                                                    w1v(W1[bb])[:, c, hci * 128:(hci + 1) * 128],
                                                    nh(c, nt),
                                                    start=(c == 0), stop=(c == DC - 1)))
                            em.grp("pe", fns, r=W1[bb].keys() + sum(
                                [nhk(c, nt) for c in range(DC)], []), w=[X.PK[pbA]])
                            bias = (C["eb1c"].ap()[:, e, hc:hc + 1] if e < 8 else
                                    C["sb1c"].ap()[:, hc:hc + 1])
                            em.op("act", lambda sc, hc=hc, nt=nt, pbA=pbA, bias=bias:
                                  sc.activation(aw(hc, nt), X.PB[pbA][:, :], AF.Relu,
                                                bias=bias),
                                  r=[X.PK[pbA], ("C", "eb1c" if e < 8 else "sb1c")],
                                  w=awk(hc, nt))
                    if band + 2 < 4:
                        load_band1(e, bg, band + 2)
                load_w2b(e, 0, bg)
                load_w2b(e, 1, bg)
                for o in range(8):
                    ob = o % 2
                    for nt in range(2):
                        pbo = 4 + (o * 2 + nt) % 2
                        fns = []
                        for hh in range(16):
                            fns.append(lambda t, hh=hh, ob=ob, nt=nt, pbo=pbo: t.matmul(
                                X.PB[pbo][:, :], W2[ob].ap(rc0=hh * 128, rcols=128),
                                aw(hh, nt), start=(hh == 0), stop=(hh == 15)))
                        em.grp("pe", fns, r=W2[ob].keys() + ACTW.keys(), w=[X.PK[pbo]])
                        if e < 8:
                            par = (o * 2 + nt) % 2
                            em.op("dve", lambda v, nt=nt, pbo=pbo, par=par: v.tensor_tensor(
                                FLSP[par].ap(), X.PB[pbo][:, :],
                                DW.ap(rc0=nt * 512, rcols=512), ALU.mult),
                                r=[X.PK[pbo]] + DW.keys(nt * 512, 512), w=FLSP[par].keys())
                            em.op("dve", lambda v, o=o, nt=nt, par=par: v.tensor_tensor(
                                G[o].ap(rc0=nt * 512, rcols=512),
                                G[o].ap(rc0=nt * 512, rcols=512), FLSP[par].ap(), ALU.add),
                                r=G[o].keys(nt * 512, 512) + FLSP[par].keys(),
                                w=G[o].keys(nt * 512, 512))
                        else:
                            em.op("dve", lambda v, o=o, nt=nt, pbo=pbo: v.tensor_tensor(
                                G[o].ap(rc0=nt * 512, rcols=512),
                                G[o].ap(rc0=nt * 512, rcols=512), X.PB[pbo][:, :],
                                ALU.add),
                                r=G[o].keys(nt * 512, 512) + [X.PK[pbo]],
                                w=G[o].keys(nt * 512, 512))
                    if o + 2 < 8:
                        load_w2b(e, o + 2, bg)
                    if o == 4 or o == 6:
                        band_n = 0 if o == 4 else 1
                        if bg == 0:
                            load_band1(e, 1, band_n)
                        else:
                            nxt = (e + 1, half) if e + 1 < 9 else ((0, half + 1) if half == 0 else None)
                            if nxt is not None and nxt[1] < 2:
                                load_band1(nxt[0], 0, band_n)
        for nt in range(2):
            for c in range(DC):
                u = c % 2
                noff = half * 1024 + nt * 512
                em.dma("sp", X1C[u].ap(), X.x1d[c * 128:(c + 1) * 128, noff:noff + 512],
                       r=[("D", "x1")], w=X1C[u].keys())
                em.op("act", lambda sc, c=c, nt=nt: sc.activation(
                    GSIL.ap(), G[c].ap(rc0=nt * 512, rcols=512), AF.Silu),
                    r=G[c].keys(nt * 512, 512), w=GSIL.keys())
                em.op("dve", lambda v, c=c, noff=noff: v.tensor_tensor(
                    GSIL.ap(), GSIL.ap(), X.XN[c].ap(rc0=noff, rcols=512), ALU.mult),
                    r=GSIL.keys() + X.XN[c].keys(noff, 512), w=GSIL.keys())
                em.op("dve", lambda v, u=u: v.tensor_tensor(
                    OUTT[u].ap(), GSIL.ap(), X1C[u].ap(), ALU.add),
                    r=GSIL.keys() + X1C[u].keys(), w=OUTT[u].keys())
                em.dma("sp", X.out[c * 128:(c + 1) * 128, noff:noff + 512],
                       OUTT[u].ap(), r=OUTT[u].keys(), w=[("D", "out")])


def emit_all(X, debug):
    emit_consts(X)
    P0 = X.P0
    STR0 = Reg(X, "str0", P0, 2048)
    STR1 = Reg(X, "str1", P0 + 2048, 2048)
    SQ = Reg(X, "sq", P0 + 4096, 2048)
    MBM = Reg(X, "mbm", P0 + 6144, 2048)
    MBR = Reg(X, "mbr", P0 + 8192, 2048)
    emit_ln(X, 0, "x0", STR0, STR1, SQ, MBM, MBR)
    dbg_tap_xn(X, "xn0", STR0, STR1)
    emit_mixer_l0(X)
    dbg_tap_dwfm(X, "dw0")
    emit_l0_attn(X)
    emit_l0_wo(X)
    if "s0" in X.dbg:
        X.em.dma("gp", X.dbg["s0"], X.s0d[:, :], r=[("D", "s0")], w=[("D", "dbg_s0")])
    emit_l0_ffn(X)
    if "x1" in X.dbg:
        X.em.dma("gp", X.dbg["x1"], X.x1d[:, :], r=[("D", "x1")], w=[("D", "dbg_x1")])
    emit_ln(X, 1, "x1", STR0, STR1, SQ, MBM, MBR)
    dbg_tap_xn(X, "xn1", STR0, STR1)
    emit_l1_attn(X)
    if "s1" in X.dbg:
        for c in range(DC):
            S = [STR0, STR1][c % 2]
            X.em.op("dve", lambda v, c=c, S=S: v.tensor_copy(S.ap(), X.XN[c].ap()),
                    r=X.XN[c].keys(), w=S.keys())
            X.em.dma("sp", X.dbg["s1"][c * 128:(c + 1) * 128, :], S.ap(), r=S.keys(),
                     w=[("D", "s1")])
    emit_l1_ffn(X)
    if "comb1" in X.dbg:
        X.em.dma("sp", X.dbg["comb1"][0:8, :], X.DWFM.ap(p0=0, pn=8)[0:8, :],
                 r=X.DWFM.keys(), w=[("D", "comb1")])


def dbg_tap_xn(X, nm, STR0, STR1):
    if nm not in X.dbg:
        return
    em = X.em
    for c in range(DC):
        S = [STR0, STR1][c % 2]
        em.op("dve", lambda v, c=c, S=S: v.tensor_copy(S.ap(), X.XN[c].ap()),
              r=X.XN[c].keys(), w=S.keys())
        em.dma("sp", X.dbg[nm][c * 128:(c + 1) * 128, :], S.ap(), r=S.keys(),
               w=[("D", nm)])


def dbg_tap_dwfm(X, nm):
    if nm not in X.dbg:
        return
    em = X.em
    em.dma("sp", X.dbg[nm][0:8, :], X.DWFM.ap(p0=0, pn=8)[0:8, :], r=X.DWFM.keys(),
           w=[("D", nm)])


def kernel(**inputs):
    H = host_prep(inputs)
    x = np.asarray(inputs["L0_attn_X"], np.float32)
    nc = build_program(debug=tuple(DEBUG))
    in_maps = []
    for b in range(8):
        m = {k: H[k] for k in COMMON}
        m["xT"] = np.ascontiguousarray(x[b].T)
        in_maps.append(m)
    res = run_bass_kernel_spmd(nc, in_maps, core_ids=list(range(8)))
    outs = [r["out"].T for r in res.results]
    kernel.last_results = res.results
    return np.stack(outs, 0)


# version 07
# speedup vs baseline: 1.0862x; 1.0862x over previous
import numpy as np
import ml_dtypes
import concourse.bass as bass
import concourse.mybir as mybir
from concourse.bass_utils import run_bass_kernel_spmd

F32 = mybir.dt.float32
BF16 = mybir.dt.bfloat16
AF = mybir.ActivationFunctionType
ALU = mybir.AluOpType
AX = mybir.AxisListType

N = 2048
D = 1024
DC = 8
NH = 8
DH = 128
LAMBDA_INIT = 0.8 - 0.6 * float(np.exp(-0.3))
SC0 = 1.0 / float(np.sqrt(128.0))
SC1 = 1.0 / 32.0
NHALF = 1024
NT = 512

ACOLS = 46080
BLK = 512

DEBUG = []


class EM:
    QUEUES = ("sp", "gp")
    NLANES = 6

    def __init__(self, nc, es):
        self.nc = nc
        self.streams = ("pe", "act", "dve", "sp", "gp")
        self.entities = ["pe", "act", "dve", "sp", "gp"]
        for q in self.QUEUES:
            self.entities += [f"{q}@{l}" for l in range(self.NLANES)]
        self.sems = {e: es.enter_context(nc.semaphore(f"sem_{e.replace('@','_')}"))
                     for e in self.entities}
        self.cnt = {e: 0 for e in self.entities}
        self.rr = {q: 0 for q in self.QUEUES}
        self.waited = {e: {f: 0 for f in self.entities} for e in self.streams}
        self.last_w = {}
        self.readers = {}
        self.plan = {e: [] for e in self.streams}

    def _unit(self, ent):
        return 16 if "@" in ent else 1

    def _need(self, stream, ent, c):
        if self.waited[stream][ent] < c:
            self.plan[stream].append(("w", ent, c * self._unit(ent)))
            self.waited[stream][ent] = c

    def _pre(self, stream, ent, r, w):
        for k in r:
            lw = self.last_w.get(k)
            if lw is not None:
                self._need(stream, lw[0], lw[1])
        for k in w:
            lw = self.last_w.get(k)
            if lw is not None and lw[0] != ent:
                self._need(stream, lw[0], lw[1])
            for f, c in self.readers.get(k, {}).items():
                if f != ent:
                    self._need(stream, f, c)

    def _post(self, ent, r, w):
        self.cnt[ent] += 1
        c = self.cnt[ent]
        for k in w:
            self.last_w[k] = (ent, c)
            self.readers[k] = {}
        for k in r:
            self.readers.setdefault(k, {})[ent] = c
        return c

    def op(self, e, fn, r=(), w=()):
        self._pre(e, e, r, w)
        self.plan[e].append(("o", fn, e))
        self._post(e, r, w)

    def grp(self, e, fns, r=(), w=()):
        self._pre(e, e, r, w)
        self.plan[e].append(("g", fns, e))
        self._post(e, r, w)

    def dma(self, q, out, in_, r=(), w=()):
        lane = self.rr[q]
        self.rr[q] = (lane + 1) % self.NLANES
        ent = f"{q}@{lane}"
        self._need(q, ent, self.cnt[ent])
        self._pre(q, ent, r, w)
        self.plan[q].append(("o", lambda eng: eng.dma_start(out=out, in_=in_), ent))
        self._post(ent, r, w)

    def final_wait(self):
        for f in self.entities:
            if self.cnt[f]:
                self._need("sp", f, self.cnt[f])

    def emit(self, block):
        secs = dict(pe=block.tensor, act=block.scalar, dve=block.vector,
                    sp=block.sync, gp=block.gpsimd)
        sems = self.sems
        for e in self.streams:
            plan = self.plan[e]

            def body(eng, _plan=plan):
                for item in _plan:
                    if item[0] == "w":
                        eng.wait_ge(sems[item[1]], item[2])
                    elif item[0] == "o":
                        ins = item[1](eng)
                        ins.then_inc(sems[item[2]], self._unit(item[2]))
                    else:
                        ins = None
                        for fn in item[1]:
                            ins = fn(eng)
                        ins.then_inc(sems[item[2]], self._unit(item[2]))
            secs[e](body)


class Reg:

    def __init__(self, ctx, name, c0, cols, dtype=F32):
        self.ctx, self.name, self.c0, self.cols, self.dtype = ctx, name, c0, cols, dtype
        assert c0 % BLK == 0 or dtype is not None
        assert c0 + cols <= ACOLS, (name, c0, cols)

    def keys(self, rc0=0, rcols=None):
        div = 1 if self.dtype == F32 else 2
        if rcols is None:
            rcols = self.cols * div - rc0
        a = (self.c0 + rc0 // div) // BLK
        b = (self.c0 + (rc0 + rcols + div - 1) // div + BLK - 1) // BLK
        return [("A", i) for i in range(a, b)]

    def ap(self, p0=0, pn=128, rc0=0, rcols=None):
        if self.dtype == F32:
            if rcols is None:
                rcols = self.cols - rc0
            return self.ctx.a32[p0:p0 + pn, self.c0 + rc0:self.c0 + rc0 + rcols]
        else:
            if rcols is None:
                rcols = 2 * self.cols - rc0
            return self.ctx.a16[p0:p0 + pn, 2 * self.c0 + rc0:2 * self.c0 + rc0 + rcols]

    def v3(self, c, inner):
        return self.ap().rearrange("p (c n) -> p c n", c=c)


class Ctx:
    pass


BF16_DRAM = ["wq_t", "wk_t", "wv_t", "wo_t", "w1t", "w3t", "w2cm",
             "wq1", "wk1", "wv1", "sw1", "sw2cm", "ew1", "ew2cm"]


def host_prep(inp):
    f = np.float32
    c = lambda a: np.ascontiguousarray(a, dtype=f)
    H = {}
    H["wq_t"] = c(inp["L0_attn_Wq"].T)
    H["wk_t"] = c(inp["L0_attn_Wk"].T)
    H["wv_t"] = c(inp["L0_attn_Wv"].T)
    H["wo_t"] = c(inp["L0_attn_Wo"].T)
    H["lamA"] = c(np.concatenate([inp["L0_attn_lambda_q1"], inp["L0_attn_lambda_k1"]], 1))
    H["lamB"] = c(np.concatenate([inp["L0_attn_lambda_q2"], inp["L0_attn_lambda_k2"]], 1))
    H["rms_row"] = c(inp["L0_attn_rms_scale"][None, :])
    H["gate_t"] = c(inp["L0_ffn_gate_w"].T)
    H["w1t"] = c(np.transpose(inp["L0_ffn_w1"], (0, 2, 1)))
    H["w3t"] = c(np.transpose(inp["L0_ffn_w3"], (0, 2, 1)))
    w2 = np.asarray(inp["L0_ffn_w2"], f)
    H["w2cm"] = c(w2.reshape(8, 8, 128, 16, 128).transpose(0, 1, 4, 3, 2))
    H["wq1"] = c(inp["L1_attn_W_query"])
    H["wk1"] = c(inp["L1_attn_W_key"])
    H["wv1"] = c(inp["L1_attn_W_value"])
    H["nwc"] = c(inp["L1_ffn_norm_weight"].reshape(8, 128).T)
    H["rout_t"] = c(inp["L1_ffn_router_w"].T)
    H["sw1"] = c(inp["L1_ffn_sw1"])
    H["sb1c"] = c(inp["L1_ffn_sb1"].reshape(32, 128).T)
    sw2 = np.asarray(inp["L1_ffn_sw2"], f)
    H["sw2cm"] = c(sw2.reshape(32, 128, 8, 128).transpose(2, 1, 0, 3))
    H["sb2c"] = c(inp["L1_ffn_sb2"].reshape(8, 128).T)
    H["ew1"] = c(inp["L1_ffn_ew1"])
    H["eb1c"] = c(np.transpose(np.asarray(inp["L1_ffn_eb1"], f).reshape(8, 32, 128), (2, 0, 1)))
    ew2 = np.asarray(inp["L1_ffn_ew2"], f)
    H["ew2cm"] = c(ew2.reshape(8, 32, 128, 8, 128).transpose(0, 3, 2, 1, 4))
    H["eb2"] = c(inp["L1_ffn_eb2"])
    H["ident_b"] = np.eye(128, dtype=ml_dtypes.bfloat16)
    H["ident_f"] = np.eye(128, dtype=f)
    H["ones_c"] = np.ones((128, 1), f)
    H["ones_r"] = np.ones((1, 128), f)
    H["ones3"] = np.ones((128, 128), f)
    sel = np.zeros((8, 1024), f)
    for e in range(8):
        sel[e, e * 128:(e + 1) * 128] = 1.0
    H["sel"] = sel
    for nm in BF16_DRAM:
        H[nm] = H[nm].astype(ml_dtypes.bfloat16)
    return H


PER_CORE = "xT"
COMMON = ["wq_t", "wk_t", "wv_t", "wo_t", "lamA", "lamB", "rms_row", "gate_t",
          "w1t", "w3t", "w2cm", "wq1", "wk1", "wv1", "nwc", "rout_t", "sw1",
          "sb1c", "sw2cm", "sb2c", "ew1", "eb1c", "ew2cm", "eb2",
          "ident_b", "ident_f", "ones_c", "ones_r", "ones3", "sel"]


def build_program(debug=()):
    from contextlib import ExitStack
    es = ExitStack()
    nc = bass.Bass("TRN2", target_bir_lowering=False, debug=False)
    X = Ctx()
    X.nc = nc

    d = {}
    d["xT"] = nc.dram_tensor("xT", [1024, 2048], F32, kind="ExternalInput").ap()
    for nm, shp in [("wq_t", [1024, 2048]), ("wk_t", [1024, 2048]),
                    ("wv_t", [1024, 2048]), ("wo_t", [2048, 1024]),
                    ("lamA", [8, 256]), ("lamB", [8, 256]), ("rms_row", [1, 256]),
                    ("gate_t", [1024, 8]),
                    ("w1t", [8, 1024, 2048]), ("w3t", [8, 1024, 2048]),
                    ("w2cm", [8, 8, 128, 16, 128]),
                    ("wq1", [1024, 1024]), ("wk1", [1024, 1024]), ("wv1", [1024, 1024]),
                    ("nwc", [128, 8]), ("rout_t", [1024, 8]), ("sw1", [1024, 4096]),
                    ("sb1c", [128, 32]), ("sw2cm", [8, 128, 32, 128]), ("sb2c", [128, 8]),
                    ("ew1", [8, 1024, 4096]), ("eb1c", [128, 8, 32]),
                    ("ew2cm", [8, 8, 128, 32, 128]), ("eb2", [8, 1024]),
                    ("ident_b", None), ("ident_f", [128, 128]), ("ones_c", [128, 1]),
                    ("ones_r", [1, 128]), ("ones3", [128, 128]), ("sel", [8, 1024])]:
        if nm == "ident_b":
            d[nm] = nc.dram_tensor(nm, [128, 128], BF16, kind="ExternalInput").ap()
        else:
            d[nm] = nc.dram_tensor(nm, shp, BF16 if nm in BF16_DRAM else F32,
                                   kind="ExternalInput").ap()
    out = nc.dram_tensor("out", [1024, 2048], F32, kind="ExternalOutput").ap()
    x1d = nc.dram_tensor("x1d", [1024, 2048], F32, kind="Internal").ap()
    s0d = nc.dram_tensor("s0d", [1024, 2048], BF16, kind="Internal").ap()
    dbg = {}
    for nm in debug:
        dbg[nm] = nc.dram_tensor("dbg_" + nm, [1024, 2048], F32, kind="ExternalOutput").ap()
    X.d, X.out, X.x1d, X.s0d, X.dbg = d, out, x1d, s0d, dbg

    AR = nc.alloc_sbuf_tensor("arena", [128, ACOLS], F32)
    X.a32 = AR.ap()
    X.a16 = AR.ap().bitcast(BF16)
    C = {}
    C["idb"] = nc.alloc_sbuf_tensor("sb_idb", [128, 128], BF16)
    C["idf"] = nc.alloc_sbuf_tensor("sb_idf", [128, 128], F32)
    C["ones_c"] = nc.alloc_sbuf_tensor("sb_ones_c", [128, 1], F32)
    C["ones_r"] = nc.alloc_sbuf_tensor("sb_ones_r", [1, 128], F32)
    C["ones3"] = nc.alloc_sbuf_tensor("sb_ones3", [128, 128], F32)
    C["rms_bc"] = nc.alloc_sbuf_tensor("sb_rms_bc", [128, 256], F32)
    C["nlam"] = nc.alloc_sbuf_tensor("sb_nlam", [128, 8], F32)
    C["gate"] = nc.alloc_sbuf_tensor("sb_gate", [128, 8, 8], F32)
    C["rout"] = nc.alloc_sbuf_tensor("sb_rout", [128, 8, 8], F32)
    C["routnw"] = nc.alloc_sbuf_tensor("sb_routnw", [128, 8, 8], F32)
    C["eb1c"] = nc.alloc_sbuf_tensor("sb_eb1c", [128, 8, 32], F32)
    C["sb1c"] = nc.alloc_sbuf_tensor("sb_sb1c", [128, 32], F32)
    C["sb2c"] = nc.alloc_sbuf_tensor("sb_sb2c", [128, 8], F32)
    C["nwc"] = nc.alloc_sbuf_tensor("sb_nwc", [128, 8], F32)
    C["lamA"] = nc.alloc_sbuf_tensor("sb_lamA", [8, 256], F32)
    C["lamB"] = nc.alloc_sbuf_tensor("sb_lamB", [8, 256], F32)
    C["sel"] = nc.alloc_sbuf_tensor("sb_sel", [8, 1024], F32)
    C["eb2"] = nc.alloc_sbuf_tensor("sb_eb2", [8, 1024], F32)
    C["rmsrow"] = nc.alloc_sbuf_tensor("sb_rmsrow", [1, 256], F32)
    C["lamrow"] = nc.alloc_sbuf_tensor("sb_lamrow", [1, 16], F32)
    X.C = C

    PSA = nc.alloc_psum_tensor("psa", [128, 2048], F32)
    X.PSA = PSA
    PB = [PSA.ap()[:, i * 512:(i + 1) * 512] for i in range(4)] + \
         [nc.alloc_psum_tensor(f"pb{i}", [128, 512], F32).ap() for i in range(4, 8)]
    X.PB = PB
    X.PBb = [p.bitcast(BF16) for p in PB]
    PK = [("P", i) for i in range(8)]
    X.PK = PK

    em = EM(nc, es)
    X.em = em

    R = lambda name, c0, cols, dt=F32: Reg(X, name, c0, cols, dt)
    X.XN = [R(f"xn{c}", c * 1024, 1024, BF16) for c in range(DC)]
    X.ROWA = R("rowA", 8192, 2048)
    X.ROWB = R("rowB", 10240, 2048)
    X.ROWC = R("rowC", 12288, 2048)
    X.SMALL = R("small", 14336, 128)
    X.DWFM = R("dwfm", 14464, 2048)
    X.LAMW = R("lamw", 16512, 128)
    X.DWBC = R("dwbc", 16640, 2048)
    X.P0 = 18688

    emit_all(X, debug)

    em.final_wait()
    block = es.enter_context(nc.Block())
    em.emit(block)
    es.close()
    return nc


def k1_broadcast(X, src_row_ap, psb, dst_ap, dst_keys, src_keys, scale=1.0, pbase=0):
    em, C = X.em, X.C
    n = src_row_ap.shape[-1]
    em.op("pe", lambda t: t.matmul(X.PB[psb][:, 0:n], C["ones3"].ap()[pbase:pbase + 1, :],
                                   src_row_ap, start=True, stop=True),
          r=src_keys + [("C", "ones3")], w=[X.PK[psb]])
    em.op("act", lambda s: s.activation(dst_ap, X.PB[psb][:, 0:n], AF.Copy, scale=scale),
          r=[X.PK[psb]], w=dst_keys)


def stat_mm(X, psb, row_slice, rhs_ap, rhs_keys, start, stop, prow=0):
    em = X.em
    em.grp("pe", [lambda t: t.matmul(X.PB[psb][prow:prow + 1, row_slice],
                                     X.C["ones_c"].ap(), rhs_ap, start=start, stop=stop)],
           r=rhs_keys + [("C", "ones_c")], w=[X.PK[psb]])


def emit_ln_rows_slice(X, layer, j, sum_b, sq_b):
    em = X.em
    RA, RB, RC = X.ROWA, X.ROWB, X.ROWC
    row = lambda R, c0, n=512: R.ap(p0=0, pn=1, rc0=c0, rcols=n)
    ss = lambda i: row(RB, i * 512)
    em.op("act", lambda s: s.activation(row(RA, j * 512), X.PB[sum_b][0:1, :],
                                        AF.Copy, scale=1.0 / D),
          r=[X.PK[sum_b]], w=RA.keys())
    em.op("act", lambda s: s.activation(ss(0), row(RA, j * 512), AF.Square),
          r=RA.keys(), w=RB.keys())
    em.op("dve", lambda v: v.scalar_tensor_tensor(ss(1), X.PB[sq_b][0:1, :],
                                                  1.0 / D, ss(0), ALU.mult,
                                                  ALU.subtract),
          r=[X.PK[sq_b]] + RB.keys(), w=RB.keys())
    em.op("dve", lambda v: v.tensor_scalar(ss(2), ss(1), 1e-5, None, ALU.add),
          r=RB.keys(), w=RB.keys())
    em.op("act", lambda s: s.activation(ss(3), ss(2), AF.Ln), r=RB.keys(), w=RB.keys())
    em.op("act", lambda s: s.activation(row(RC, j * 512), ss(3), AF.Exp, scale=-0.5),
          r=RB.keys(), w=RC.keys())
    if layer == 1:
        em.op("dve", lambda v: v.reciprocal(ss(3), ss(2)), r=RB.keys(), w=RB.keys())
        em.op("dve", lambda v: v.tensor_tensor(ss(0), ss(1), ss(3), ALU.mult),
              r=RB.keys(), w=RB.keys())
        em.op("dve", lambda v: v.tensor_scalar(ss(0), ss(0), 1e-8, None, ALU.add),
              r=RB.keys(), w=RB.keys())
        em.op("act", lambda s: s.activation(ss(3), ss(0), AF.Ln), r=RB.keys(), w=RB.keys())
        em.op("act", lambda s: s.activation(
            X.DWFM.ap(p0=0, pn=1, rc0=j * 512, rcols=512), ss(3), AF.Exp, scale=-0.5),
            r=RB.keys(), w=X.DWFM.keys())


def emit_consts(X):
    em, C, d = X.em, X.C, X.d
    kc = lambda n: [("C", n)]
    em.dma("sp", C["idb"].ap(), d["ident_b"], w=kc("idb"))
    em.dma("sp", C["idf"].ap(), d["ident_f"], w=kc("idf"))
    em.dma("sp", C["ones_c"].ap(), d["ones_c"], w=kc("ones_c"))
    em.dma("sp", C["ones_r"].ap(), d["ones_r"], w=kc("ones_r"))
    em.dma("sp", C["ones3"].ap(), d["ones3"], w=kc("ones3"))
    em.dma("sp", C["rmsrow"].ap(), d["rms_row"], w=kc("rmsrow"))
    em.dma("sp", C["gate"].ap(), d["gate_t"].rearrange("(c p) e -> p c e", p=128),
           w=kc("gate"))
    em.dma("sp", C["rout"].ap(), d["rout_t"].rearrange("(c p) e -> p c e", p=128),
           w=kc("rout"))
    em.dma("sp", C["eb1c"].ap(), d["eb1c"], w=kc("eb1c"))
    em.dma("sp", C["sb1c"].ap(), d["sb1c"], w=kc("sb1c"))
    em.dma("sp", C["sb2c"].ap(), d["sb2c"], w=kc("sb2c"))
    em.dma("sp", C["nwc"].ap(), d["nwc"], w=kc("nwc"))
    em.dma("sp", C["lamA"].ap(), d["lamA"], w=kc("lamA"))
    em.dma("sp", C["lamB"].ap(), d["lamB"], w=kc("lamB"))
    em.dma("sp", C["sel"].ap(), d["sel"], w=kc("sel"))
    em.dma("sp", C["eb2"].ap(), d["eb2"], w=kc("eb2"))
    for c in range(DC):
        em.op("dve", lambda v, c=c: v.tensor_scalar(
            C["routnw"].ap()[:, c, :], C["rout"].ap()[:, c, :],
            C["nwc"].ap()[:, c:c + 1], None, ALU.mult),
            r=kc("rout") + kc("nwc"), w=kc("routnw"))
    em.op("pe", lambda t: t.matmul(X.PB[6][:, 0:256], C["ones_r"].ap(),
                                   C["rmsrow"].ap(), start=True, stop=True),
          r=kc("rmsrow") + kc("ones_r"), w=[X.PK[6]])
    em.op("act", lambda s: s.activation(C["rms_bc"].ap(), X.PB[6][:, 0:256],
                                        AF.Copy, scale=(1.0 - LAMBDA_INIT)),
          r=[X.PK[6]], w=kc("rms_bc"))
    LW = X.LAMW
    for i, nm in enumerate(("lamA", "lamB")):
        t = C[nm]
        em.op("dve", lambda v, t=t, i=i: v.tensor_tensor(
            LW.ap(p0=0, pn=8, rc0=0, rcols=128)[0:8, :],
            t.ap()[:, 0:128], t.ap()[:, 128:256], ALU.mult),
            r=kc(nm), w=LW.keys())
        em.op("dve", lambda v, i=i: v.reduce_sum(
            X.SMALL.ap(p0=0, pn=8, rc0=8 + i, rcols=1)[0:8, :],
            LW.ap(p0=0, pn=8)[0:8, 0:128], axis=AX.X),
            r=LW.keys(), w=X.SMALL.keys())
        em.op("pe", lambda t_, i=i: t_.transpose(
            X.PB[6][0:1, i * 8:(i + 1) * 8],
            X.SMALL.ap(p0=0, pn=8, rc0=8 + i, rcols=1)[0:8, :],
            C["idf"].ap()[0:8, 0:8]),
            r=X.SMALL.keys() + kc("idf"), w=[X.PK[6]])
        em.op("act", lambda s, i=i: s.activation(
            C["lamrow"].ap()[0:1, i * 8:(i + 1) * 8],
            X.PB[6][0:1, i * 8:(i + 1) * 8], AF.Exp),
            r=[X.PK[6]], w=kc("lamrow"))
    em.op("dve", lambda v: v.tensor_tensor(
        C["lamrow"].ap()[0:1, 8:16], C["lamrow"].ap()[0:1, 8:16],
        C["lamrow"].ap()[0:1, 0:8], ALU.subtract),
        r=kc("lamrow"), w=kc("lamrow"))
    em.op("pe", lambda t: t.matmul(X.PB[6][:, 0:8], C["ones_r"].ap(),
                                   C["lamrow"].ap()[0:1, 8:16], start=True, stop=True),
          r=kc("lamrow") + kc("ones_r"), w=[X.PK[6]])
    em.op("dve", lambda v: v.tensor_scalar(C["nlam"].ap(), X.PB[6][:, 0:8],
                                           -LAMBDA_INIT, None, ALU.add),
          r=[X.PK[6]], w=kc("nlam"))


def emit_ln(X, layer, src, STR0, STR1, SQ, MBM, MBR):
    em, C = X.em, X.C
    dsrc = X.d["xT"] if src == "x0" else X.x1d
    dkey = [("D", src)]
    strs = [STR0, STR1]
    if src == "x0":
        for c in range(DC):
            S = strs[c % 2]
            em.dma("sp", S.ap(), dsrc[c * 128:(c + 1) * 128, :], r=dkey, w=S.keys())
            em.op("act", lambda s, S=S: s.activation(SQ.ap(), S.ap(), AF.Square),
                  r=S.keys(), w=SQ.keys())
            for j in range(4):
                stat_mm(X, j, slice(0, 512), S.ap(rc0=j * 512, rcols=512), S.keys(j * 512, 512),
                        start=(c == 0), stop=(c == DC - 1))
                stat_mm(X, 4 + j, slice(0, 512), SQ.ap(rc0=j * 512, rcols=512), SQ.keys(j * 512, 512),
                        start=(c == 0), stop=(c == DC - 1))
        emit_ln_rows(X, layer)
    RA, RC = X.ROWA, X.ROWC
    for j in range(4):
        k1_broadcast(X, RA.ap(p0=0, pn=1, rc0=j * 512, rcols=512), 6,
                     MBM.ap(rc0=j * 512, rcols=512), MBM.keys(j * 512, 512), RA.keys(), 1.0)
        k1_broadcast(X, RC.ap(p0=0, pn=1, rc0=j * 512, rcols=512), 7,
                     MBR.ap(rc0=j * 512, rcols=512), MBR.keys(j * 512, 512), RC.keys(), 1.0)
    if layer == 1:
        emit_alpha(X)
    for c in range(DC):
        S = strs[c % 2]
        em.dma("sp", S.ap(), dsrc[c * 128:(c + 1) * 128, :], r=dkey, w=S.keys())
        em.op("dve", lambda v, S=S: v.tensor_tensor(SQ.ap(), S.ap(), MBM.ap(), ALU.subtract),
              r=S.keys() + MBM.keys(), w=SQ.keys())
        em.op("dve", lambda v, S=S: v.tensor_tensor(S.ap(), SQ.ap(), MBR.ap(), ALU.mult),
              r=SQ.keys() + MBR.keys(), w=S.keys())
        em.op("dve", lambda v, c=c, S=S: v.tensor_copy(X.XN[c].ap(), S.ap()),
              r=S.keys(), w=X.XN[c].keys())
        if layer == 0:
            for t in range(16):
                em.grp("pe", [lambda tt_, c=c, t=t, S=S: tt_.matmul(
                    X.PB[0][:, t * 8:(t + 1) * 8],
                    S.ap(rc0=t * 128, rcols=128), X.C["gate"].ap()[:, c, :],
                    start=(c == 0 and t == 0), stop=(c == DC - 1 and t == 15))],
                    r=S.keys() + [("C", "gate")], w=[X.PK[0]])


def emit_ln_rows(X, layer):
    for j in range(4):
        emit_ln_rows_slice(X, layer, j, j, 4 + j)


def emit_alpha(X):
    em = X.em
    em.op("dve", lambda v: v.tensor_tensor(
        X.ROWC.ap(p0=0, pn=1), X.ROWC.ap(p0=0, pn=1),
        X.DWFM.ap(p0=0, pn=1, rc0=0, rcols=2048), ALU.mult),
        r=X.ROWC.keys() + X.DWFM.keys(), w=X.ROWC.keys())


def emit_mixer_l0(X):
    em, C = X.em, X.C
    SM = X.SMALL
    P0 = X.P0
    LG = Reg(X, "lg", P0, 512)
    for tile in range(16):
        emit_mixer_tail(X, tile, LG, l1=False, logits_ap=X.PB[0][:, tile * 8:(tile + 1) * 8],
                        logits_keys=[X.PK[0]])


def emit_mixer_tail(X, tile, LG, l1, logits_ap, logits_keys, tr_bank=6):
    em, C = X.em, X.C
    SM = X.SMALL
    sm = lambda i: SM.ap(rc0=16 + i, rcols=1)
    smk = SM.keys()
    g = lambda i: LG.ap(rc0=i, rcols=8)
    gk = LG.keys()
    if l1:
        em.op("act", lambda s: s.activation(g(0), logits_ap, AF.Sigmoid), r=logits_keys, w=gk)
    else:
        em.op("dve", lambda v: v.tensor_copy(g(0), logits_ap), r=logits_keys, w=gk)
    em.op("dve", lambda v: v.reduce_max(sm(0), g(0), axis=AX.X), r=gk, w=smk)
    em.op("dve", lambda v: v.tensor_scalar(g(56), g(0), sm(0), None, ALU.is_ge), r=gk + smk, w=gk)
    em.op("dve", lambda v: v.scalar_tensor_tensor(g(64), g(56), -1e30 if not l1 else -1e3,
                                                  g(0), ALU.mult, ALU.add), r=gk, w=gk)
    em.op("dve", lambda v: v.reduce_max(sm(4), g(64), axis=AX.X), r=gk, w=smk)
    if not l1:
        em.op("dve", lambda v: v.reduce_max(sm(1), g(0), axis=AX.X, negate=True), r=gk, w=smk)
        em.op("dve", lambda v: v.reduce_max(sm(5), g(64), axis=AX.X, negate=True), r=gk, w=smk)
        em.op("act", lambda sc: sc.activation(g(8), g(0), AF.Abs), r=gk, w=gk)
        for br, (vv, nvv, zz, rr, srccol) in enumerate([(0, 1, 2, 3, 0), (4, 5, 6, 7, 64)]):
            em.op("dve", lambda v, vv=vv: v.tensor_scalar(g(16), g(8), sm(vv), None, ALU.max),
                  r=gk + smk, w=gk)
            em.op("dve", lambda v: v.reciprocal(g(24), g(16)), r=gk, w=gk)
            em.op("dve", lambda v, vv=vv: v.tensor_scalar(g(16), g(0), sm(vv), None, ALU.subtract),
                  r=gk + smk, w=gk)
            em.op("dve", lambda v: v.tensor_tensor(g(16), g(16), g(24), ALU.mult), r=gk, w=gk)
            em.op("dve", lambda v: v.tensor_scalar(g(32), g(16), -0.02, None, ALU.is_ge),
                  r=gk, w=gk)
            em.op("act", lambda s, nvv=nvv, srccol=srccol: s.activation(
                g(40), g(srccol), AF.Exp, bias=sm(nvv)), r=gk + smk, w=gk)
            em.op("dve", lambda v: v.tensor_tensor(g(48), g(32), g(40), ALU.mult), r=gk, w=gk)
            em.op("dve", lambda v, zz=zz: v.reduce_sum(sm(zz), g(48), axis=AX.X), r=gk, w=smk)
            em.op("dve", lambda v, zz=zz, rr=rr: v.reciprocal(sm(rr), sm(zz)), r=smk, w=smk)
    else:
        em.op("dve", lambda v: v.tensor_tensor(sm(2), sm(0), sm(4), ALU.add), r=smk, w=smk)
        em.op("dve", lambda v: v.tensor_scalar(sm(2), sm(2), 1e-6, None, ALU.add), r=smk, w=smk)
        em.op("dve", lambda v: v.reciprocal(sm(6), sm(2)), r=smk, w=smk)
        em.op("dve", lambda v: v.tensor_tensor(sm(3), sm(0), sm(6), ALU.mult), r=smk, w=smk)
        em.op("dve", lambda v: v.tensor_tensor(sm(7), sm(4), sm(6), ALU.mult), r=smk, w=smk)
    em.op("dve", lambda v: v.tensor_scalar(g(72), g(64), sm(4), None, ALU.is_ge),
          r=gk + smk, w=gk)
    em.op("dve", lambda v: v.tensor_scalar(g(80), g(56), sm(3), None, ALU.mult),
          r=gk + smk, w=gk)
    em.op("dve", lambda v: v.scalar_tensor_tensor(g(80), g(72), sm(7), g(80), ALU.mult, ALU.add),
          r=gk + smk, w=gk)
    em.op("pe", lambda t: t.transpose(X.PB[tr_bank][0:8, 0:128], g(80), C["idf"].ap()),
          r=gk + [("C", "idf")], w=[X.PK[tr_bank]])
    em.op("dve", lambda v: v.tensor_copy(
        X.DWFM.ap(p0=0, pn=8, rc0=tile * 128, rcols=128)[0:8, :], X.PB[tr_bank][0:8, 0:128]),
        r=[X.PK[tr_bank]], w=X.DWFM.keys(tile * 128, 128))


def emit_l0_attn(X):
    em, C = X.em, X.C
    P0 = X.P0
    R = lambda n, c0, cols, dt=F32: Reg(X, n, c0, cols, dt)
    WH = [R("wh0", P0, 1024, BF16), R("wh1", P0 + 1024, 1024, BF16),
          R("wh2", P0 + 25600, 1024, BF16)]
    VH = R("vh", P0 + 2048, 2048, BF16)
    QH = R("qh", P0 + 4096, 2048, BF16)
    KH = R("kh", 16640, 2048, BF16)
    RB = [R("rb0", P0 + 6144, 1024, BF16), R("rb1", P0 + 7168, 1024, BF16)]
    ATT = R("att", P0 + 8192, 1024, BF16)
    E1 = R("e1", 8192, 1024, BF16)
    E2 = R("e2", 9216, 1024, BF16)
    X.OALL = R("oall", P0 + 9216, 16384, BF16)
    WH3 = lambda b: WH[b].ap().rearrange("p (c k) -> p c k", c=8)
    VH3 = lambda: VH.ap().rearrange("p (c k) -> p c k", c=16)
    sm = lambda i: X.SMALL.ap(rc0=i, rcols=1)
    smk = X.SMALL.keys()
    skc = lambda i, n=1: [("S", j) for j in range(i, i + n)]
    d = X.d
    PK = X.PK

    def stage_a_scores(h, qt, br):
        qoff = br * 2048 + qt * 128
        base = 40 + br * 12
        for i in range(4):
            em.op("pe", lambda t, i=i, qoff=qoff, br=br: t.matmul(
                X.PB[i][:, :], QH.ap(rc0=qoff, rcols=128),
                KH.ap(rc0=br * 2048 + i * 512, rcols=512),
                start=True, stop=True),
                r=QH.keys(qoff, 128) + KH.keys(br * 2048 + i * 512, 512),
                w=[PK[i]])
            if i % 2 == 1:
                em.op("dve", lambda v, i=i, base=base: v.reduce_max(
                    sm(base + i // 2), X.PSA.ap()[:, (i - 1) * 512:(i + 1) * 512],
                    axis=AX.X, negate=True),
                    r=[PK[i - 1], PK[i]], w=skc(base + i // 2))

    def stage_a_softmax(h, qt, br):
        base = 40 + br * 12
        EB = E1 if br == 0 else E2
        for j in range(2):
            em.op("act", lambda sc, j=j, base=base, EB=EB: sc.activation(
                EB.ap(rc0=j * 1024, rcols=1024),
                X.PSA.ap()[:, j * 1024:(j + 1) * 1024], AF.Exp,
                bias=sm(base + j), accum_out=sm(base + 2 + j)),
                r=[PK[2 * j], PK[2 * j + 1]] + skc(base + j), w=EB.keys() + skc(base + 2 + j))
        em.op("dve", lambda v, base=base: v.tensor_reduce(
            sm(base + 4), X.SMALL.ap(rc0=base, rcols=2), AX.X, ALU.min),
            r=skc(base, 2), w=skc(base + 4))
        em.op("dve", lambda v, base=base: v.tensor_scalar(
            X.SMALL.ap(rc0=base + 5, rcols=2), X.SMALL.ap(rc0=base, rcols=2),
            sm(base + 4), -1.0, ALU.subtract, ALU.mult),
            r=skc(base, 2) + skc(base + 4), w=skc(base + 5, 2))
        em.op("act", lambda sc, base=base: sc.activation(
            X.SMALL.ap(rc0=base + 7, rcols=2), X.SMALL.ap(rc0=base + 5, rcols=2),
            AF.Exp), r=skc(base + 5, 2), w=skc(base + 7, 2))
        em.op("dve", lambda v, base=base: v.tensor_tensor(
            X.SMALL.ap(rc0=base + 9, rcols=2), X.SMALL.ap(rc0=base + 2, rcols=2),
            X.SMALL.ap(rc0=base + 7, rcols=2), ALU.mult),
            r=skc(base + 2, 2) + skc(base + 7, 2), w=skc(base + 9, 2))
        em.op("dve", lambda v, base=base: v.reduce_sum(
            sm(base + 11), X.SMALL.ap(rc0=base + 9, rcols=2), AX.X),
            r=skc(base + 9, 2), w=skc(base + 11))
        em.op("dve", lambda v, base=base: v.reciprocal(sm(base + 11), sm(base + 11)),
              r=skc(base + 11), w=skc(base + 11))

    def stage_a_combine(h, qt):
        RBt = RB[qt % 2]
        em.op("dve", lambda v, h=h: v.tensor_tensor(sm(76), sm(63),
                                                    C["nlam"].ap()[:, h:h + 1], ALU.mult),
              r=skc(63) + [("C", "nlam")], w=skc(76))
        em.op("dve", lambda v: v.tensor_scalar(
            X.SMALL.ap(rc0=78, rcols=2), X.SMALL.ap(rc0=47, rcols=2),
            sm(51), None, ALU.mult), r=skc(47, 2) + skc(51), w=skc(78, 2))
        em.op("dve", lambda v: v.tensor_scalar(
            X.SMALL.ap(rc0=80, rcols=2), X.SMALL.ap(rc0=59, rcols=2),
            sm(76), None, ALU.mult), r=skc(59, 2) + skc(76), w=skc(80, 2))
        for j in range(2):
            em.op("dve", lambda v, j=j: v.tensor_scalar(
                E1.ap(rc0=j * 1024, rcols=1024), E1.ap(rc0=j * 1024, rcols=1024),
                sm(78 + j), None, ALU.mult),
                r=E1.keys() + skc(78 + j), w=E1.keys())
            em.op("dve", lambda v, j=j: v.tensor_scalar(
                E2.ap(rc0=j * 1024, rcols=1024), E2.ap(rc0=j * 1024, rcols=1024),
                sm(80 + j), None, ALU.mult),
                r=E2.keys() + skc(80 + j), w=E2.keys())
        em.op("dve", lambda v, RBt=RBt: v.tensor_tensor(
            RBt.ap(), E1.ap(), E2.ap(), ALU.add),
            r=E2.keys() + E1.keys(), w=RBt.keys())

    def stage_b(h, qt):
        RBt = RB[qt % 2]
        for half16 in range(2):
            pb = 6 + half16
            for j in range(8):
                kc = half16 * 8 + j
                em.op("pe", lambda t, kc=kc, j=j, pb=pb, RBt=RBt: t.transpose(
                    X.PBb[pb][:, j * 128:(j + 1) * 128],
                    RBt.ap(rc0=kc * 128, rcols=128), C["idb"].ap()),
                    r=RBt.keys(kc * 128, 128) + [("C", "idb")], w=[PK[pb]])
            em.op("act", lambda sc, half16=half16, pb=pb: sc.activation(
                ATT.ap(rc0=half16 * 1024, rcols=1024), X.PBb[pb][:, 0:1024], AF.Copy),
                r=[PK[pb]], w=ATT.keys(half16 * 1024, 1024))
        pbO = 4 + qt % 2
        fns = []
        for kc in range(16):
            fns.append(lambda t, kc=kc, pbO=pbO: t.matmul(
                X.PB[pbO][:, 0:256], ATT.ap(rc0=kc * 128, rcols=128),
                VH3()[:, kc, :], start=(kc == 0), stop=(kc == 15)))
        em.grp("pe", fns, r=ATT.keys() + VH.keys(), w=[PK[pbO]])
        em.op("act", lambda sc, pbO=pbO, RBt=RBt: sc.activation(
            RBt.ap(rc0=256, rcols=256), X.PB[pbO][:, 0:256], AF.Square,
            accum_out=sm(84)),
            r=[PK[pbO]], w=RBt.keys(0, 512) + skc(84))
        em.op("dve", lambda v: v.tensor_scalar(sm(85), sm(84), 1.0 / 256.0, 1e-5,
                                               ALU.mult, ALU.add), r=skc(84), w=skc(85))
        em.op("act", lambda sc: sc.activation(sm(85), sm(85), AF.Ln), r=skc(85), w=skc(85))
        em.op("act", lambda sc: sc.activation(sm(86), sm(85), AF.Exp, scale=-0.5),
              r=skc(85), w=skc(86))
        em.op("dve", lambda v, pbO=pbO, RBt=RBt: v.scalar_tensor_tensor(
            RBt.ap(rc0=0, rcols=256), X.PB[pbO][:, 0:256], sm(86),
            C["rms_bc"].ap(), ALU.mult, ALU.mult),
            r=[PK[pbO]] + skc(86) + [("C", "rms_bc")], w=RBt.keys(0, 512))
        for j in range(2):
            pb = 6 + j
            em.op("pe", lambda t, j=j, pb=pb, RBt=RBt: t.transpose(
                X.PBb[pb][:, 0:128], RBt.ap(rc0=j * 128, rcols=128), C["idb"].ap()),
                r=RBt.keys(0, 512) + [("C", "idb")], w=[PK[pb]])
            ec = 2 * h + j
            em.op("dve", lambda v, ec=ec, qt=qt, pb=pb: v.tensor_copy(
                X.OALL.ap(rc0=ec * 2048 + qt * 128, rcols=128), X.PBb[pb][:, 0:128]),
                r=[PK[pb]], w=X.OALL.keys(ec * 2048 + qt * 128, 128))

    for h in range(NH):
        wsrc = [("wq_t", 0), ("wk_t", 1), ("wv_t", 2)]
        for nm, b in wsrc:
            em.dma("gp", WH3(b)[:, :, :],
                   d[nm][:, h * 256:(h + 1) * 256].rearrange("(c p) n -> p c n", p=128),
                   r=[("D", nm)], w=WH[b].keys())
            if nm == "wq_t":
                for j in range(2):
                    for nk in range(4):
                        pb = 4 + nk % 2
                        fns = []
                        for c in range(DC):
                            fns.append(lambda t, c=c, j=j, nk=nk, pb=pb, b=b: t.matmul(
                                X.PB[pb][:, :], WH3(b)[:, c, j * 128:(j + 1) * 128],
                                X.XN[c].ap(rc0=nk * 512, rcols=512),
                                start=(c == 0), stop=(c == DC - 1)))
                        em.grp("pe", fns, r=WH[b].keys() + sum(
                            [X.XN[c].keys(nk * 512, 512) for c in range(DC)], []),
                            w=[PK[pb]])
                        em.op("act", lambda sc, j=j, nk=nk, pb=pb: sc.activation(
                            QH.ap(rc0=j * 2048 + nk * 512, rcols=512), X.PB[pb][:, :],
                            AF.Copy, scale=SC0),
                            r=[PK[pb]], w=QH.keys(j * 2048 + nk * 512, 512))
            elif nm == "wk_t":
                for j in range(2):
                    for nk in range(4):
                        pb = 4 + nk % 2
                        fns = []
                        for c in range(DC):
                            fns.append(lambda t, c=c, j=j, nk=nk, pb=pb, b=b: t.matmul(
                                X.PB[pb][:, :], WH3(b)[:, c, j * 128:(j + 1) * 128],
                                X.XN[c].ap(rc0=nk * 512, rcols=512),
                                start=(c == 0), stop=(c == DC - 1)))
                        em.grp("pe", fns, r=WH[b].keys() + sum(
                            [X.XN[c].keys(nk * 512, 512) for c in range(DC)], []),
                            w=[PK[pb]])
                        em.op("act", lambda sc, j=j, nk=nk, pb=pb: sc.activation(
                            KH.ap(rc0=j * 2048 + nk * 512, rcols=512), X.PB[pb][:, :],
                            AF.Copy),
                            r=[PK[pb]], w=KH.keys(j * 2048 + nk * 512, 512))
            else:
                for kt in range(16):
                    pb = 4 + kt % 2
                    fns = []
                    for c in range(DC):
                        fns.append(lambda t, c=c, kt=kt, pb=pb, b=b: t.matmul(
                            X.PB[pb][:, 0:256], X.XN[c].ap(rc0=kt * 128, rcols=128),
                            WH3(b)[:, c, :], start=(c == 0), stop=(c == DC - 1)))
                    em.grp("pe", fns, r=WH[b].keys() + sum(
                        [X.XN[c].keys(kt * 128, 128) for c in range(DC)], []),
                        w=[PK[pb]])
                    em.op("act", lambda sc, kt=kt, pb=pb: sc.activation(
                        VH3()[:, kt, :], X.PB[pb][:, 0:256], AF.Copy),
                        r=[PK[pb]], w=VH.keys(kt * 256, 256))

        for qt in range(16):
            stage_a_scores(h, qt, 0)
            if qt > 0:
                stage_b(h, qt - 1)
            stage_a_softmax(h, qt, 0)
            stage_a_scores(h, qt, 1)
            stage_a_softmax(h, qt, 1)
            stage_a_combine(h, qt)
        stage_b(h, 15)


def emit_l0_wo(X):
    em, C, d = X.em, X.C, X.d
    P0 = X.P0
    WOF = Reg(X, "wof", P0, 8192, BF16)
    SEV = Reg(X, "sev", P0 + 8192, 1024, BF16)
    WOF3 = lambda: WOF.ap().rearrange("p (c k) -> p c k", c=16)
    em.dma("gp", WOF3(), d["wo_t"].rearrange("(c p) n -> p c n", p=128),
           r=[("D", "wo_t")], w=WOF.keys())
    for dc in range(DC):
        for nk in range(4):
            pb = 4 + nk % 2
            fns = []
            for ec in range(16):
                fns.append(lambda t, ec=ec, dc=dc, nk=nk, pb=pb: t.matmul(
                    X.PB[pb][:, :], WOF3()[:, ec, dc * 128:(dc + 1) * 128],
                    X.OALL.ap(rc0=ec * 2048 + nk * 512, rcols=512),
                    start=(ec == 0), stop=(ec == 15)))
            em.grp("pe", fns, r=WOF.keys() + X.OALL.keys(), w=[X.PK[pb]])
            slot = nk % 2
            em.op("act", lambda sc, pb=pb, slot=slot: sc.activation(
                SEV.ap(rc0=slot * 512, rcols=512), X.PB[pb][:, :], AF.Copy),
                r=[X.PK[pb]], w=SEV.keys(slot * 512, 512))
            em.dma("sp", X.s0d[dc * 128:(dc + 1) * 128, nk * 512:(nk + 1) * 512],
                   SEV.ap(rc0=slot * 512, rcols=512), r=SEV.keys(slot * 512, 512),
                   w=[("D", "s0")])


def emit_l0_ffn(X):
    em, C, d = X.em, X.C, X.d
    P0 = X.P0
    R = lambda n, c0, cols, dt=F32: Reg(X, n, c0, cols, dt)
    W1 = [R("w1a", P0, 2048, BF16), R("w1b", P0 + 2048, 2048, BF16)]
    W3 = [R("w3a", P0 + 4096, 2048, BF16), R("w3b", P0 + 6144, 2048, BF16)]
    W2 = [R("w2a", P0 + 8192, 1024, BF16), R("w2b", P0 + 9216, 1024, BF16)]
    ACTW = R("actw", P0 + 10240, 8192, BF16)
    GSIL = R("gsil", P0 + 18432, 512)
    SIL = [GSIL, R("silb", 10240, 512)]
    FLSP = [R("flsa", 10752, 512), R("flsb", 11264, 512)]
    G = [R(f"g{o}", 37888 + o * 1024, 1024) for o in range(8)]
    X.G = G
    w3v = lambda W: W.ap().rearrange("p (c k) -> p c k", c=8)
    aw = lambda ic, nt: ACTW.ap(rc0=(ic * 2 + nt) * 512, rcols=512)
    awk = lambda ic, nt: ACTW.keys((ic * 2 + nt) * 512, 512)
    DW = X.DWBC

    XST = [R("xst0", P0 + 10240, 512), R("xst1", P0 + 10752, 512)]
    SST = [R("sst0", P0 + 11264, 256, BF16), R("sst1", P0 + 11520, 256, BF16)]
    X1T = [R("x1t0", P0 + 11776, 512), R("x1t1", P0 + 12288, 512)]
    SQT = [R("sqt0", P0 + 12800, 512), R("sqt1", P0 + 13312, 512)]
    FLS = GSIL

    for half in range(2):
        for o in range(8):
            em.op("dve", lambda v, o=o: v.memset(G[o].ap(), 0.0), w=G[o].keys())
        for e in range(8):
            for jslice in range(2):
                em.op("pe", lambda t, e=e, half=half, jslice=jslice: t.matmul(
                    X.PB[6][:, :],
                    C["sel"].ap()[:, e * 128:(e + 1) * 128],
                    X.DWFM.ap(p0=0, pn=8, rc0=half * 1024 + jslice * 512, rcols=512),
                    start=True, stop=True),
                    r=[("C", "sel")] + X.DWFM.keys(), w=[X.PK[6]])
                em.op("dve", lambda v, jslice=jslice: v.tensor_copy(
                    DW.ap(rc0=jslice * 512, rcols=512), X.PB[6][:, :]),
                    r=[X.PK[6]], w=DW.keys(jslice * 512, 512))
            def load_band(e_, band_):
                bb_ = band_ % 2
                for hv in range(2):
                    q_ = "gp" if hv == 0 else "sp"
                    em.dma(q_, w3v(W1[bb_])[:, hv * 4:(hv + 1) * 4, :],
                           d["w1t"][e_][hv * 512:(hv + 1) * 512,
                                        band_ * 512:(band_ + 1) * 512]
                           .rearrange("(c p) n -> p c n", p=128),
                           r=[("D", "w1t")], w=W1[bb_].keys(hv * 2048, 2048))
                    em.dma(q_, w3v(W3[bb_])[:, hv * 4:(hv + 1) * 4, :],
                           d["w3t"][e_][hv * 512:(hv + 1) * 512,
                                        band_ * 512:(band_ + 1) * 512]
                           .rearrange("(c p) n -> p c n", p=128),
                           r=[("D", "w3t")], w=W3[bb_].keys(hv * 2048, 2048))
            X.l0_load_band = load_band
            if e == 0 and half == 0:
                load_band(0, 0)
                load_band(0, 1)
            for band in range(4):
                bb = band % 2
                for ici in range(4):
                    ic = band * 4 + ici
                    for nt in range(2):
                        pbA = (ic * 2 + nt) % 2
                        pbB = 2 + pbA
                        noff = half * 1024 + nt * 512
                        fns = []
                        for c in range(DC):
                            fns.append(lambda t, c=c, ici=ici, bb=bb, pbA=pbA, noff=noff:
                                       t.matmul(X.PB[pbA][:, :],
                                                w3v(W1[bb])[:, c, ici * 128:(ici + 1) * 128],
                                                X.XN[c].ap(rc0=noff, rcols=512),
                                                start=(c == 0), stop=(c == DC - 1)))
                        em.grp("pe", fns, r=W1[bb].keys() +
                               sum([X.XN[c].keys(noff, 512) for c in range(DC)], []),
                               w=[X.PK[pbA]])
                        fns = []
                        for c in range(DC):
                            fns.append(lambda t, c=c, ici=ici, bb=bb, pbB=pbB, noff=noff:
                                       t.matmul(X.PB[pbB][:, :],
                                                w3v(W3[bb])[:, c, ici * 128:(ici + 1) * 128],
                                                X.XN[c].ap(rc0=noff, rcols=512),
                                                start=(c == 0), stop=(c == DC - 1)))
                        em.grp("pe", fns, r=W3[bb].keys() +
                               sum([X.XN[c].keys(noff, 512) for c in range(DC)], []),
                               w=[X.PK[pbB]])
                        par = (ic * 2 + nt) % 2
                        em.op("act", lambda sc, pbA=pbA, par=par: sc.activation(
                            SIL[par].ap(), X.PB[pbA][:, :], AF.Silu),
                            r=[X.PK[pbA]], w=SIL[par].keys())
                        em.op("dve", lambda v, ic=ic, nt=nt, pbB=pbB, par=par: v.tensor_tensor(
                            aw(ic, nt), SIL[par].ap(), X.PB[pbB][:, :], ALU.mult),
                            r=SIL[par].keys() + [X.PK[pbB]], w=awk(ic, nt))
                if band + 2 < 4:
                    load_band(e, band + 2)
            def load_w2(e_, o_):
                for hv in range(2):
                    em.dma("gp" if hv == 0 else "sp",
                           W2[o_ % 2].ap(rc0=hv * 1024, rcols=1024),
                           d["w2cm"][e_, o_][:, hv * 8:(hv + 1) * 8, :]
                           .rearrange("p c k -> p (c k)"),
                           r=[("D", "w2cm")], w=W2[o_ % 2].keys(hv * 1024, 1024))
            load_w2(e, 0)
            load_w2(e, 1)
            for o in range(8):
                ob = o % 2
                if o >= 2:
                    pass
                for nt in range(2):
                    pbo = 4 + (o * 2 + nt) % 2
                    fns = []
                    for ic in range(16):
                        fns.append(lambda t, ic=ic, ob=ob, nt=nt, pbo=pbo: t.matmul(
                            X.PB[pbo][:, :],
                            W2[ob].ap(rc0=ic * 128, rcols=128),
                            aw(ic, nt), start=(ic == 0), stop=(ic == 15)))
                    em.grp("pe", fns, r=W2[ob].keys() + ACTW.keys(), w=[X.PK[pbo]])
                    par = (o * 2 + nt) % 2
                    em.op("dve", lambda v, nt=nt, pbo=pbo, par=par: v.tensor_tensor(
                        FLSP[par].ap(), X.PB[pbo][:, :], DW.ap(rc0=nt * 512, rcols=512),
                        ALU.mult),
                        r=[X.PK[pbo]] + DW.keys(nt * 512, 512), w=FLSP[par].keys())
                    em.op("dve", lambda v, o=o, nt=nt, par=par: v.tensor_tensor(
                        G[o].ap(rc0=nt * 512, rcols=512),
                        G[o].ap(rc0=nt * 512, rcols=512), FLSP[par].ap(), ALU.add),
                        r=G[o].keys(nt * 512, 512) + FLSP[par].keys(),
                        w=G[o].keys(nt * 512, 512))
                if o + 2 < 8:
                    load_w2(e, o + 2)
                nxt = (e + 1, half) if e + 1 < 8 else ((0, half + 1) if half == 0 else None)
                if nxt is not None and nxt[1] < 2:
                    if o == 4:
                        X.l0_load_band_next = True
                        load_band(nxt[0], 0)
                    if o == 6:
                        load_band(nxt[0], 1)
        if "g0" in X.dbg:
            for c in range(DC):
                em.dma("sp", X.dbg["g0"][c * 128:(c + 1) * 128,
                                         half * 1024:(half + 1) * 1024],
                       G[c].ap(), r=G[c].keys(), w=[("D", "g0")])
        for nt in range(2):
            j = half * 2 + nt
            sb, qb = 2 * (j % 2), 2 * (j % 2) + 1
            for c in range(DC):
                u = c % 2
                noff = half * 1024 + nt * 512
                em.dma("sp", XST[u].ap(), d["xT"][c * 128:(c + 1) * 128,
                                                  noff:noff + 512],
                       r=[("D", "x0")], w=XST[u].keys())
                em.dma("sp", SST[u].ap(), X.s0d[c * 128:(c + 1) * 128, noff:noff + 512],
                       r=[("D", "s0")], w=SST[u].keys())
                em.op("act", lambda sc, c=c, nt=nt, half=half: sc.activation(
                    GSIL.ap(), X.G[c].ap(rc0=nt * 512, rcols=512), AF.Silu),
                    r=X.G[c].keys(nt * 512, 512), w=GSIL.keys())
                em.op("dve", lambda v, u=u: v.tensor_tensor(
                    GSIL.ap(), GSIL.ap(), SST[u].ap(), ALU.mult),
                    r=GSIL.keys() + SST[u].keys(), w=GSIL.keys())
                em.op("dve", lambda v, u=u: v.tensor_tensor(
                    X1T[u].ap(), GSIL.ap(), XST[u].ap(), ALU.add),
                    r=GSIL.keys() + XST[u].keys(), w=X1T[u].keys())
                em.dma("sp", X.x1d[c * 128:(c + 1) * 128, noff:noff + 512],
                       X1T[u].ap(), r=X1T[u].keys(), w=[("D", "x1")])
                em.op("act", lambda sc, u=u: sc.activation(SQT[u].ap(), X1T[u].ap(),
                                                           AF.Square),
                      r=X1T[u].keys(), w=SQT[u].keys())
                stat_mm(X, sb, slice(0, 512), X1T[u].ap(), X1T[u].keys(),
                        start=(c == 0), stop=(c == DC - 1))
                stat_mm(X, qb, slice(0, 512), SQT[u].ap(), SQT[u].keys(),
                        start=(c == 0), stop=(c == DC - 1))
            emit_ln_rows_slice(X, 1, j, sb, qb)


def emit_l1_attn(X):
    em, C, d = X.em, X.C, X.d
    P0 = X.P0
    R = lambda n, c0, cols, dt=F32: Reg(X, n, c0, cols, dt)
    K8 = R("l1k", P0, 8192, BF16)
    Q8 = R("l1q", P0 + 8192, 8192, BF16)
    WQA = R("l1wa", 16640, 2048, BF16)
    WQB = R("l1wb", 10240, 2048, BF16)
    SCE = R("l1e", 35072, 1024, BF16)
    SCA = R("l1a", 36096, 1024, BF16)
    SCT = SCE
    OTM = R("l1o", 37120, 512, BF16)
    V16 = R("l1v", 37888, 8192, BF16)
    V3 = lambda: V16.ap().rearrange("p (c k) -> p c k", c=16)

    def wv3(c):
        W = WQA if c < 4 else WQB
        return W.ap().rearrange("p (c k) -> p c k", c=4)[:, c % 4, :]

    wkeys = WQA.keys() + WQB.keys()
    sm = lambda i: X.SMALL.ap(rc0=i, rcols=1)
    smk = X.SMALL.keys()
    skc = lambda i, n=1: [("S", j) for j in range(i, i + n)]

    for wi, (nm, scale) in enumerate((("wq1", SC1), ("wk1", 1.0), ("wv1", 1.0))):
        em.dma("gp", WQA.ap().rearrange("p (c k) -> p c k", c=4),
               d[nm][0:512, :].rearrange("(c p) n -> p c n", p=128),
               r=[("D", nm)], w=WQA.keys())
        em.dma("gp", WQB.ap().rearrange("p (c k) -> p c k", c=4),
               d[nm][512:1024, :].rearrange("(c p) n -> p c n", p=128),
               r=[("D", nm)], w=WQB.keys())
        if wi < 2:
            DST = Q8 if wi == 0 else K8
            for ec in range(DC):
                for nk in range(4):
                    pb = 4 + nk % 2
                    fns = []
                    for c in range(DC):
                        fns.append(lambda t, c=c, ec=ec, nk=nk, pb=pb: t.matmul(
                            X.PB[pb][:, :], wv3(c)[:, ec * 128:(ec + 1) * 128],
                            X.XN[c].ap(rc0=nk * 512, rcols=512),
                            start=(c == 0), stop=(c == DC - 1)))
                    em.grp("pe", fns, r=wkeys + sum(
                        [X.XN[c].keys(nk * 512, 512) for c in range(DC)], []),
                        w=[X.PK[pb]])
                    em.op("act", lambda sc, ec=ec, nk=nk, pb=pb, DST=DST, scale=scale:
                          sc.activation(DST.ap(rc0=ec * 2048 + nk * 512, rcols=512),
                                        X.PB[pb][:, :], AF.Copy, scale=scale),
                          r=[X.PK[pb]], w=DST.keys(ec * 2048 + nk * 512, 512))
        else:
            for kt in range(16):
                for eh in range(2):
                    pb = 4 + (kt * 2 + eh) % 2
                    fns = []
                    for c in range(DC):
                        fns.append(lambda t, c=c, kt=kt, eh=eh, pb=pb: t.matmul(
                            X.PB[pb][:, :], X.XN[c].ap(rc0=kt * 128, rcols=128),
                            wv3(c)[:, eh * 512:(eh + 1) * 512],
                            start=(c == 0), stop=(c == DC - 1)))
                    em.grp("pe", fns, r=wkeys + sum(
                        [X.XN[c].keys(kt * 128, 128) for c in range(DC)], []),
                        w=[X.PK[pb]])
                    em.op("act", lambda sc, kt=kt, eh=eh, pb=pb: sc.activation(
                        V3()[:, kt, eh * 512:(eh + 1) * 512], X.PB[pb][:, :], AF.Copy),
                        r=[X.PK[pb]], w=V16.keys(kt * 1024 + eh * 512, 512))

    if "l1q" in X.dbg:
        for ec in range(DC):
            em.op("dve", lambda v, ec=ec: v.tensor_copy(SCE.ap(), Q8.ap(rc0=ec * 2048, rcols=2048)),
                  r=Q8.keys(), w=SCE.keys())
            em.dma("sp", X.dbg["l1q"][ec * 128:(ec + 1) * 128, :].bitcast(BF16)[:, 0:2048],
                   SCE.ap(), r=SCE.keys(), w=[("D", "l1q")])
            em.op("dve", lambda v, ec=ec: v.tensor_copy(SCE.ap(), K8.ap(rc0=ec * 2048, rcols=2048)),
                  r=K8.keys(), w=SCE.keys())
            em.dma("sp", X.dbg["l1k"][ec * 128:(ec + 1) * 128, :].bitcast(BF16)[:, 0:2048],
                   SCE.ap(), r=SCE.keys(), w=[("D", "l1k")])

    for jt in range(16):
        for i in range(4):
            fns = []
            for ec in range(DC):
                fns.append(lambda t, ec=ec, i=i, jt=jt: t.matmul(
                    X.PB[i][:, :], K8.ap(rc0=ec * 2048 + jt * 128, rcols=128),
                    Q8.ap(rc0=ec * 2048 + i * 512, rcols=512),
                    start=(ec == 0), stop=(ec == DC - 1)))
            em.grp("pe", fns, r=K8.keys() + Q8.keys(), w=[X.PK[i]])
        for i in range(4):
            em.op("dve", lambda v, i=i: v.reduce_max(
                sm(40 + i), X.PB[i][:, :], axis=AX.X, negate=True),
                r=[X.PK[i]], w=skc(40 + i))
        em.op("dve", lambda v: v.tensor_reduce(
            sm(44), X.SMALL.ap(rc0=40, rcols=4), AX.X, ALU.min),
            r=skc(40, 4), w=skc(44))
        for i in range(4):
            em.op("act", lambda sc, i=i: sc.activation(
                SCE.ap(rc0=i * 512, rcols=512), X.PB[i][:, :], AF.Exp,
                bias=sm(44), accum_out=sm(45 + i)),
                r=[X.PK[i]] + skc(44), w=SCE.keys(i * 512, 512) + skc(45 + i))
        em.op("dve", lambda v: v.reduce_sum(sm(49), X.SMALL.ap(rc0=45, rcols=4), AX.X),
              r=skc(45, 4), w=skc(49))
        em.op("dve", lambda v: v.reciprocal(sm(50), sm(49)), r=skc(49), w=skc(50))
        em.op("dve", lambda v: v.tensor_scalar(SCA.ap(), SCE.ap(), sm(50), None, ALU.mult),
              r=SCE.keys() + skc(50), w=SCA.keys())
        for kc in range(16):
            pb = 6 + kc % 2
            em.op("pe", lambda t, kc=kc, pb=pb: t.transpose(
                X.PBb[pb][:, 0:128], SCA.ap(rc0=kc * 128, rcols=128), C["idb"].ap()),
                r=SCA.keys(kc * 128, 128) + [("C", "idb")], w=[X.PK[pb]])
            em.op("dve", lambda v, kc=kc, pb=pb: v.tensor_copy(
                SCT.ap(rc0=kc * 128, rcols=128), X.PBb[pb][:, 0:128]),
                r=[X.PK[pb]], w=SCT.keys(kc * 128, 128))
        for eh in range(2):
            pb = 4 + eh
            fns = []
            for kc in range(16):
                fns.append(lambda t, kc=kc, eh=eh, pb=pb: t.matmul(
                    X.PB[pb][:, :], SCT.ap(rc0=kc * 128, rcols=128),
                    V3()[:, kc, eh * 512:(eh + 1) * 512],
                    start=(kc == 0), stop=(kc == 15)))
            em.grp("pe", fns, r=SCT.keys() + V16.keys(), w=[X.PK[pb]])
            em.op("act", lambda sc, eh=eh, pb=pb: sc.activation(
                OTM.ap(rc0=eh * 512, rcols=512), X.PB[pb][:, :], AF.Copy),
                r=[X.PK[pb]], w=OTM.keys(eh * 512, 512))
        if jt == 15 and "l1at" in X.dbg:
            em.dma("sp", X.dbg["l1at"][0:128, :].bitcast(BF16)[:, 0:2048], SCA.ap(),
                   r=SCA.keys(), w=[("D", "l1at")])
            em.dma("sp", X.dbg["l1at"][256:384, 0:64], X.SMALL.ap(rc0=40, rcols=64),
                   r=X.SMALL.keys(), w=[("D", "l1at")])
            em.dma("sp", X.dbg["l1at"][384:512, :].bitcast(BF16)[:, 0:1024],
                   OTM.ap(rc0=0, rcols=1024), r=OTM.keys(), w=[("D", "l1at")])
        for ec in range(DC):
            pb = 6 + ec % 2
            em.op("pe", lambda t, ec=ec, pb=pb: t.transpose(
                X.PBb[pb][:, 0:128], OTM.ap(rc0=ec * 128, rcols=128), C["idb"].ap()),
                r=OTM.keys(ec * 128, 128) + [("C", "idb")], w=[X.PK[pb]])
            em.op("dve", lambda v, ec=ec, jt=jt, pb=pb: v.tensor_copy(
                X.XN[ec].ap(rc0=jt * 128, rcols=128), X.PBb[pb][:, 0:128]),
                r=[X.PK[pb]], w=X.XN[ec].keys(jt * 128, 128))


def emit_l1_ffn(X):
    em, C, d = X.em, X.C, X.d
    P0 = X.P0
    R = lambda n, c0, cols, dt=F32: Reg(X, n, c0, cols, dt)
    NHR = R("nh", P0, 4096, BF16)
    ACTW = R("actw1", P0 + 4096, 8192, BF16)
    MH = R("mh", P0 + 4096, 1024)
    AH = R("ah", P0 + 5120, 1024)
    X1S = [R("x1sa", P0 + 6144, 1024), R("x1sb", P0 + 7168, 1024)]
    XC = R("xc", P0 + 8192, 1024)
    T2 = R("t2", P0 + 9216, 1024)
    LG1 = R("lg1", P0 + 10240, 512)
    W1 = [R("l1w1a", P0 + 12288, 2048, BF16), R("l1w1b", P0 + 14336, 2048, BF16)]
    W2 = [R("l1w2a", P0 + 16384, 1024, BF16), R("l1w2b", P0 + 17408, 1024, BF16)]
    GSIL = R("gsil1", P0 + 18432, 512)
    FLSP = [R("fls1a", 10752, 512), R("fls1b", 11264, 512)]
    G = [R(f"g1_{o}", 37888 + o * 1024, 1024) for o in range(8)]
    X1C = [R("x1ca", P0 + 4096, 512), R("x1cb", P0 + 4608, 512)]
    OUTT = [R("outta", P0 + 5120, 512), R("outtb", P0 + 5632, 512)]
    w1v = lambda W: W.ap().rearrange("p (c k) -> p c k", c=8)
    aw = lambda hc, nt: ACTW.ap(rc0=((hc % 16) * 2 + nt) * 512, rcols=512)
    awk = lambda hc, nt: ACTW.keys(((hc % 16) * 2 + nt) * 512, 512)
    DW = X.DWBC
    nh = lambda c, nt: NHR.ap(rc0=c * 1024 + nt * 512, rcols=512)
    nhk = lambda c, nt: NHR.keys(c * 1024 + nt * 512, 512)

    for half in range(2):
        for j in range(2):
            k1_broadcast(X, X.ROWA.ap(p0=0, pn=1, rc0=half * 1024 + j * 512, rcols=512),
                         2, MH.ap(rc0=j * 512, rcols=512), MH.keys(j * 512, 512),
                         X.ROWA.keys())
            k1_broadcast(X, X.ROWC.ap(p0=0, pn=1, rc0=half * 1024 + j * 512, rcols=512),
                         3, AH.ap(rc0=j * 512, rcols=512), AH.keys(j * 512, 512),
                         X.ROWC.keys())
        for c in range(DC):
            u = c % 2
            em.dma("sp", X1S[u].ap(), X.x1d[c * 128:(c + 1) * 128,
                                           half * 1024:(half + 1) * 1024],
                   r=[("D", "x1")], w=X1S[u].keys())
            em.op("dve", lambda v, u=u: v.tensor_tensor(XC.ap(), X1S[u].ap(), MH.ap(),
                                                        ALU.subtract),
                  r=X1S[u].keys() + MH.keys(), w=XC.keys())
            em.op("dve", lambda v: v.tensor_tensor(T2.ap(), XC.ap(), AH.ap(), ALU.mult),
                  r=XC.keys() + AH.keys(), w=T2.keys())
            for nt in range(2):
                em.op("dve", lambda v, c=c, nt=nt: v.tensor_scalar(
                    nh(c, nt), T2.ap(rc0=nt * 512, rcols=512),
                    C["nwc"].ap()[:, c:c + 1], None, ALU.mult),
                    r=T2.keys() + [("C", "nwc")], w=nhk(c, nt))
            for t in range(8):
                em.grp("pe", [lambda tt_, c=c, t=t: tt_.matmul(
                    X.PB[6][:, t * 8:(t + 1) * 8], T2.ap(rc0=t * 128, rcols=128),
                    C["routnw"].ap()[:, c, :],
                    start=(c == 0 and t == 0), stop=(c == DC - 1 and t == 7))],
                    r=T2.keys() + [("C", "routnw")], w=[X.PK[6]])
        if "l1lg" in X.dbg:
            em.op("dve", lambda v, half=half: v.tensor_copy(
                XC.ap(rc0=0, rcols=64), X.PB[6][:, 0:64]), r=[X.PK[6]], w=XC.keys())
            em.dma("sp", X.dbg["l1lg"][half * 128:(half + 1) * 128, 0:64],
                   XC.ap(rc0=0, rcols=64), r=XC.keys(), w=[("D", "l1lg")])
            em.dma("sp", X.dbg["l1lg"][256:384, 0:64],
                   C["routnw"].ap().rearrange("p c e -> p (c e)"), r=[("C", "routnw")],
                   w=[("D", "l1lg")])
            em.dma("sp", X.dbg["l1lg"][384:512, 0:1024], T2.ap(), r=T2.keys(),
                   w=[("D", "l1lg")])
            for c in range(DC):
                em.dma("sp", X.dbg["l1nh"][c * 128:(c + 1) * 128, :].bitcast(BF16)[:, half * 1024:(half + 1) * 1024],
                       NHR.ap(rc0=c * 1024, rcols=1024), r=NHR.keys(), w=[("D", "l1nh")])
        for t in range(8):
            emit_mixer_tail(X, half * 8 + t, LG1, l1=True,
                            logits_ap=X.PB[6][:, t * 8:(t + 1) * 8],
                            logits_keys=[X.PK[6]], tr_bank=7)
        for o in range(8):
            for nt in range(2):
                em.grp("pe", [lambda t, o=o, nt=nt, half=half: t.matmul(
                    X.PB[7][:, 0:512], C["eb2"].ap()[0:8, o * 128:(o + 1) * 128],
                    X.DWFM.ap(p0=0, pn=8, rc0=half * 1024 + nt * 512, rcols=512),
                    start=True, stop=True)],
                    r=[("C", "eb2")] + X.DWFM.keys(), w=[X.PK[7]])
                em.op("dve", lambda v, o=o, nt=nt: v.tensor_scalar(
                    G[o].ap(rc0=nt * 512, rcols=512), X.PB[7][:, 0:512],
                    C["sb2c"].ap()[:, o:o + 1], None, ALU.add),
                    r=[X.PK[7]] + [("C", "sb2c")], w=G[o].keys(nt * 512, 512))
        for e in range(9):
            if e < 8:
                for js in range(2):
                    em.op("pe", lambda t, e=e, half=half, js=js: t.matmul(
                        X.PB[6][:, :], C["sel"].ap()[:, e * 128:(e + 1) * 128],
                        X.DWFM.ap(p0=0, pn=8, rc0=half * 1024 + js * 512, rcols=512),
                        start=True, stop=True),
                        r=[("C", "sel")] + X.DWFM.keys(), w=[X.PK[6]])
                    em.op("dve", lambda v, js=js: v.tensor_copy(
                        DW.ap(rc0=js * 512, rcols=512), X.PB[6][:, :]),
                        r=[X.PK[6]], w=DW.keys(js * 512, 512))
            def load_band1(e_, bg_, band_):
                bb_ = band_ % 2
                hb_ = bg_ * 2048 + band_ * 512
                srcf = d["ew1"][e_] if e_ < 8 else d["sw1"]
                for hv in range(2):
                    em.dma("gp" if hv == 0 else "sp",
                           w1v(W1[bb_])[:, hv * 4:(hv + 1) * 4, :],
                           srcf[hv * 512:(hv + 1) * 512, hb_:hb_ + 512]
                           .rearrange("(c p) n -> p c n", p=128),
                           r=[("D", "ew1" if e_ < 8 else "sw1")],
                           w=W1[bb_].keys(hv * 2048, 2048))

            def load_w2b(e_, o_, bg_):
                srcf = (d["ew2cm"][e_, o_] if e_ < 8 else d["sw2cm"][o_])
                for hv in range(2):
                    em.dma("gp" if hv == 0 else "sp",
                           W2[o_ % 2].ap(rc0=hv * 1024, rcols=1024),
                           srcf[:, bg_ * 16 + hv * 8:bg_ * 16 + (hv + 1) * 8, :]
                           .rearrange("p c k -> p (c k)"),
                           r=[("D", "ew2cm" if e_ < 8 else "sw2cm")],
                           w=W2[o_ % 2].keys(hv * 1024, 1024))

            if e == 0 and half == 0:
                load_band1(0, 0, 0)
                load_band1(0, 0, 1)
            for bg in range(2):
                for band in range(4):
                    bb = band % 2
                    for hci in range(4):
                        hc = bg * 16 + band * 4 + hci
                        for nt in range(2):
                            pbA = (hc * 2 + nt) % 2
                            fns = []
                            for c in range(DC):
                                fns.append(lambda t, c=c, hci=hci, bb=bb, pbA=pbA, nt=nt:
                                           t.matmul(X.PB[pbA][:, :],
                                                    w1v(W1[bb])[:, c, hci * 128:(hci + 1) * 128],
                                                    nh(c, nt),
                                                    start=(c == 0), stop=(c == DC - 1)))
                            em.grp("pe", fns, r=W1[bb].keys() + sum(
                                [nhk(c, nt) for c in range(DC)], []), w=[X.PK[pbA]])
                            bias = (C["eb1c"].ap()[:, e, hc:hc + 1] if e < 8 else
                                    C["sb1c"].ap()[:, hc:hc + 1])
                            em.op("act", lambda sc, hc=hc, nt=nt, pbA=pbA, bias=bias:
                                  sc.activation(aw(hc, nt), X.PB[pbA][:, :], AF.Relu,
                                                bias=bias),
                                  r=[X.PK[pbA], ("C", "eb1c" if e < 8 else "sb1c")],
                                  w=awk(hc, nt))
                    if band + 2 < 4:
                        load_band1(e, bg, band + 2)
                load_w2b(e, 0, bg)
                load_w2b(e, 1, bg)
                for o in range(8):
                    ob = o % 2
                    for nt in range(2):
                        pbo = 4 + (o * 2 + nt) % 2
                        fns = []
                        for hh in range(16):
                            fns.append(lambda t, hh=hh, ob=ob, nt=nt, pbo=pbo: t.matmul(
                                X.PB[pbo][:, :], W2[ob].ap(rc0=hh * 128, rcols=128),
                                aw(hh, nt), start=(hh == 0), stop=(hh == 15)))
                        em.grp("pe", fns, r=W2[ob].keys() + ACTW.keys(), w=[X.PK[pbo]])
                        if e < 8:
                            par = (o * 2 + nt) % 2
                            em.op("dve", lambda v, nt=nt, pbo=pbo, par=par: v.tensor_tensor(
                                FLSP[par].ap(), X.PB[pbo][:, :],
                                DW.ap(rc0=nt * 512, rcols=512), ALU.mult),
                                r=[X.PK[pbo]] + DW.keys(nt * 512, 512), w=FLSP[par].keys())
                            em.op("dve", lambda v, o=o, nt=nt, par=par: v.tensor_tensor(
                                G[o].ap(rc0=nt * 512, rcols=512),
                                G[o].ap(rc0=nt * 512, rcols=512), FLSP[par].ap(), ALU.add),
                                r=G[o].keys(nt * 512, 512) + FLSP[par].keys(),
                                w=G[o].keys(nt * 512, 512))
                        else:
                            em.op("dve", lambda v, o=o, nt=nt, pbo=pbo: v.tensor_tensor(
                                G[o].ap(rc0=nt * 512, rcols=512),
                                G[o].ap(rc0=nt * 512, rcols=512), X.PB[pbo][:, :],
                                ALU.add),
                                r=G[o].keys(nt * 512, 512) + [X.PK[pbo]],
                                w=G[o].keys(nt * 512, 512))
                    if o + 2 < 8:
                        load_w2b(e, o + 2, bg)
                    if o == 4 or o == 6:
                        band_n = 0 if o == 4 else 1
                        if bg == 0:
                            load_band1(e, 1, band_n)
                        else:
                            nxt = (e + 1, half) if e + 1 < 9 else ((0, half + 1) if half == 0 else None)
                            if nxt is not None and nxt[1] < 2:
                                load_band1(nxt[0], 0, band_n)
        for nt in range(2):
            for c in range(DC):
                u = c % 2
                noff = half * 1024 + nt * 512
                em.dma("sp", X1C[u].ap(), X.x1d[c * 128:(c + 1) * 128, noff:noff + 512],
                       r=[("D", "x1")], w=X1C[u].keys())
                em.op("act", lambda sc, c=c, nt=nt: sc.activation(
                    GSIL.ap(), G[c].ap(rc0=nt * 512, rcols=512), AF.Silu),
                    r=G[c].keys(nt * 512, 512), w=GSIL.keys())
                em.op("dve", lambda v, c=c, noff=noff: v.tensor_tensor(
                    GSIL.ap(), GSIL.ap(), X.XN[c].ap(rc0=noff, rcols=512), ALU.mult),
                    r=GSIL.keys() + X.XN[c].keys(noff, 512), w=GSIL.keys())
                em.op("dve", lambda v, u=u: v.tensor_tensor(
                    OUTT[u].ap(), GSIL.ap(), X1C[u].ap(), ALU.add),
                    r=GSIL.keys() + X1C[u].keys(), w=OUTT[u].keys())
                em.dma("sp", X.out[c * 128:(c + 1) * 128, noff:noff + 512],
                       OUTT[u].ap(), r=OUTT[u].keys(), w=[("D", "out")])


def emit_all(X, debug):
    emit_consts(X)
    P0 = X.P0
    STR0 = Reg(X, "str0", P0, 2048)
    STR1 = Reg(X, "str1", P0 + 2048, 2048)
    SQ = Reg(X, "sq", P0 + 4096, 2048)
    MBM = Reg(X, "mbm", P0 + 6144, 2048)
    MBR = Reg(X, "mbr", P0 + 8192, 2048)
    emit_ln(X, 0, "x0", STR0, STR1, SQ, MBM, MBR)
    dbg_tap_xn(X, "xn0", STR0, STR1)
    emit_mixer_l0(X)
    dbg_tap_dwfm(X, "dw0")
    emit_l0_attn(X)
    emit_l0_wo(X)
    if "s0" in X.dbg:
        X.em.dma("gp", X.dbg["s0"], X.s0d[:, :], r=[("D", "s0")], w=[("D", "dbg_s0")])
    emit_l0_ffn(X)
    if "x1" in X.dbg:
        X.em.dma("gp", X.dbg["x1"], X.x1d[:, :], r=[("D", "x1")], w=[("D", "dbg_x1")])
    emit_ln(X, 1, "x1", STR0, STR1, SQ, MBM, MBR)
    dbg_tap_xn(X, "xn1", STR0, STR1)
    emit_l1_attn(X)
    if "s1" in X.dbg:
        for c in range(DC):
            S = [STR0, STR1][c % 2]
            X.em.op("dve", lambda v, c=c, S=S: v.tensor_copy(S.ap(), X.XN[c].ap()),
                    r=X.XN[c].keys(), w=S.keys())
            X.em.dma("sp", X.dbg["s1"][c * 128:(c + 1) * 128, :], S.ap(), r=S.keys(),
                     w=[("D", "s1")])
    emit_l1_ffn(X)
    if "comb1" in X.dbg:
        X.em.dma("sp", X.dbg["comb1"][0:8, :], X.DWFM.ap(p0=0, pn=8)[0:8, :],
                 r=X.DWFM.keys(), w=[("D", "comb1")])


def dbg_tap_xn(X, nm, STR0, STR1):
    if nm not in X.dbg:
        return
    em = X.em
    for c in range(DC):
        S = [STR0, STR1][c % 2]
        em.op("dve", lambda v, c=c, S=S: v.tensor_copy(S.ap(), X.XN[c].ap()),
              r=X.XN[c].keys(), w=S.keys())
        em.dma("sp", X.dbg[nm][c * 128:(c + 1) * 128, :], S.ap(), r=S.keys(),
               w=[("D", nm)])


def dbg_tap_dwfm(X, nm):
    if nm not in X.dbg:
        return
    em = X.em
    em.dma("sp", X.dbg[nm][0:8, :], X.DWFM.ap(p0=0, pn=8)[0:8, :], r=X.DWFM.keys(),
           w=[("D", nm)])


def kernel(**inputs):
    H = host_prep(inputs)
    x = np.asarray(inputs["L0_attn_X"], np.float32)
    nc = build_program(debug=tuple(DEBUG))
    in_maps = []
    for b in range(8):
        m = {k: H[k] for k in COMMON}
        m["xT"] = np.ascontiguousarray(x[b].T)
        in_maps.append(m)
    res = run_bass_kernel_spmd(nc, in_maps, core_ids=list(range(8)))
    outs = [r["out"].T for r in res.results]
    kernel.last_results = res.results
    return np.stack(outs, 0)


# version 08 (final)
# speedup vs baseline: 1.1066x; 1.0187x over previous
import numpy as np
import ml_dtypes
import concourse.bass as bass
import concourse.mybir as mybir
from concourse.bass_utils import run_bass_kernel_spmd

F32 = mybir.dt.float32
BF16 = mybir.dt.bfloat16
AF = mybir.ActivationFunctionType
ALU = mybir.AluOpType
AX = mybir.AxisListType

N = 2048
D = 1024
DC = 8
NH = 8
DH = 128
LAMBDA_INIT = 0.8 - 0.6 * float(np.exp(-0.3))
SC0 = 1.0 / float(np.sqrt(128.0))
SC1 = 1.0 / 32.0
NHALF = 1024
NT = 512

ACOLS = 46080
BLK = 512

DEBUG = []


class EM:
    QUEUES = ("sp", "gp")
    NLANES = 6

    def __init__(self, nc, es):
        self.nc = nc
        self.streams = ("pe", "act", "dve", "sp", "gp")
        self.entities = ["pe", "act", "dve", "sp", "gp"]
        for q in self.QUEUES:
            self.entities += [f"{q}@{l}" for l in range(self.NLANES)]
        self.sems = {e: es.enter_context(nc.semaphore(f"sem_{e.replace('@','_')}"))
                     for e in self.entities}
        self.cnt = {e: 0 for e in self.entities}
        self.rr = {q: 0 for q in self.QUEUES}
        self.waited = {e: {f: 0 for f in self.entities} for e in self.streams}
        self.last_w = {}
        self.readers = {}
        self.plan = {e: [] for e in self.streams}

    def _unit(self, ent):
        return 16 if "@" in ent else 1

    def _need(self, stream, ent, c):
        if self.waited[stream][ent] < c:
            self.plan[stream].append(("w", ent, c * self._unit(ent)))
            self.waited[stream][ent] = c

    def _pre(self, stream, ent, r, w):
        for k in r:
            lw = self.last_w.get(k)
            if lw is not None:
                self._need(stream, lw[0], lw[1])
        for k in w:
            lw = self.last_w.get(k)
            if lw is not None and lw[0] != ent:
                self._need(stream, lw[0], lw[1])
            for f, c in self.readers.get(k, {}).items():
                if f != ent:
                    self._need(stream, f, c)

    def _post(self, ent, r, w):
        self.cnt[ent] += 1
        c = self.cnt[ent]
        for k in w:
            self.last_w[k] = (ent, c)
            self.readers[k] = {}
        for k in r:
            self.readers.setdefault(k, {})[ent] = c
        return c

    def op(self, e, fn, r=(), w=()):
        self._pre(e, e, r, w)
        self.plan[e].append(("o", fn, e))
        self._post(e, r, w)

    def grp(self, e, fns, r=(), w=()):
        self._pre(e, e, r, w)
        self.plan[e].append(("g", fns, e))
        self._post(e, r, w)

    def dma(self, q, out, in_, r=(), w=()):
        lane = self.rr[q]
        self.rr[q] = (lane + 1) % self.NLANES
        ent = f"{q}@{lane}"
        self._need(q, ent, self.cnt[ent])
        self._pre(q, ent, r, w)
        self.plan[q].append(("o", lambda eng: eng.dma_start(out=out, in_=in_), ent))
        self._post(ent, r, w)

    def final_wait(self):
        for f in self.entities:
            if self.cnt[f]:
                self._need("sp", f, self.cnt[f])

    def emit(self, block):
        secs = dict(pe=block.tensor, act=block.scalar, dve=block.vector,
                    sp=block.sync, gp=block.gpsimd)
        sems = self.sems
        for e in self.streams:
            plan = self.plan[e]

            def body(eng, _plan=plan):
                for item in _plan:
                    if item[0] == "w":
                        eng.wait_ge(sems[item[1]], item[2])
                    elif item[0] == "o":
                        ins = item[1](eng)
                        ins.then_inc(sems[item[2]], self._unit(item[2]))
                    else:
                        ins = None
                        for fn in item[1]:
                            ins = fn(eng)
                        ins.then_inc(sems[item[2]], self._unit(item[2]))
            secs[e](body)


class Reg:

    def __init__(self, ctx, name, c0, cols, dtype=F32):
        self.ctx, self.name, self.c0, self.cols, self.dtype = ctx, name, c0, cols, dtype
        assert c0 % BLK == 0 or dtype is not None
        assert c0 + cols <= ACOLS, (name, c0, cols)

    def keys(self, rc0=0, rcols=None):
        div = 1 if self.dtype == F32 else 2
        if rcols is None:
            rcols = self.cols * div - rc0
        a = (self.c0 + rc0 // div) // BLK
        b = (self.c0 + (rc0 + rcols + div - 1) // div + BLK - 1) // BLK
        return [("A", i) for i in range(a, b)]

    def ap(self, p0=0, pn=128, rc0=0, rcols=None):
        if self.dtype == F32:
            if rcols is None:
                rcols = self.cols - rc0
            return self.ctx.a32[p0:p0 + pn, self.c0 + rc0:self.c0 + rc0 + rcols]
        else:
            if rcols is None:
                rcols = 2 * self.cols - rc0
            return self.ctx.a16[p0:p0 + pn, 2 * self.c0 + rc0:2 * self.c0 + rc0 + rcols]

    def v3(self, c, inner):
        return self.ap().rearrange("p (c n) -> p c n", c=c)


class Ctx:
    pass


BF16_DRAM = ["wq_t", "wk_t", "wv_t", "wo_t", "w1t", "w3t", "w2cm",
             "wq1", "wk1", "wv1", "sw1", "sw2cm", "ew1", "ew2cm"]


def host_prep(inp):
    f = np.float32
    c = lambda a: np.ascontiguousarray(a, dtype=f)
    H = {}
    H["wq_t"] = c(inp["L0_attn_Wq"].T)
    H["wk_t"] = c(inp["L0_attn_Wk"].T)
    H["wv_t"] = c(inp["L0_attn_Wv"].T)
    H["wo_t"] = c(inp["L0_attn_Wo"].T)
    H["lamA"] = c(np.concatenate([inp["L0_attn_lambda_q1"], inp["L0_attn_lambda_k1"]], 1))
    H["lamB"] = c(np.concatenate([inp["L0_attn_lambda_q2"], inp["L0_attn_lambda_k2"]], 1))
    H["rms_row"] = c(inp["L0_attn_rms_scale"][None, :])
    H["gate_t"] = c(inp["L0_ffn_gate_w"].T)
    H["w1t"] = c(np.transpose(inp["L0_ffn_w1"], (0, 2, 1)))
    H["w3t"] = c(np.transpose(inp["L0_ffn_w3"], (0, 2, 1)))
    w2 = np.asarray(inp["L0_ffn_w2"], f)
    H["w2cm"] = c(w2.reshape(8, 8, 128, 16, 128).transpose(0, 1, 4, 3, 2))
    H["wq1"] = c(inp["L1_attn_W_query"])
    H["wk1"] = c(inp["L1_attn_W_key"])
    H["wv1"] = c(inp["L1_attn_W_value"])
    H["nwc"] = c(inp["L1_ffn_norm_weight"].reshape(8, 128).T)
    H["rout_t"] = c(inp["L1_ffn_router_w"].T)
    H["sw1"] = c(inp["L1_ffn_sw1"])
    H["sb1c"] = c(inp["L1_ffn_sb1"].reshape(32, 128).T)
    sw2 = np.asarray(inp["L1_ffn_sw2"], f)
    H["sw2cm"] = c(sw2.reshape(32, 128, 8, 128).transpose(2, 1, 0, 3))
    H["sb2c"] = c(inp["L1_ffn_sb2"].reshape(8, 128).T)
    H["ew1"] = c(inp["L1_ffn_ew1"])
    H["eb1c"] = c(np.transpose(np.asarray(inp["L1_ffn_eb1"], f).reshape(8, 32, 128), (2, 0, 1)))
    ew2 = np.asarray(inp["L1_ffn_ew2"], f)
    H["ew2cm"] = c(ew2.reshape(8, 32, 128, 8, 128).transpose(0, 3, 2, 1, 4))
    H["eb2"] = c(inp["L1_ffn_eb2"])
    H["ident_b"] = np.eye(128, dtype=ml_dtypes.bfloat16)
    H["ident_f"] = np.eye(128, dtype=f)
    H["ones_c"] = np.ones((128, 1), f)
    H["ones_r"] = np.ones((1, 128), f)
    H["ones3"] = np.ones((128, 128), f)
    sel = np.zeros((8, 1024), f)
    for e in range(8):
        sel[e, e * 128:(e + 1) * 128] = 1.0
    H["sel"] = sel
    for nm in BF16_DRAM:
        H[nm] = H[nm].astype(ml_dtypes.bfloat16)
    return H


PER_CORE = "xT"
COMMON = ["wq_t", "wk_t", "wv_t", "wo_t", "lamA", "lamB", "rms_row", "gate_t",
          "w1t", "w3t", "w2cm", "wq1", "wk1", "wv1", "nwc", "rout_t", "sw1",
          "sb1c", "sw2cm", "sb2c", "ew1", "eb1c", "ew2cm", "eb2",
          "ident_b", "ident_f", "ones_c", "ones_r", "ones3", "sel"]


def build_program(debug=()):
    from contextlib import ExitStack
    es = ExitStack()
    nc = bass.Bass("TRN2", target_bir_lowering=False, debug=False)
    X = Ctx()
    X.nc = nc

    d = {}
    d["xT"] = nc.dram_tensor("xT", [1024, 2048], F32, kind="ExternalInput").ap()
    for nm, shp in [("wq_t", [1024, 2048]), ("wk_t", [1024, 2048]),
                    ("wv_t", [1024, 2048]), ("wo_t", [2048, 1024]),
                    ("lamA", [8, 256]), ("lamB", [8, 256]), ("rms_row", [1, 256]),
                    ("gate_t", [1024, 8]),
                    ("w1t", [8, 1024, 2048]), ("w3t", [8, 1024, 2048]),
                    ("w2cm", [8, 8, 128, 16, 128]),
                    ("wq1", [1024, 1024]), ("wk1", [1024, 1024]), ("wv1", [1024, 1024]),
                    ("nwc", [128, 8]), ("rout_t", [1024, 8]), ("sw1", [1024, 4096]),
                    ("sb1c", [128, 32]), ("sw2cm", [8, 128, 32, 128]), ("sb2c", [128, 8]),
                    ("ew1", [8, 1024, 4096]), ("eb1c", [128, 8, 32]),
                    ("ew2cm", [8, 8, 128, 32, 128]), ("eb2", [8, 1024]),
                    ("ident_b", None), ("ident_f", [128, 128]), ("ones_c", [128, 1]),
                    ("ones_r", [1, 128]), ("ones3", [128, 128]), ("sel", [8, 1024])]:
        if nm == "ident_b":
            d[nm] = nc.dram_tensor(nm, [128, 128], BF16, kind="ExternalInput").ap()
        else:
            d[nm] = nc.dram_tensor(nm, shp, BF16 if nm in BF16_DRAM else F32,
                                   kind="ExternalInput").ap()
    out = nc.dram_tensor("out", [1024, 2048], F32, kind="ExternalOutput").ap()
    x1d = nc.dram_tensor("x1d", [1024, 2048], F32, kind="Internal").ap()
    s0d = nc.dram_tensor("s0d", [1024, 2048], BF16, kind="Internal").ap()
    dbg = {}
    for nm in debug:
        dbg[nm] = nc.dram_tensor("dbg_" + nm, [1024, 2048], F32, kind="ExternalOutput").ap()
    X.d, X.out, X.x1d, X.s0d, X.dbg = d, out, x1d, s0d, dbg

    AR = nc.alloc_sbuf_tensor("arena", [128, ACOLS], F32)
    X.a32 = AR.ap()
    X.a16 = AR.ap().bitcast(BF16)
    C = {}
    C["idb"] = nc.alloc_sbuf_tensor("sb_idb", [128, 128], BF16)
    C["idf"] = nc.alloc_sbuf_tensor("sb_idf", [128, 128], F32)
    C["ones_c"] = nc.alloc_sbuf_tensor("sb_ones_c", [128, 1], F32)
    C["ones_r"] = nc.alloc_sbuf_tensor("sb_ones_r", [1, 128], F32)
    C["ones3"] = nc.alloc_sbuf_tensor("sb_ones3", [128, 128], F32)
    C["rms_bc"] = nc.alloc_sbuf_tensor("sb_rms_bc", [128, 256], F32)
    C["nlam"] = nc.alloc_sbuf_tensor("sb_nlam", [128, 8], F32)
    C["gate"] = nc.alloc_sbuf_tensor("sb_gate", [128, 8, 8], F32)
    C["rout"] = nc.alloc_sbuf_tensor("sb_rout", [128, 8, 8], F32)
    C["routnw"] = nc.alloc_sbuf_tensor("sb_routnw", [128, 8, 8], F32)
    C["eb1c"] = nc.alloc_sbuf_tensor("sb_eb1c", [128, 8, 32], F32)
    C["sb1c"] = nc.alloc_sbuf_tensor("sb_sb1c", [128, 32], F32)
    C["sb2c"] = nc.alloc_sbuf_tensor("sb_sb2c", [128, 8], F32)
    C["nwc"] = nc.alloc_sbuf_tensor("sb_nwc", [128, 8], F32)
    C["lamA"] = nc.alloc_sbuf_tensor("sb_lamA", [8, 256], F32)
    C["lamB"] = nc.alloc_sbuf_tensor("sb_lamB", [8, 256], F32)
    C["sel"] = nc.alloc_sbuf_tensor("sb_sel", [8, 1024], F32)
    C["eb2"] = nc.alloc_sbuf_tensor("sb_eb2", [8, 1024], F32)
    C["rmsrow"] = nc.alloc_sbuf_tensor("sb_rmsrow", [1, 256], F32)
    C["lamrow"] = nc.alloc_sbuf_tensor("sb_lamrow", [1, 16], F32)
    X.C = C

    PSA = nc.alloc_psum_tensor("psa", [128, 2048], F32)
    X.PSA = PSA
    PB = [PSA.ap()[:, i * 512:(i + 1) * 512] for i in range(4)] + \
         [nc.alloc_psum_tensor(f"pb{i}", [128, 512], F32).ap() for i in range(4, 8)]
    X.PB = PB
    X.PBb = [p.bitcast(BF16) for p in PB]
    PK = [("P", i) for i in range(8)]
    X.PK = PK

    em = EM(nc, es)
    X.em = em

    R = lambda name, c0, cols, dt=F32: Reg(X, name, c0, cols, dt)
    X.XN = [R(f"xn{c}", c * 1024, 1024, BF16) for c in range(DC)]
    X.ROWA = R("rowA", 8192, 2048)
    X.ROWB = R("rowB", 10240, 2048)
    X.ROWC = R("rowC", 12288, 2048)
    X.SMALL = R("small", 14336, 128)
    X.DWFM = R("dwfm", 14464, 2048)
    X.LAMW = R("lamw", 16512, 128)
    X.DWBC = R("dwbc", 16640, 2048)
    X.P0 = 18688

    emit_all(X, debug)

    em.final_wait()
    block = es.enter_context(nc.Block())
    em.emit(block)
    es.close()
    return nc


def k1_broadcast(X, src_row_ap, psb, dst_ap, dst_keys, src_keys, scale=1.0, pbase=0):
    em, C = X.em, X.C
    n = src_row_ap.shape[-1]
    em.op("pe", lambda t: t.matmul(X.PB[psb][:, 0:n], C["ones3"].ap()[pbase:pbase + 1, :],
                                   src_row_ap, start=True, stop=True),
          r=src_keys + [("C", "ones3")], w=[X.PK[psb]])
    em.op("act", lambda s: s.activation(dst_ap, X.PB[psb][:, 0:n], AF.Copy, scale=scale),
          r=[X.PK[psb]], w=dst_keys)


def stat_mm(X, psb, row_slice, rhs_ap, rhs_keys, start, stop, prow=0):
    em = X.em
    em.grp("pe", [lambda t: t.matmul(X.PB[psb][prow:prow + 1, row_slice],
                                     X.C["ones_c"].ap(), rhs_ap, start=start, stop=stop)],
           r=rhs_keys + [("C", "ones_c")], w=[X.PK[psb]])


def emit_ln_rows_slice(X, layer, j, sum_b, sq_b):
    em = X.em
    RA, RB, RC = X.ROWA, X.ROWB, X.ROWC
    row = lambda R, c0, n=512: R.ap(p0=0, pn=1, rc0=c0, rcols=n)
    ss = lambda i: row(RB, i * 512)
    em.op("act", lambda s: s.activation(row(RA, j * 512), X.PB[sum_b][0:1, :],
                                        AF.Copy, scale=1.0 / D),
          r=[X.PK[sum_b]], w=RA.keys())
    em.op("act", lambda s: s.activation(ss(0), row(RA, j * 512), AF.Square),
          r=RA.keys(), w=RB.keys())
    em.op("dve", lambda v: v.scalar_tensor_tensor(ss(1), X.PB[sq_b][0:1, :],
                                                  1.0 / D, ss(0), ALU.mult,
                                                  ALU.subtract),
          r=[X.PK[sq_b]] + RB.keys(), w=RB.keys())
    em.op("dve", lambda v: v.tensor_scalar(ss(2), ss(1), 1e-5, None, ALU.add),
          r=RB.keys(), w=RB.keys())
    em.op("act", lambda s: s.activation(ss(3), ss(2), AF.Ln), r=RB.keys(), w=RB.keys())
    em.op("act", lambda s: s.activation(row(RC, j * 512), ss(3), AF.Exp, scale=-0.5),
          r=RB.keys(), w=RC.keys())
    if layer == 1:
        em.op("dve", lambda v: v.reciprocal(ss(3), ss(2)), r=RB.keys(), w=RB.keys())
        em.op("dve", lambda v: v.tensor_tensor(ss(0), ss(1), ss(3), ALU.mult),
              r=RB.keys(), w=RB.keys())
        em.op("dve", lambda v: v.tensor_scalar(ss(0), ss(0), 1e-8, None, ALU.add),
              r=RB.keys(), w=RB.keys())
        em.op("act", lambda s: s.activation(ss(3), ss(0), AF.Ln), r=RB.keys(), w=RB.keys())
        em.op("act", lambda s: s.activation(
            X.DWFM.ap(p0=0, pn=1, rc0=j * 512, rcols=512), ss(3), AF.Exp, scale=-0.5),
            r=RB.keys(), w=X.DWFM.keys())


def emit_consts(X):
    em, C, d = X.em, X.C, X.d
    kc = lambda n: [("C", n)]
    em.dma("sp", C["idb"].ap(), d["ident_b"], w=kc("idb"))
    em.dma("sp", C["idf"].ap(), d["ident_f"], w=kc("idf"))
    em.dma("sp", C["ones_c"].ap(), d["ones_c"], w=kc("ones_c"))
    em.dma("sp", C["ones_r"].ap(), d["ones_r"], w=kc("ones_r"))
    em.dma("sp", C["ones3"].ap(), d["ones3"], w=kc("ones3"))
    em.dma("sp", C["rmsrow"].ap(), d["rms_row"], w=kc("rmsrow"))
    em.dma("sp", C["gate"].ap(), d["gate_t"].rearrange("(c p) e -> p c e", p=128),
           w=kc("gate"))
    em.dma("sp", C["rout"].ap(), d["rout_t"].rearrange("(c p) e -> p c e", p=128),
           w=kc("rout"))
    em.dma("sp", C["eb1c"].ap(), d["eb1c"], w=kc("eb1c"))
    em.dma("sp", C["sb1c"].ap(), d["sb1c"], w=kc("sb1c"))
    em.dma("sp", C["sb2c"].ap(), d["sb2c"], w=kc("sb2c"))
    em.dma("sp", C["nwc"].ap(), d["nwc"], w=kc("nwc"))
    em.dma("sp", C["lamA"].ap(), d["lamA"], w=kc("lamA"))
    em.dma("sp", C["lamB"].ap(), d["lamB"], w=kc("lamB"))
    em.dma("sp", C["sel"].ap(), d["sel"], w=kc("sel"))
    em.dma("sp", C["eb2"].ap(), d["eb2"], w=kc("eb2"))
    for c in range(DC):
        em.op("dve", lambda v, c=c: v.tensor_scalar(
            C["routnw"].ap()[:, c, :], C["rout"].ap()[:, c, :],
            C["nwc"].ap()[:, c:c + 1], None, ALU.mult),
            r=kc("rout") + kc("nwc"), w=kc("routnw"))
    em.op("pe", lambda t: t.matmul(X.PB[6][:, 0:256], C["ones_r"].ap(),
                                   C["rmsrow"].ap(), start=True, stop=True),
          r=kc("rmsrow") + kc("ones_r"), w=[X.PK[6]])
    em.op("act", lambda s: s.activation(C["rms_bc"].ap(), X.PB[6][:, 0:256],
                                        AF.Copy, scale=(1.0 - LAMBDA_INIT)),
          r=[X.PK[6]], w=kc("rms_bc"))
    LW = X.LAMW
    for i, nm in enumerate(("lamA", "lamB")):
        t = C[nm]
        em.op("dve", lambda v, t=t, i=i: v.tensor_tensor(
            LW.ap(p0=0, pn=8, rc0=0, rcols=128)[0:8, :],
            t.ap()[:, 0:128], t.ap()[:, 128:256], ALU.mult),
            r=kc(nm), w=LW.keys())
        em.op("dve", lambda v, i=i: v.reduce_sum(
            X.SMALL.ap(p0=0, pn=8, rc0=8 + i, rcols=1)[0:8, :],
            LW.ap(p0=0, pn=8)[0:8, 0:128], axis=AX.X),
            r=LW.keys(), w=X.SMALL.keys())
        em.op("pe", lambda t_, i=i: t_.transpose(
            X.PB[6][0:1, i * 8:(i + 1) * 8],
            X.SMALL.ap(p0=0, pn=8, rc0=8 + i, rcols=1)[0:8, :],
            C["idf"].ap()[0:8, 0:8]),
            r=X.SMALL.keys() + kc("idf"), w=[X.PK[6]])
        em.op("act", lambda s, i=i: s.activation(
            C["lamrow"].ap()[0:1, i * 8:(i + 1) * 8],
            X.PB[6][0:1, i * 8:(i + 1) * 8], AF.Exp),
            r=[X.PK[6]], w=kc("lamrow"))
    em.op("dve", lambda v: v.tensor_tensor(
        C["lamrow"].ap()[0:1, 8:16], C["lamrow"].ap()[0:1, 8:16],
        C["lamrow"].ap()[0:1, 0:8], ALU.subtract),
        r=kc("lamrow"), w=kc("lamrow"))
    em.op("pe", lambda t: t.matmul(X.PB[6][:, 0:8], C["ones_r"].ap(),
                                   C["lamrow"].ap()[0:1, 8:16], start=True, stop=True),
          r=kc("lamrow") + kc("ones_r"), w=[X.PK[6]])
    em.op("dve", lambda v: v.tensor_scalar(C["nlam"].ap(), X.PB[6][:, 0:8],
                                           -LAMBDA_INIT, None, ALU.add),
          r=[X.PK[6]], w=kc("nlam"))


def emit_ln(X, layer, src, STR0, STR1, SQ, MBM, MBR):
    em, C = X.em, X.C
    dsrc = X.d["xT"] if src == "x0" else X.x1d
    dkey = [("D", src)]
    strs = [STR0, STR1]
    if src == "x0":
        for c in range(DC):
            S = strs[c % 2]
            em.dma("sp", S.ap(), dsrc[c * 128:(c + 1) * 128, :], r=dkey, w=S.keys())
            em.op("act", lambda s, S=S: s.activation(SQ.ap(), S.ap(), AF.Square),
                  r=S.keys(), w=SQ.keys())
            for j in range(4):
                stat_mm(X, j, slice(0, 512), S.ap(rc0=j * 512, rcols=512), S.keys(j * 512, 512),
                        start=(c == 0), stop=(c == DC - 1))
                stat_mm(X, 4 + j, slice(0, 512), SQ.ap(rc0=j * 512, rcols=512), SQ.keys(j * 512, 512),
                        start=(c == 0), stop=(c == DC - 1))
        emit_ln_rows(X, layer)
    RA, RC = X.ROWA, X.ROWC
    for j in range(4):
        k1_broadcast(X, RA.ap(p0=0, pn=1, rc0=j * 512, rcols=512), 6,
                     MBM.ap(rc0=j * 512, rcols=512), MBM.keys(j * 512, 512), RA.keys(), 1.0)
        k1_broadcast(X, RC.ap(p0=0, pn=1, rc0=j * 512, rcols=512), 7,
                     MBR.ap(rc0=j * 512, rcols=512), MBR.keys(j * 512, 512), RC.keys(), 1.0)
    if layer == 1:
        emit_alpha(X)
    for c in range(DC):
        S = strs[c % 2]
        em.dma("sp", S.ap(), dsrc[c * 128:(c + 1) * 128, :], r=dkey, w=S.keys())
        em.op("dve", lambda v, S=S: v.tensor_tensor(SQ.ap(), S.ap(), MBM.ap(), ALU.subtract),
              r=S.keys() + MBM.keys(), w=SQ.keys())
        em.op("dve", lambda v, S=S: v.tensor_tensor(S.ap(), SQ.ap(), MBR.ap(), ALU.mult),
              r=SQ.keys() + MBR.keys(), w=S.keys())
        em.op("dve", lambda v, c=c, S=S: v.tensor_copy(X.XN[c].ap(), S.ap()),
              r=S.keys(), w=X.XN[c].keys())
        if layer == 0:
            for t in range(16):
                em.grp("pe", [lambda tt_, c=c, t=t, S=S: tt_.matmul(
                    X.PB[0][:, t * 8:(t + 1) * 8],
                    S.ap(rc0=t * 128, rcols=128), X.C["gate"].ap()[:, c, :],
                    start=(c == 0 and t == 0), stop=(c == DC - 1 and t == 15))],
                    r=S.keys() + [("C", "gate")], w=[X.PK[0]])


def emit_ln_rows(X, layer):
    for j in range(4):
        emit_ln_rows_slice(X, layer, j, j, 4 + j)


def emit_alpha(X):
    em = X.em
    em.op("dve", lambda v: v.tensor_tensor(
        X.ROWC.ap(p0=0, pn=1), X.ROWC.ap(p0=0, pn=1),
        X.DWFM.ap(p0=0, pn=1, rc0=0, rcols=2048), ALU.mult),
        r=X.ROWC.keys() + X.DWFM.keys(), w=X.ROWC.keys())


def emit_mixer_l0(X):
    em, C = X.em, X.C
    SM = X.SMALL
    P0 = X.P0
    LG = Reg(X, "lg", P0, 512)
    for tile in range(16):
        emit_mixer_tail(X, tile, LG, l1=False, logits_ap=X.PB[0][:, tile * 8:(tile + 1) * 8],
                        logits_keys=[X.PK[0]])


def emit_mixer_tail(X, tile, LG, l1, logits_ap, logits_keys, tr_bank=6):
    em, C = X.em, X.C
    SM = X.SMALL
    sm = lambda i: SM.ap(rc0=16 + i, rcols=1)
    smk = SM.keys()
    g = lambda i: LG.ap(rc0=i, rcols=8)
    gk = LG.keys()
    if l1:
        em.op("act", lambda s: s.activation(g(0), logits_ap, AF.Sigmoid), r=logits_keys, w=gk)
    else:
        em.op("dve", lambda v: v.tensor_copy(g(0), logits_ap), r=logits_keys, w=gk)
    em.op("dve", lambda v: v.reduce_max(sm(0), g(0), axis=AX.X), r=gk, w=smk)
    em.op("dve", lambda v: v.tensor_scalar(g(56), g(0), sm(0), None, ALU.is_ge), r=gk + smk, w=gk)
    em.op("dve", lambda v: v.scalar_tensor_tensor(g(64), g(56), -1e30 if not l1 else -1e3,
                                                  g(0), ALU.mult, ALU.add), r=gk, w=gk)
    em.op("dve", lambda v: v.reduce_max(sm(4), g(64), axis=AX.X), r=gk, w=smk)
    if not l1:
        em.op("dve", lambda v: v.reduce_max(sm(1), g(0), axis=AX.X, negate=True), r=gk, w=smk)
        em.op("dve", lambda v: v.reduce_max(sm(5), g(64), axis=AX.X, negate=True), r=gk, w=smk)
        em.op("act", lambda sc: sc.activation(g(8), g(0), AF.Abs), r=gk, w=gk)
        for br, (vv, nvv, zz, rr, srccol) in enumerate([(0, 1, 2, 3, 0), (4, 5, 6, 7, 64)]):
            em.op("dve", lambda v, vv=vv: v.tensor_scalar(g(16), g(8), sm(vv), None, ALU.max),
                  r=gk + smk, w=gk)
            em.op("dve", lambda v: v.reciprocal(g(24), g(16)), r=gk, w=gk)
            em.op("dve", lambda v, vv=vv: v.tensor_scalar(g(16), g(0), sm(vv), None, ALU.subtract),
                  r=gk + smk, w=gk)
            em.op("dve", lambda v: v.tensor_tensor(g(16), g(16), g(24), ALU.mult), r=gk, w=gk)
            em.op("dve", lambda v: v.tensor_scalar(g(32), g(16), -0.02, None, ALU.is_ge),
                  r=gk, w=gk)
            em.op("act", lambda s, nvv=nvv, srccol=srccol: s.activation(
                g(40), g(srccol), AF.Exp, bias=sm(nvv)), r=gk + smk, w=gk)
            em.op("dve", lambda v: v.tensor_tensor(g(48), g(32), g(40), ALU.mult), r=gk, w=gk)
            em.op("dve", lambda v, zz=zz: v.reduce_sum(sm(zz), g(48), axis=AX.X), r=gk, w=smk)
            em.op("dve", lambda v, zz=zz, rr=rr: v.reciprocal(sm(rr), sm(zz)), r=smk, w=smk)
    else:
        em.op("dve", lambda v: v.tensor_tensor(sm(2), sm(0), sm(4), ALU.add), r=smk, w=smk)
        em.op("dve", lambda v: v.tensor_scalar(sm(2), sm(2), 1e-6, None, ALU.add), r=smk, w=smk)
        em.op("dve", lambda v: v.reciprocal(sm(6), sm(2)), r=smk, w=smk)
        em.op("dve", lambda v: v.tensor_tensor(sm(3), sm(0), sm(6), ALU.mult), r=smk, w=smk)
        em.op("dve", lambda v: v.tensor_tensor(sm(7), sm(4), sm(6), ALU.mult), r=smk, w=smk)
    em.op("dve", lambda v: v.tensor_scalar(g(72), g(64), sm(4), None, ALU.is_ge),
          r=gk + smk, w=gk)
    em.op("dve", lambda v: v.tensor_scalar(g(80), g(56), sm(3), None, ALU.mult),
          r=gk + smk, w=gk)
    em.op("dve", lambda v: v.scalar_tensor_tensor(g(80), g(72), sm(7), g(80), ALU.mult, ALU.add),
          r=gk + smk, w=gk)
    em.op("pe", lambda t: t.transpose(X.PB[tr_bank][0:8, 0:128], g(80), C["idf"].ap()),
          r=gk + [("C", "idf")], w=[X.PK[tr_bank]])
    em.op("dve", lambda v: v.tensor_copy(
        X.DWFM.ap(p0=0, pn=8, rc0=tile * 128, rcols=128)[0:8, :], X.PB[tr_bank][0:8, 0:128]),
        r=[X.PK[tr_bank]], w=X.DWFM.keys(tile * 128, 128))


def emit_l0_attn(X):
    em, C = X.em, X.C
    P0 = X.P0
    R = lambda n, c0, cols, dt=F32: Reg(X, n, c0, cols, dt)
    WH = [R("wh0", P0, 1024, BF16), R("wh1", P0 + 1024, 1024, BF16),
          R("wh2", P0 + 25600, 1024, BF16)]
    VH = R("vh", P0 + 2048, 2048, BF16)
    QH = R("qh", P0 + 4096, 2048, BF16)
    KH = R("kh", 16640, 2048, BF16)
    RB = [R("rb0", P0 + 6144, 1024, BF16), R("rb1", P0 + 7168, 1024, BF16)]
    ATT = R("att", P0 + 8192, 1024, BF16)
    E1 = R("e1", 8192, 1024, BF16)
    E2 = R("e2", 9216, 1024, BF16)
    X.OALL = R("oall", P0 + 9216, 16384, BF16)
    WH3 = lambda b: WH[b].ap().rearrange("p (c k) -> p c k", c=8)
    VH3 = lambda: VH.ap().rearrange("p (c k) -> p c k", c=16)
    sm = lambda i: X.SMALL.ap(rc0=i, rcols=1)
    smk = X.SMALL.keys()
    skc = lambda i, n=1: [("S", j) for j in range(i, i + n)]
    d = X.d
    PK = X.PK

    def stage_a_scores(h, qt, br):
        qoff = br * 2048 + qt * 128
        base = 40 + br * 12
        for i in range(4):
            em.op("pe", lambda t, i=i, qoff=qoff, br=br: t.matmul(
                X.PB[i][:, :], QH.ap(rc0=qoff, rcols=128),
                KH.ap(rc0=br * 2048 + i * 512, rcols=512),
                start=True, stop=True),
                r=QH.keys(qoff, 128) + KH.keys(br * 2048 + i * 512, 512),
                w=[PK[i]])
            if i % 2 == 1:
                em.op("dve", lambda v, i=i, base=base: v.reduce_max(
                    sm(base + i // 2), X.PSA.ap()[:, (i - 1) * 512:(i + 1) * 512],
                    axis=AX.X, negate=True),
                    r=[PK[i - 1], PK[i]], w=skc(base + i // 2))

    def stage_a_softmax(h, qt, br):
        base = 40 + br * 12
        EB = E1 if br == 0 else E2
        for j in range(2):
            em.op("act", lambda sc, j=j, base=base, EB=EB: sc.activation(
                EB.ap(rc0=j * 1024, rcols=1024),
                X.PSA.ap()[:, j * 1024:(j + 1) * 1024], AF.Exp,
                bias=sm(base + j), accum_out=sm(base + 2 + j)),
                r=[PK[2 * j], PK[2 * j + 1]] + skc(base + j), w=EB.keys() + skc(base + 2 + j))
        em.op("dve", lambda v, base=base: v.tensor_reduce(
            sm(base + 4), X.SMALL.ap(rc0=base, rcols=2), AX.X, ALU.min),
            r=skc(base, 2), w=skc(base + 4))
        em.op("dve", lambda v, base=base: v.tensor_scalar(
            X.SMALL.ap(rc0=base + 5, rcols=2), X.SMALL.ap(rc0=base, rcols=2),
            sm(base + 4), -1.0, ALU.subtract, ALU.mult),
            r=skc(base, 2) + skc(base + 4), w=skc(base + 5, 2))
        em.op("act", lambda sc, base=base: sc.activation(
            X.SMALL.ap(rc0=base + 7, rcols=2), X.SMALL.ap(rc0=base + 5, rcols=2),
            AF.Exp), r=skc(base + 5, 2), w=skc(base + 7, 2))
        em.op("dve", lambda v, base=base: v.tensor_tensor(
            X.SMALL.ap(rc0=base + 9, rcols=2), X.SMALL.ap(rc0=base + 2, rcols=2),
            X.SMALL.ap(rc0=base + 7, rcols=2), ALU.mult),
            r=skc(base + 2, 2) + skc(base + 7, 2), w=skc(base + 9, 2))
        em.op("dve", lambda v, base=base: v.reduce_sum(
            sm(base + 11), X.SMALL.ap(rc0=base + 9, rcols=2), AX.X),
            r=skc(base + 9, 2), w=skc(base + 11))
        em.op("dve", lambda v, base=base: v.reciprocal(sm(base + 11), sm(base + 11)),
              r=skc(base + 11), w=skc(base + 11))

    def stage_a_combine(h, qt):
        RBt = RB[qt % 2]
        em.op("dve", lambda v, h=h: v.tensor_tensor(sm(76), sm(63),
                                                    C["nlam"].ap()[:, h:h + 1], ALU.mult),
              r=skc(63) + [("C", "nlam")], w=skc(76))
        em.op("dve", lambda v: v.tensor_scalar(
            X.SMALL.ap(rc0=78, rcols=2), X.SMALL.ap(rc0=47, rcols=2),
            sm(51), None, ALU.mult), r=skc(47, 2) + skc(51), w=skc(78, 2))
        em.op("dve", lambda v: v.tensor_scalar(
            X.SMALL.ap(rc0=80, rcols=2), X.SMALL.ap(rc0=59, rcols=2),
            sm(76), None, ALU.mult), r=skc(59, 2) + skc(76), w=skc(80, 2))
        for j in range(2):
            em.op("dve", lambda v, j=j: v.tensor_scalar(
                E1.ap(rc0=j * 1024, rcols=1024), E1.ap(rc0=j * 1024, rcols=1024),
                sm(78 + j), None, ALU.mult),
                r=E1.keys() + skc(78 + j), w=E1.keys())
            em.op("dve", lambda v, j=j: v.tensor_scalar(
                E2.ap(rc0=j * 1024, rcols=1024), E2.ap(rc0=j * 1024, rcols=1024),
                sm(80 + j), None, ALU.mult),
                r=E2.keys() + skc(80 + j), w=E2.keys())
        em.op("dve", lambda v, RBt=RBt: v.tensor_tensor(
            RBt.ap(), E1.ap(), E2.ap(), ALU.add),
            r=E2.keys() + E1.keys(), w=RBt.keys())

    def stage_b(h, qt):
        RBt = RB[qt % 2]
        for half16 in range(2):
            pb = 6 + half16
            for j in range(8):
                kc = half16 * 8 + j
                em.op("pe", lambda t, kc=kc, j=j, pb=pb, RBt=RBt: t.transpose(
                    X.PBb[pb][:, j * 128:(j + 1) * 128],
                    RBt.ap(rc0=kc * 128, rcols=128), C["idb"].ap()),
                    r=RBt.keys(kc * 128, 128) + [("C", "idb")], w=[PK[pb]])
            em.op("act", lambda sc, half16=half16, pb=pb: sc.activation(
                ATT.ap(rc0=half16 * 1024, rcols=1024), X.PBb[pb][:, 0:1024], AF.Copy),
                r=[PK[pb]], w=ATT.keys(half16 * 1024, 1024))
        pbO = 4 + qt % 2
        fns = []
        for kc in range(16):
            fns.append(lambda t, kc=kc, pbO=pbO: t.matmul(
                X.PB[pbO][:, 0:256], ATT.ap(rc0=kc * 128, rcols=128),
                VH3()[:, kc, :], start=(kc == 0), stop=(kc == 15)))
        em.grp("pe", fns, r=ATT.keys() + VH.keys(), w=[PK[pbO]])
        em.op("act", lambda sc, pbO=pbO, RBt=RBt: sc.activation(
            RBt.ap(rc0=256, rcols=256), X.PB[pbO][:, 0:256], AF.Square,
            accum_out=sm(84)),
            r=[PK[pbO]], w=RBt.keys(0, 512) + skc(84))
        em.op("dve", lambda v: v.tensor_scalar(sm(85), sm(84), 1.0 / 256.0, 1e-5,
                                               ALU.mult, ALU.add), r=skc(84), w=skc(85))
        em.op("act", lambda sc: sc.activation(sm(85), sm(85), AF.Ln), r=skc(85), w=skc(85))
        em.op("act", lambda sc: sc.activation(sm(86), sm(85), AF.Exp, scale=-0.5),
              r=skc(85), w=skc(86))
        em.op("dve", lambda v, pbO=pbO, RBt=RBt: v.scalar_tensor_tensor(
            RBt.ap(rc0=0, rcols=256), X.PB[pbO][:, 0:256], sm(86),
            C["rms_bc"].ap(), ALU.mult, ALU.mult),
            r=[PK[pbO]] + skc(86) + [("C", "rms_bc")], w=RBt.keys(0, 512))
        for j in range(2):
            pb = 6 + j
            em.op("pe", lambda t, j=j, pb=pb, RBt=RBt: t.transpose(
                X.PBb[pb][:, 0:128], RBt.ap(rc0=j * 128, rcols=128), C["idb"].ap()),
                r=RBt.keys(0, 512) + [("C", "idb")], w=[PK[pb]])
            ec = 2 * h + j
            em.op("dve", lambda v, ec=ec, qt=qt, pb=pb: v.tensor_copy(
                X.OALL.ap(rc0=ec * 2048 + qt * 128, rcols=128), X.PBb[pb][:, 0:128]),
                r=[PK[pb]], w=X.OALL.keys(ec * 2048 + qt * 128, 128))

    for h in range(NH):
        wsrc = [("wq_t", 0), ("wk_t", 1), ("wv_t", 2)]
        for nm, b in wsrc:
            em.dma("gp", WH3(b)[:, :, :],
                   d[nm][:, h * 256:(h + 1) * 256].rearrange("(c p) n -> p c n", p=128),
                   r=[("D", nm)], w=WH[b].keys())
            if nm == "wq_t":
                for j in range(2):
                    for nk in range(4):
                        pb = 4 + nk % 2
                        fns = []
                        for c in range(DC):
                            fns.append(lambda t, c=c, j=j, nk=nk, pb=pb, b=b: t.matmul(
                                X.PB[pb][:, :], WH3(b)[:, c, j * 128:(j + 1) * 128],
                                X.XN[c].ap(rc0=nk * 512, rcols=512),
                                start=(c == 0), stop=(c == DC - 1)))
                        em.grp("pe", fns, r=WH[b].keys() + sum(
                            [X.XN[c].keys(nk * 512, 512) for c in range(DC)], []),
                            w=[PK[pb]])
                        em.op("act", lambda sc, j=j, nk=nk, pb=pb: sc.activation(
                            QH.ap(rc0=j * 2048 + nk * 512, rcols=512), X.PB[pb][:, :],
                            AF.Copy, scale=SC0),
                            r=[PK[pb]], w=QH.keys(j * 2048 + nk * 512, 512))
            elif nm == "wk_t":
                for j in range(2):
                    for nk in range(4):
                        pb = 4 + nk % 2
                        fns = []
                        for c in range(DC):
                            fns.append(lambda t, c=c, j=j, nk=nk, pb=pb, b=b: t.matmul(
                                X.PB[pb][:, :], WH3(b)[:, c, j * 128:(j + 1) * 128],
                                X.XN[c].ap(rc0=nk * 512, rcols=512),
                                start=(c == 0), stop=(c == DC - 1)))
                        em.grp("pe", fns, r=WH[b].keys() + sum(
                            [X.XN[c].keys(nk * 512, 512) for c in range(DC)], []),
                            w=[PK[pb]])
                        em.op("act", lambda sc, j=j, nk=nk, pb=pb: sc.activation(
                            KH.ap(rc0=j * 2048 + nk * 512, rcols=512), X.PB[pb][:, :],
                            AF.Copy),
                            r=[PK[pb]], w=KH.keys(j * 2048 + nk * 512, 512))
            else:
                for kt in range(16):
                    pb = 4 + kt % 2
                    fns = []
                    for c in range(DC):
                        fns.append(lambda t, c=c, kt=kt, pb=pb, b=b: t.matmul(
                            X.PB[pb][:, 0:256], X.XN[c].ap(rc0=kt * 128, rcols=128),
                            WH3(b)[:, c, :], start=(c == 0), stop=(c == DC - 1)))
                    em.grp("pe", fns, r=WH[b].keys() + sum(
                        [X.XN[c].keys(kt * 128, 128) for c in range(DC)], []),
                        w=[PK[pb]])
                    em.op("act", lambda sc, kt=kt, pb=pb: sc.activation(
                        VH3()[:, kt, :], X.PB[pb][:, 0:256], AF.Copy),
                        r=[PK[pb]], w=VH.keys(kt * 256, 256))

        for qt in range(16):
            stage_a_scores(h, qt, 0)
            if qt > 0:
                stage_b(h, qt - 1)
            stage_a_softmax(h, qt, 0)
            stage_a_scores(h, qt, 1)
            stage_a_softmax(h, qt, 1)
            stage_a_combine(h, qt)
        stage_b(h, 15)


def emit_l0_wo(X):
    em, C, d = X.em, X.C, X.d
    P0 = X.P0
    WOF = Reg(X, "wof", P0, 8192, BF16)
    SEV = Reg(X, "sev", P0 + 8192, 1024, BF16)
    WOF3 = lambda: WOF.ap().rearrange("p (c k) -> p c k", c=16)
    em.dma("gp", WOF3(), d["wo_t"].rearrange("(c p) n -> p c n", p=128),
           r=[("D", "wo_t")], w=WOF.keys())
    for dc in range(DC):
        for nk in range(4):
            pb = 4 + nk % 2
            fns = []
            for ec in range(16):
                fns.append(lambda t, ec=ec, dc=dc, nk=nk, pb=pb: t.matmul(
                    X.PB[pb][:, :], WOF3()[:, ec, dc * 128:(dc + 1) * 128],
                    X.OALL.ap(rc0=ec * 2048 + nk * 512, rcols=512),
                    start=(ec == 0), stop=(ec == 15)))
            em.grp("pe", fns, r=WOF.keys() + X.OALL.keys(), w=[X.PK[pb]])
            slot = nk % 2
            em.op("act", lambda sc, pb=pb, slot=slot: sc.activation(
                SEV.ap(rc0=slot * 512, rcols=512), X.PB[pb][:, :], AF.Copy),
                r=[X.PK[pb]], w=SEV.keys(slot * 512, 512))
            em.dma("sp", X.s0d[dc * 128:(dc + 1) * 128, nk * 512:(nk + 1) * 512],
                   SEV.ap(rc0=slot * 512, rcols=512), r=SEV.keys(slot * 512, 512),
                   w=[("D", "s0")])


def emit_l0_ffn(X):
    em, C, d = X.em, X.C, X.d
    P0 = X.P0
    R = lambda n, c0, cols, dt=F32: Reg(X, n, c0, cols, dt)
    W1 = [R("w1a", P0, 2048, BF16), R("w1b", P0 + 2048, 2048, BF16)]
    W3 = [R("w3a", P0 + 4096, 2048, BF16), R("w3b", P0 + 6144, 2048, BF16)]
    W2 = [R("w2a", P0 + 8192, 1024, BF16), R("w2b", P0 + 9216, 1024, BF16)]
    ACTW = R("actw", P0 + 10240, 8192, BF16)
    GSIL = R("gsil", P0 + 18432, 512)
    SIL = [GSIL, R("silb", 10240, 512)]
    FLSP = [R("flsa", 10752, 512), R("flsb", 11264, 512)]
    G = [R(f"g{o}", 37888 + o * 1024, 1024) for o in range(8)]
    X.G = G
    w3v = lambda W: W.ap().rearrange("p (c k) -> p c k", c=8)
    aw = lambda ic, nt: ACTW.ap(rc0=(ic * 2 + nt) * 512, rcols=512)
    awk = lambda ic, nt: ACTW.keys((ic * 2 + nt) * 512, 512)
    DW = X.DWBC

    XST = [R("xst0", P0 + 10240, 512), R("xst1", P0 + 10752, 512)]
    SST = [R("sst0", P0 + 11264, 256, BF16), R("sst1", P0 + 11520, 256, BF16)]
    X1T = [R("x1t0", P0 + 11776, 512), R("x1t1", P0 + 12288, 512)]
    SQT = [R("sqt0", P0 + 12800, 512), R("sqt1", P0 + 13312, 512)]
    FLS = GSIL

    for half in range(2):
        for o in range(8):
            em.op("dve", lambda v, o=o: v.memset(G[o].ap(), 0.0), w=G[o].keys())
        for e in range(8):
            for jslice in range(2):
                em.op("pe", lambda t, e=e, half=half, jslice=jslice: t.matmul(
                    X.PB[6][:, :],
                    C["sel"].ap()[:, e * 128:(e + 1) * 128],
                    X.DWFM.ap(p0=0, pn=8, rc0=half * 1024 + jslice * 512, rcols=512),
                    start=True, stop=True),
                    r=[("C", "sel")] + X.DWFM.keys(), w=[X.PK[6]])
                em.op("dve", lambda v, jslice=jslice: v.tensor_copy(
                    DW.ap(rc0=jslice * 512, rcols=512), X.PB[6][:, :]),
                    r=[X.PK[6]], w=DW.keys(jslice * 512, 512))
            def load_band(e_, band_):
                bb_ = band_ % 2
                em.dma("gp", w3v(W1[bb_]),
                       d["w1t"][e_][:, band_ * 512:(band_ + 1) * 512]
                       .rearrange("(c p) n -> p c n", p=128),
                       r=[("D", "w1t")], w=W1[bb_].keys())
                em.dma("gp", w3v(W3[bb_]),
                       d["w3t"][e_][:, band_ * 512:(band_ + 1) * 512]
                       .rearrange("(c p) n -> p c n", p=128),
                       r=[("D", "w3t")], w=W3[bb_].keys())
            X.l0_load_band = load_band
            if e == 0 and half == 0:
                load_band(0, 0)
                load_band(0, 1)
            for band in range(4):
                bb = band % 2
                for ici in range(4):
                    ic = band * 4 + ici
                    for nt in range(2):
                        pbA = (ic * 2 + nt) % 2
                        pbB = 2 + pbA
                        noff = half * 1024 + nt * 512
                        fns = []
                        for c in range(DC):
                            fns.append(lambda t, c=c, ici=ici, bb=bb, pbA=pbA, noff=noff:
                                       t.matmul(X.PB[pbA][:, :],
                                                w3v(W1[bb])[:, c, ici * 128:(ici + 1) * 128],
                                                X.XN[c].ap(rc0=noff, rcols=512),
                                                start=(c == 0), stop=(c == DC - 1)))
                        em.grp("pe", fns, r=W1[bb].keys() +
                               sum([X.XN[c].keys(noff, 512) for c in range(DC)], []),
                               w=[X.PK[pbA]])
                        fns = []
                        for c in range(DC):
                            fns.append(lambda t, c=c, ici=ici, bb=bb, pbB=pbB, noff=noff:
                                       t.matmul(X.PB[pbB][:, :],
                                                w3v(W3[bb])[:, c, ici * 128:(ici + 1) * 128],
                                                X.XN[c].ap(rc0=noff, rcols=512),
                                                start=(c == 0), stop=(c == DC - 1)))
                        em.grp("pe", fns, r=W3[bb].keys() +
                               sum([X.XN[c].keys(noff, 512) for c in range(DC)], []),
                               w=[X.PK[pbB]])
                        par = (ic * 2 + nt) % 2
                        em.op("act", lambda sc, pbA=pbA, par=par: sc.activation(
                            SIL[par].ap(), X.PB[pbA][:, :], AF.Silu),
                            r=[X.PK[pbA]], w=SIL[par].keys())
                        em.op("dve", lambda v, ic=ic, nt=nt, pbB=pbB, par=par: v.tensor_tensor(
                            aw(ic, nt), SIL[par].ap(), X.PB[pbB][:, :], ALU.mult),
                            r=SIL[par].keys() + [X.PK[pbB]], w=awk(ic, nt))
                if band + 2 < 4:
                    load_band(e, band + 2)
            def load_w2(e_, o_):
                em.dma("gp", W2[o_ % 2].ap(),
                       d["w2cm"][e_, o_].rearrange("p c k -> p (c k)"),
                       r=[("D", "w2cm")], w=W2[o_ % 2].keys())
            load_w2(e, 0)
            load_w2(e, 1)
            for o in range(8):
                ob = o % 2
                if o >= 2:
                    pass
                for nt in range(2):
                    pbo = 4 + (o * 2 + nt) % 2
                    fns = []
                    for ic in range(16):
                        fns.append(lambda t, ic=ic, ob=ob, nt=nt, pbo=pbo: t.matmul(
                            X.PB[pbo][:, :],
                            W2[ob].ap(rc0=ic * 128, rcols=128),
                            aw(ic, nt), start=(ic == 0), stop=(ic == 15)))
                    em.grp("pe", fns, r=W2[ob].keys() + ACTW.keys(), w=[X.PK[pbo]])
                    par = (o * 2 + nt) % 2
                    em.op("dve", lambda v, nt=nt, pbo=pbo, par=par: v.tensor_tensor(
                        FLSP[par].ap(), X.PB[pbo][:, :], DW.ap(rc0=nt * 512, rcols=512),
                        ALU.mult),
                        r=[X.PK[pbo]] + DW.keys(nt * 512, 512), w=FLSP[par].keys())
                    em.op("dve", lambda v, o=o, nt=nt, par=par: v.tensor_tensor(
                        G[o].ap(rc0=nt * 512, rcols=512),
                        G[o].ap(rc0=nt * 512, rcols=512), FLSP[par].ap(), ALU.add),
                        r=G[o].keys(nt * 512, 512) + FLSP[par].keys(),
                        w=G[o].keys(nt * 512, 512))
                if o + 2 < 8:
                    load_w2(e, o + 2)
                nxt = (e + 1, half) if e + 1 < 8 else ((0, half + 1) if half == 0 else None)
                if nxt is not None and nxt[1] < 2:
                    if o == 4:
                        X.l0_load_band_next = True
                        load_band(nxt[0], 0)
                    if o == 6:
                        load_band(nxt[0], 1)
        if "g0" in X.dbg:
            for c in range(DC):
                em.dma("sp", X.dbg["g0"][c * 128:(c + 1) * 128,
                                         half * 1024:(half + 1) * 1024],
                       G[c].ap(), r=G[c].keys(), w=[("D", "g0")])
        for nt in range(2):
            j = half * 2 + nt
            sb, qb = 2 * (j % 2), 2 * (j % 2) + 1
            for c in range(DC):
                u = c % 2
                noff = half * 1024 + nt * 512
                em.dma("sp", XST[u].ap(), d["xT"][c * 128:(c + 1) * 128,
                                                  noff:noff + 512],
                       r=[("D", "x0")], w=XST[u].keys())
                em.dma("sp", SST[u].ap(), X.s0d[c * 128:(c + 1) * 128, noff:noff + 512],
                       r=[("D", "s0")], w=SST[u].keys())
                em.op("act", lambda sc, c=c, nt=nt, half=half: sc.activation(
                    GSIL.ap(), X.G[c].ap(rc0=nt * 512, rcols=512), AF.Silu),
                    r=X.G[c].keys(nt * 512, 512), w=GSIL.keys())
                em.op("dve", lambda v, u=u: v.tensor_tensor(
                    GSIL.ap(), GSIL.ap(), SST[u].ap(), ALU.mult),
                    r=GSIL.keys() + SST[u].keys(), w=GSIL.keys())
                em.op("dve", lambda v, u=u: v.tensor_tensor(
                    X1T[u].ap(), GSIL.ap(), XST[u].ap(), ALU.add),
                    r=GSIL.keys() + XST[u].keys(), w=X1T[u].keys())
                em.dma("sp", X.x1d[c * 128:(c + 1) * 128, noff:noff + 512],
                       X1T[u].ap(), r=X1T[u].keys(), w=[("D", "x1")])
                em.op("act", lambda sc, u=u: sc.activation(SQT[u].ap(), X1T[u].ap(),
                                                           AF.Square),
                      r=X1T[u].keys(), w=SQT[u].keys())
                stat_mm(X, sb, slice(0, 512), X1T[u].ap(), X1T[u].keys(),
                        start=(c == 0), stop=(c == DC - 1))
                stat_mm(X, qb, slice(0, 512), SQT[u].ap(), SQT[u].keys(),
                        start=(c == 0), stop=(c == DC - 1))
            emit_ln_rows_slice(X, 1, j, sb, qb)


def emit_l1_attn(X):
    em, C, d = X.em, X.C, X.d
    P0 = X.P0
    R = lambda n, c0, cols, dt=F32: Reg(X, n, c0, cols, dt)
    K8 = R("l1k", P0, 8192, BF16)
    Q8 = R("l1q", P0 + 8192, 8192, BF16)
    WQA = R("l1wa", 16640, 2048, BF16)
    WQB = R("l1wb", 10240, 2048, BF16)
    SCE = R("l1e", 35072, 1024, BF16)
    SCA = R("l1a", 36096, 1024, BF16)
    SCT = SCE
    OTM = R("l1o", 37120, 512, BF16)
    V16 = R("l1v", 37888, 8192, BF16)
    V3 = lambda: V16.ap().rearrange("p (c k) -> p c k", c=16)

    def wv3(c):
        W = WQA if c < 4 else WQB
        return W.ap().rearrange("p (c k) -> p c k", c=4)[:, c % 4, :]

    wkeys = WQA.keys() + WQB.keys()
    sm = lambda i: X.SMALL.ap(rc0=i, rcols=1)
    smk = X.SMALL.keys()
    skc = lambda i, n=1: [("S", j) for j in range(i, i + n)]

    for wi, (nm, scale) in enumerate((("wq1", SC1), ("wk1", 1.0), ("wv1", 1.0))):
        em.dma("gp", WQA.ap().rearrange("p (c k) -> p c k", c=4),
               d[nm][0:512, :].rearrange("(c p) n -> p c n", p=128),
               r=[("D", nm)], w=WQA.keys())
        em.dma("gp", WQB.ap().rearrange("p (c k) -> p c k", c=4),
               d[nm][512:1024, :].rearrange("(c p) n -> p c n", p=128),
               r=[("D", nm)], w=WQB.keys())
        if wi < 2:
            DST = Q8 if wi == 0 else K8
            for ec in range(DC):
                for nk in range(4):
                    pb = 4 + nk % 2
                    fns = []
                    for c in range(DC):
                        fns.append(lambda t, c=c, ec=ec, nk=nk, pb=pb: t.matmul(
                            X.PB[pb][:, :], wv3(c)[:, ec * 128:(ec + 1) * 128],
                            X.XN[c].ap(rc0=nk * 512, rcols=512),
                            start=(c == 0), stop=(c == DC - 1)))
                    em.grp("pe", fns, r=wkeys + sum(
                        [X.XN[c].keys(nk * 512, 512) for c in range(DC)], []),
                        w=[X.PK[pb]])
                    em.op("act", lambda sc, ec=ec, nk=nk, pb=pb, DST=DST, scale=scale:
                          sc.activation(DST.ap(rc0=ec * 2048 + nk * 512, rcols=512),
                                        X.PB[pb][:, :], AF.Copy, scale=scale),
                          r=[X.PK[pb]], w=DST.keys(ec * 2048 + nk * 512, 512))
        else:
            for kt in range(16):
                for eh in range(2):
                    pb = 4 + (kt * 2 + eh) % 2
                    fns = []
                    for c in range(DC):
                        fns.append(lambda t, c=c, kt=kt, eh=eh, pb=pb: t.matmul(
                            X.PB[pb][:, :], X.XN[c].ap(rc0=kt * 128, rcols=128),
                            wv3(c)[:, eh * 512:(eh + 1) * 512],
                            start=(c == 0), stop=(c == DC - 1)))
                    em.grp("pe", fns, r=wkeys + sum(
                        [X.XN[c].keys(kt * 128, 128) for c in range(DC)], []),
                        w=[X.PK[pb]])
                    em.op("act", lambda sc, kt=kt, eh=eh, pb=pb: sc.activation(
                        V3()[:, kt, eh * 512:(eh + 1) * 512], X.PB[pb][:, :], AF.Copy),
                        r=[X.PK[pb]], w=V16.keys(kt * 1024 + eh * 512, 512))

    if "l1q" in X.dbg:
        for ec in range(DC):
            em.op("dve", lambda v, ec=ec: v.tensor_copy(SCE.ap(), Q8.ap(rc0=ec * 2048, rcols=2048)),
                  r=Q8.keys(), w=SCE.keys())
            em.dma("sp", X.dbg["l1q"][ec * 128:(ec + 1) * 128, :].bitcast(BF16)[:, 0:2048],
                   SCE.ap(), r=SCE.keys(), w=[("D", "l1q")])
            em.op("dve", lambda v, ec=ec: v.tensor_copy(SCE.ap(), K8.ap(rc0=ec * 2048, rcols=2048)),
                  r=K8.keys(), w=SCE.keys())
            em.dma("sp", X.dbg["l1k"][ec * 128:(ec + 1) * 128, :].bitcast(BF16)[:, 0:2048],
                   SCE.ap(), r=SCE.keys(), w=[("D", "l1k")])

    for jt in range(16):
        for i in range(4):
            fns = []
            for ec in range(DC):
                fns.append(lambda t, ec=ec, i=i, jt=jt: t.matmul(
                    X.PB[i][:, :], K8.ap(rc0=ec * 2048 + jt * 128, rcols=128),
                    Q8.ap(rc0=ec * 2048 + i * 512, rcols=512),
                    start=(ec == 0), stop=(ec == DC - 1)))
            em.grp("pe", fns, r=K8.keys() + Q8.keys(), w=[X.PK[i]])
        for i in range(4):
            em.op("dve", lambda v, i=i: v.reduce_max(
                sm(40 + i), X.PB[i][:, :], axis=AX.X, negate=True),
                r=[X.PK[i]], w=skc(40 + i))
        em.op("dve", lambda v: v.tensor_reduce(
            sm(44), X.SMALL.ap(rc0=40, rcols=4), AX.X, ALU.min),
            r=skc(40, 4), w=skc(44))
        for i in range(4):
            em.op("act", lambda sc, i=i: sc.activation(
                SCE.ap(rc0=i * 512, rcols=512), X.PB[i][:, :], AF.Exp,
                bias=sm(44), accum_out=sm(45 + i)),
                r=[X.PK[i]] + skc(44), w=SCE.keys(i * 512, 512) + skc(45 + i))
        em.op("dve", lambda v: v.reduce_sum(sm(49), X.SMALL.ap(rc0=45, rcols=4), AX.X),
              r=skc(45, 4), w=skc(49))
        em.op("dve", lambda v: v.reciprocal(sm(50), sm(49)), r=skc(49), w=skc(50))
        em.op("dve", lambda v: v.tensor_scalar(SCA.ap(), SCE.ap(), sm(50), None, ALU.mult),
              r=SCE.keys() + skc(50), w=SCA.keys())
        for kc in range(16):
            pb = 6 + kc % 2
            em.op("pe", lambda t, kc=kc, pb=pb: t.transpose(
                X.PBb[pb][:, 0:128], SCA.ap(rc0=kc * 128, rcols=128), C["idb"].ap()),
                r=SCA.keys(kc * 128, 128) + [("C", "idb")], w=[X.PK[pb]])
            em.op("dve", lambda v, kc=kc, pb=pb: v.tensor_copy(
                SCT.ap(rc0=kc * 128, rcols=128), X.PBb[pb][:, 0:128]),
                r=[X.PK[pb]], w=SCT.keys(kc * 128, 128))
        for eh in range(2):
            pb = 4 + eh
            fns = []
            for kc in range(16):
                fns.append(lambda t, kc=kc, eh=eh, pb=pb: t.matmul(
                    X.PB[pb][:, :], SCT.ap(rc0=kc * 128, rcols=128),
                    V3()[:, kc, eh * 512:(eh + 1) * 512],
                    start=(kc == 0), stop=(kc == 15)))
            em.grp("pe", fns, r=SCT.keys() + V16.keys(), w=[X.PK[pb]])
            em.op("act", lambda sc, eh=eh, pb=pb: sc.activation(
                OTM.ap(rc0=eh * 512, rcols=512), X.PB[pb][:, :], AF.Copy),
                r=[X.PK[pb]], w=OTM.keys(eh * 512, 512))
        if jt == 15 and "l1at" in X.dbg:
            em.dma("sp", X.dbg["l1at"][0:128, :].bitcast(BF16)[:, 0:2048], SCA.ap(),
                   r=SCA.keys(), w=[("D", "l1at")])
            em.dma("sp", X.dbg["l1at"][256:384, 0:64], X.SMALL.ap(rc0=40, rcols=64),
                   r=X.SMALL.keys(), w=[("D", "l1at")])
            em.dma("sp", X.dbg["l1at"][384:512, :].bitcast(BF16)[:, 0:1024],
                   OTM.ap(rc0=0, rcols=1024), r=OTM.keys(), w=[("D", "l1at")])
        for ec in range(DC):
            pb = 6 + ec % 2
            em.op("pe", lambda t, ec=ec, pb=pb: t.transpose(
                X.PBb[pb][:, 0:128], OTM.ap(rc0=ec * 128, rcols=128), C["idb"].ap()),
                r=OTM.keys(ec * 128, 128) + [("C", "idb")], w=[X.PK[pb]])
            em.op("dve", lambda v, ec=ec, jt=jt, pb=pb: v.tensor_copy(
                X.XN[ec].ap(rc0=jt * 128, rcols=128), X.PBb[pb][:, 0:128]),
                r=[X.PK[pb]], w=X.XN[ec].keys(jt * 128, 128))


def emit_l1_ffn(X):
    em, C, d = X.em, X.C, X.d
    P0 = X.P0
    R = lambda n, c0, cols, dt=F32: Reg(X, n, c0, cols, dt)
    NHR = R("nh", P0, 4096, BF16)
    ACTW = R("actw1", P0 + 4096, 8192, BF16)
    MH = R("mh", P0 + 4096, 1024)
    AH = R("ah", P0 + 5120, 1024)
    X1S = [R("x1sa", P0 + 6144, 1024), R("x1sb", P0 + 7168, 1024)]
    XC = R("xc", P0 + 8192, 1024)
    T2 = R("t2", P0 + 9216, 1024)
    LG1 = R("lg1", P0 + 10240, 512)
    W1 = [R("l1w1a", P0 + 12288, 2048, BF16), R("l1w1b", P0 + 14336, 2048, BF16)]
    W2 = [R("l1w2a", P0 + 16384, 1024, BF16), R("l1w2b", P0 + 17408, 1024, BF16)]
    GSIL = R("gsil1", P0 + 18432, 512)
    FLSP = [R("fls1a", 10752, 512), R("fls1b", 11264, 512)]
    G = [R(f"g1_{o}", 37888 + o * 1024, 1024) for o in range(8)]
    X1C = [R("x1ca", P0 + 4096, 512), R("x1cb", P0 + 4608, 512)]
    OUTT = [R("outta", P0 + 5120, 512), R("outtb", P0 + 5632, 512)]
    w1v = lambda W: W.ap().rearrange("p (c k) -> p c k", c=8)
    aw = lambda hc, nt: ACTW.ap(rc0=((hc % 16) * 2 + nt) * 512, rcols=512)
    awk = lambda hc, nt: ACTW.keys(((hc % 16) * 2 + nt) * 512, 512)
    DW = X.DWBC
    nh = lambda c, nt: NHR.ap(rc0=c * 1024 + nt * 512, rcols=512)
    nhk = lambda c, nt: NHR.keys(c * 1024 + nt * 512, 512)

    for half in range(2):
        for j in range(2):
            k1_broadcast(X, X.ROWA.ap(p0=0, pn=1, rc0=half * 1024 + j * 512, rcols=512),
                         2, MH.ap(rc0=j * 512, rcols=512), MH.keys(j * 512, 512),
                         X.ROWA.keys())
            k1_broadcast(X, X.ROWC.ap(p0=0, pn=1, rc0=half * 1024 + j * 512, rcols=512),
                         3, AH.ap(rc0=j * 512, rcols=512), AH.keys(j * 512, 512),
                         X.ROWC.keys())
        for c in range(DC):
            u = c % 2
            em.dma("sp", X1S[u].ap(), X.x1d[c * 128:(c + 1) * 128,
                                           half * 1024:(half + 1) * 1024],
                   r=[("D", "x1")], w=X1S[u].keys())
            em.op("dve", lambda v, u=u: v.tensor_tensor(XC.ap(), X1S[u].ap(), MH.ap(),
                                                        ALU.subtract),
                  r=X1S[u].keys() + MH.keys(), w=XC.keys())
            em.op("dve", lambda v: v.tensor_tensor(T2.ap(), XC.ap(), AH.ap(), ALU.mult),
                  r=XC.keys() + AH.keys(), w=T2.keys())
            for nt in range(2):
                em.op("dve", lambda v, c=c, nt=nt: v.tensor_scalar(
                    nh(c, nt), T2.ap(rc0=nt * 512, rcols=512),
                    C["nwc"].ap()[:, c:c + 1], None, ALU.mult),
                    r=T2.keys() + [("C", "nwc")], w=nhk(c, nt))
            for t in range(8):
                em.grp("pe", [lambda tt_, c=c, t=t: tt_.matmul(
                    X.PB[6][:, t * 8:(t + 1) * 8], T2.ap(rc0=t * 128, rcols=128),
                    C["routnw"].ap()[:, c, :],
                    start=(c == 0 and t == 0), stop=(c == DC - 1 and t == 7))],
                    r=T2.keys() + [("C", "routnw")], w=[X.PK[6]])
        if "l1lg" in X.dbg:
            em.op("dve", lambda v, half=half: v.tensor_copy(
                XC.ap(rc0=0, rcols=64), X.PB[6][:, 0:64]), r=[X.PK[6]], w=XC.keys())
            em.dma("sp", X.dbg["l1lg"][half * 128:(half + 1) * 128, 0:64],
                   XC.ap(rc0=0, rcols=64), r=XC.keys(), w=[("D", "l1lg")])
            em.dma("sp", X.dbg["l1lg"][256:384, 0:64],
                   C["routnw"].ap().rearrange("p c e -> p (c e)"), r=[("C", "routnw")],
                   w=[("D", "l1lg")])
            em.dma("sp", X.dbg["l1lg"][384:512, 0:1024], T2.ap(), r=T2.keys(),
                   w=[("D", "l1lg")])
            for c in range(DC):
                em.dma("sp", X.dbg["l1nh"][c * 128:(c + 1) * 128, :].bitcast(BF16)[:, half * 1024:(half + 1) * 1024],
                       NHR.ap(rc0=c * 1024, rcols=1024), r=NHR.keys(), w=[("D", "l1nh")])
        for t in range(8):
            emit_mixer_tail(X, half * 8 + t, LG1, l1=True,
                            logits_ap=X.PB[6][:, t * 8:(t + 1) * 8],
                            logits_keys=[X.PK[6]], tr_bank=7)
        for o in range(8):
            for nt in range(2):
                em.grp("pe", [lambda t, o=o, nt=nt, half=half: t.matmul(
                    X.PB[7][:, 0:512], C["eb2"].ap()[0:8, o * 128:(o + 1) * 128],
                    X.DWFM.ap(p0=0, pn=8, rc0=half * 1024 + nt * 512, rcols=512),
                    start=True, stop=True)],
                    r=[("C", "eb2")] + X.DWFM.keys(), w=[X.PK[7]])
                em.op("dve", lambda v, o=o, nt=nt: v.tensor_scalar(
                    G[o].ap(rc0=nt * 512, rcols=512), X.PB[7][:, 0:512],
                    C["sb2c"].ap()[:, o:o + 1], None, ALU.add),
                    r=[X.PK[7]] + [("C", "sb2c")], w=G[o].keys(nt * 512, 512))
        for e in range(9):
            if e < 8:
                for js in range(2):
                    em.op("pe", lambda t, e=e, half=half, js=js: t.matmul(
                        X.PB[6][:, :], C["sel"].ap()[:, e * 128:(e + 1) * 128],
                        X.DWFM.ap(p0=0, pn=8, rc0=half * 1024 + js * 512, rcols=512),
                        start=True, stop=True),
                        r=[("C", "sel")] + X.DWFM.keys(), w=[X.PK[6]])
                    em.op("dve", lambda v, js=js: v.tensor_copy(
                        DW.ap(rc0=js * 512, rcols=512), X.PB[6][:, :]),
                        r=[X.PK[6]], w=DW.keys(js * 512, 512))
            def load_band1(e_, bg_, band_):
                bb_ = band_ % 2
                hb_ = bg_ * 2048 + band_ * 512
                src = (d["ew1"][e_][:, hb_:hb_ + 512] if e_ < 8 else
                       d["sw1"][:, hb_:hb_ + 512])
                em.dma("gp", w1v(W1[bb_]),
                       src.rearrange("(c p) n -> p c n", p=128),
                       r=[("D", "ew1" if e_ < 8 else "sw1")], w=W1[bb_].keys())

            def load_w2b(e_, o_, bg_):
                src = (d["ew2cm"][e_, o_][:, bg_ * 16:(bg_ + 1) * 16, :] if e_ < 8 else
                       d["sw2cm"][o_][:, bg_ * 16:(bg_ + 1) * 16, :])
                em.dma("gp", W2[o_ % 2].ap(),
                       src.rearrange("p c k -> p (c k)"),
                       r=[("D", "ew2cm" if e_ < 8 else "sw2cm")], w=W2[o_ % 2].keys())

            if e == 0 and half == 0:
                load_band1(0, 0, 0)
                load_band1(0, 0, 1)
            for bg in range(2):
                for band in range(4):
                    bb = band % 2
                    for hci in range(4):
                        hc = bg * 16 + band * 4 + hci
                        for nt in range(2):
                            pbA = (hc * 2 + nt) % 2
                            fns = []
                            for c in range(DC):
                                fns.append(lambda t, c=c, hci=hci, bb=bb, pbA=pbA, nt=nt:
                                           t.matmul(X.PB[pbA][:, :],
                                                    w1v(W1[bb])[:, c, hci * 128:(hci + 1) * 128],
                                                    nh(c, nt),
                                                    start=(c == 0), stop=(c == DC - 1)))
                            em.grp("pe", fns, r=W1[bb].keys() + sum(
                                [nhk(c, nt) for c in range(DC)], []), w=[X.PK[pbA]])
                            bias = (C["eb1c"].ap()[:, e, hc:hc + 1] if e < 8 else
                                    C["sb1c"].ap()[:, hc:hc + 1])
                            em.op("act", lambda sc, hc=hc, nt=nt, pbA=pbA, bias=bias:
                                  sc.activation(aw(hc, nt), X.PB[pbA][:, :], AF.Relu,
                                                bias=bias),
                                  r=[X.PK[pbA], ("C", "eb1c" if e < 8 else "sb1c")],
                                  w=awk(hc, nt))
                    if band + 2 < 4:
                        load_band1(e, bg, band + 2)
                load_w2b(e, 0, bg)
                load_w2b(e, 1, bg)
                for o in range(8):
                    ob = o % 2
                    for nt in range(2):
                        pbo = 4 + (o * 2 + nt) % 2
                        fns = []
                        for hh in range(16):
                            fns.append(lambda t, hh=hh, ob=ob, nt=nt, pbo=pbo: t.matmul(
                                X.PB[pbo][:, :], W2[ob].ap(rc0=hh * 128, rcols=128),
                                aw(hh, nt), start=(hh == 0), stop=(hh == 15)))
                        em.grp("pe", fns, r=W2[ob].keys() + ACTW.keys(), w=[X.PK[pbo]])
                        if e < 8:
                            par = (o * 2 + nt) % 2
                            em.op("dve", lambda v, nt=nt, pbo=pbo, par=par: v.tensor_tensor(
                                FLSP[par].ap(), X.PB[pbo][:, :],
                                DW.ap(rc0=nt * 512, rcols=512), ALU.mult),
                                r=[X.PK[pbo]] + DW.keys(nt * 512, 512), w=FLSP[par].keys())
                            em.op("dve", lambda v, o=o, nt=nt, par=par: v.tensor_tensor(
                                G[o].ap(rc0=nt * 512, rcols=512),
                                G[o].ap(rc0=nt * 512, rcols=512), FLSP[par].ap(), ALU.add),
                                r=G[o].keys(nt * 512, 512) + FLSP[par].keys(),
                                w=G[o].keys(nt * 512, 512))
                        else:
                            em.op("dve", lambda v, o=o, nt=nt, pbo=pbo: v.tensor_tensor(
                                G[o].ap(rc0=nt * 512, rcols=512),
                                G[o].ap(rc0=nt * 512, rcols=512), X.PB[pbo][:, :],
                                ALU.add),
                                r=G[o].keys(nt * 512, 512) + [X.PK[pbo]],
                                w=G[o].keys(nt * 512, 512))
                    if o + 2 < 8:
                        load_w2b(e, o + 2, bg)
                    if o == 4 or o == 6:
                        band_n = 0 if o == 4 else 1
                        if bg == 0:
                            load_band1(e, 1, band_n)
                        else:
                            nxt = (e + 1, half) if e + 1 < 9 else ((0, half + 1) if half == 0 else None)
                            if nxt is not None and nxt[1] < 2:
                                load_band1(nxt[0], 0, band_n)
        for nt in range(2):
            for c in range(DC):
                u = c % 2
                noff = half * 1024 + nt * 512
                em.dma("sp", X1C[u].ap(), X.x1d[c * 128:(c + 1) * 128, noff:noff + 512],
                       r=[("D", "x1")], w=X1C[u].keys())
                em.op("act", lambda sc, c=c, nt=nt: sc.activation(
                    GSIL.ap(), G[c].ap(rc0=nt * 512, rcols=512), AF.Silu),
                    r=G[c].keys(nt * 512, 512), w=GSIL.keys())
                em.op("dve", lambda v, c=c, noff=noff: v.tensor_tensor(
                    GSIL.ap(), GSIL.ap(), X.XN[c].ap(rc0=noff, rcols=512), ALU.mult),
                    r=GSIL.keys() + X.XN[c].keys(noff, 512), w=GSIL.keys())
                em.op("dve", lambda v, u=u: v.tensor_tensor(
                    OUTT[u].ap(), GSIL.ap(), X1C[u].ap(), ALU.add),
                    r=GSIL.keys() + X1C[u].keys(), w=OUTT[u].keys())
                em.dma("sp", X.out[c * 128:(c + 1) * 128, noff:noff + 512],
                       OUTT[u].ap(), r=OUTT[u].keys(), w=[("D", "out")])


def emit_all(X, debug):
    emit_consts(X)
    P0 = X.P0
    STR0 = Reg(X, "str0", P0, 2048)
    STR1 = Reg(X, "str1", P0 + 2048, 2048)
    SQ = Reg(X, "sq", P0 + 4096, 2048)
    MBM = Reg(X, "mbm", P0 + 6144, 2048)
    MBR = Reg(X, "mbr", P0 + 8192, 2048)
    emit_ln(X, 0, "x0", STR0, STR1, SQ, MBM, MBR)
    dbg_tap_xn(X, "xn0", STR0, STR1)
    emit_mixer_l0(X)
    dbg_tap_dwfm(X, "dw0")
    emit_l0_attn(X)
    emit_l0_wo(X)
    if "s0" in X.dbg:
        X.em.dma("gp", X.dbg["s0"], X.s0d[:, :], r=[("D", "s0")], w=[("D", "dbg_s0")])
    emit_l0_ffn(X)
    if "x1" in X.dbg:
        X.em.dma("gp", X.dbg["x1"], X.x1d[:, :], r=[("D", "x1")], w=[("D", "dbg_x1")])
    emit_ln(X, 1, "x1", STR0, STR1, SQ, MBM, MBR)
    dbg_tap_xn(X, "xn1", STR0, STR1)
    emit_l1_attn(X)
    if "s1" in X.dbg:
        for c in range(DC):
            S = [STR0, STR1][c % 2]
            X.em.op("dve", lambda v, c=c, S=S: v.tensor_copy(S.ap(), X.XN[c].ap()),
                    r=X.XN[c].keys(), w=S.keys())
            X.em.dma("sp", X.dbg["s1"][c * 128:(c + 1) * 128, :], S.ap(), r=S.keys(),
                     w=[("D", "s1")])
    emit_l1_ffn(X)
    if "comb1" in X.dbg:
        X.em.dma("sp", X.dbg["comb1"][0:8, :], X.DWFM.ap(p0=0, pn=8)[0:8, :],
                 r=X.DWFM.keys(), w=[("D", "comb1")])


def dbg_tap_xn(X, nm, STR0, STR1):
    if nm not in X.dbg:
        return
    em = X.em
    for c in range(DC):
        S = [STR0, STR1][c % 2]
        em.op("dve", lambda v, c=c, S=S: v.tensor_copy(S.ap(), X.XN[c].ap()),
              r=X.XN[c].keys(), w=S.keys())
        em.dma("sp", X.dbg[nm][c * 128:(c + 1) * 128, :], S.ap(), r=S.keys(),
               w=[("D", nm)])


def dbg_tap_dwfm(X, nm):
    if nm not in X.dbg:
        return
    em = X.em
    em.dma("sp", X.dbg[nm][0:8, :], X.DWFM.ap(p0=0, pn=8)[0:8, :], r=X.DWFM.keys(),
           w=[("D", nm)])


def kernel(**inputs):
    H = host_prep(inputs)
    x = np.asarray(inputs["L0_attn_X"], np.float32)
    nc = build_program(debug=tuple(DEBUG))
    in_maps = []
    for b in range(8):
        m = {k: H[k] for k in COMMON}
        m["xT"] = np.ascontiguousarray(x[b].T)
        in_maps.append(m)
    res = run_bass_kernel_spmd(nc, in_maps, core_ids=list(range(8)))
    outs = [r["out"].T for r in res.results]
    kernel.last_results = res.results
    return np.stack(outs, 0)
